# version 42
# baseline (speedup 1.0000x reference)
import os
import numpy as np
from contextlib import ExitStack
_KDBG = os.environ.get('KDBG', '')
_P1G = int(os.environ.get('P1G', '2'))
_P1B = os.environ.get('P1B', '1') == '1'
_P1T = int(os.environ.get('P1T', '4'))
_P1S = int(os.environ.get('P1S', '4'))
_P1W = int(os.environ.get('P1W', '2'))
_P1Q = int(os.environ.get('P1Q', '2'))

import concourse.bass as bass
import concourse.mybir as mybir
from concourse.bass_utils import run_bass_kernel_spmd

F32 = mybir.dt.float32
BF16 = mybir.dt.bfloat16
I32 = mybir.dt.int32
AF = mybir.ActivationFunctionType
ALU = mybir.AluOpType
AX = mybir.AxisListType

T = 4096
D = 1024
H = 16
HD = 64
LD = 256
LH = 4
FF = 4096
EPS = 1e-5
NEG = -1e9


class Sem:
    __slots__ = ("h", "n", "dma")

    def __init__(self, h, dma=False):
        self.h = h
        self.n = dma
        self.n = 0
        self.dma = dma


HARD_TOKS = set()


class St:
    def __init__(self, name, sem):
        self.name = name
        self.sem = sem
        self.ops = []
        self.wm = {}

    def wait(self, tok, force=False):
        if tok is None:
            return
        sem, val = tok
        if sem.dma:
            val = sem.n
        if sem is self.sem and not force and (id(sem), val) not in HARD_TOKS:
            return
        if self.wm.get(id(sem), 0) >= val:
            return
        self.wm[id(sem)] = val
        self.ops.append(("w", sem, val))

    def raw(self, meth, *a, **kw):
        self.ops.append(["o", meth, a, kw, []])

    def mark(self):
        self.sem.n += 1
        self.ops[-1][4].append((self.sem, 1))
        return (self.sem, self.sem.n)


class Buf:
    __slots__ = ("w", "r")

    def __init__(self):
        self.w = None
        self.r = []


_FENCED = ("vector", "scalar", "gpsimd")


def _pre(st, reads, writes):
    f = st.name in _FENCED
    for b in reads:
        if b.w is not None:
            st.wait(b.w, force=f)
    for b in writes:
        if b.w is not None:
            st.wait(b.w, force=f)
        for t in b.r:
            st.wait(t, force=f)


def _post(tok, reads, writes):
    for b in reads:
        b.r.append(tok)
    for b in writes:
        b.w = tok
        b.r = []


HARD_METHS = ("reduce_sum", "reduce_max", "tensor_reduce", "reciprocal")


def do(st, meth, *a, reads=(), writes=(), hard=False, **kw):
    _pre(st, reads, writes)
    st.raw(meth, *a, **kw)
    tok = st.mark()
    if hard or meth in HARD_METHS or kw.get("accum_out") is not None:
        HARD_TOKS.add((id(tok[0]), tok[1]))
    _post(tok, reads, writes)
    return tok


def dma(st, dsem, out, in_, reads=(), writes=(), **kw):
    _pre(st, reads, writes)
    st.wait((dsem, dsem.n))
    st.raw("dma_start", out=out, in_=in_, **kw)
    dsem.n += 16
    st.ops[-1][4].append((dsem, 16))
    tok = (dsem, dsem.n)
    _post(tok, reads, writes)
    return tok


def dmaq(st, dsem, out, in_, reads=(), writes=(), **kw):
    _pre(st, reads, writes)
    st.raw("dma_start", out=out, in_=in_, **kw)
    dsem.n += 16
    st.ops[-1][4].append((dsem, 16))
    tok = (dsem, dsem.n)
    _post(tok, reads, writes)
    return tok


def mm_group(st, mms, reads=(), writes=()):
    _pre(st, reads, writes)
    for meth, a, kw in mms:
        st.raw(meth, *a, **kw)
    tok = st.mark()
    _post(tok, reads, writes)
    return tok


def build_nc(parts=('p1', 'p2')):
    nc = bass.Bass("TRN2", target_bir_lowering=False)

    xo = nc.dram_tensor("xo", [T, D], F32, kind="ExternalInput").ap()
    xp = nc.dram_tensor("xp", [T, D], F32, kind="ExternalInput").ap()
    wqkv = nc.dram_tensor("wqkv", [D, 3 * D], F32, kind="ExternalInput").ap()
    bqkv = nc.dram_tensor("bqkv", [3 * D], F32, kind="ExternalInput").ap()
    wrq = nc.dram_tensor("wrq", [D, LD], F32, kind="ExternalInput").ap()
    brq = nc.dram_tensor("brq", [LD], F32, kind="ExternalInput").ap()
    wrkv = nc.dram_tensor("wrkv", [D, 2 * LD], F32, kind="ExternalInput").ap()
    brkv = nc.dram_tensor("brkv", [2 * LD], F32, kind="ExternalInput").ap()
    wrout = nc.dram_tensor("wrout", [LD, D], F32, kind="ExternalInput").ap()
    brout = nc.dram_tensor("brout", [D], F32, kind="ExternalInput").ap()
    wout = nc.dram_tensor("wout", [D, D], F32, kind="ExternalInput").ap()
    bout = nc.dram_tensor("bout", [D], F32, kind="ExternalInput").ap()
    glbd = nc.dram_tensor("glbd", [D, H], F32, kind="ExternalInput").ap()
    grbd = nc.dram_tensor("grbd", [D, H], F32, kind="ExternalInput").ap()
    gln = nc.dram_tensor("gln", [D], F32, kind="ExternalInput").ap()
    bln = nc.dram_tensor("bln", [D], F32, kind="ExternalInput").ap()
    q1d = nc.dram_tensor("q1d", [D, FF], I32, kind="ExternalInput").ap()
    s1d = nc.dram_tensor("s1d", [8, FF], F32, kind="ExternalInput").ap()
    b1d = nc.dram_tensor("b1d", [FF], F32, kind="ExternalInput").ap()
    q2d = nc.dram_tensor("q2d", [FF, D], I32, kind="ExternalInput").ap()
    s2d = nc.dram_tensor("s2d", [32, D], F32, kind="ExternalInput").ap()
    b2d = nc.dram_tensor("b2d", [D], F32, kind="ExternalInput").ap()
    madd_t0_in = nc.dram_tensor("madd_t0", [128, 256], F32, kind="ExternalInput").ap()
    vrow_t0_in = nc.dram_tensor("vrow_t0", [1, 128], F32, kind="ExternalInput").ap()
    tok0_in = nc.dram_tensor("tok0", [128, 1], F32, kind="ExternalInput").ap()
    latm_in = nc.dram_tensor("latm", [32, 64], F32, kind="ExternalInput").ap()

    out = nc.dram_tensor("out", [T, D], F32, kind="ExternalOutput").ap()
    xnscr = nc.dram_tensor("xnscr", [8, 128, T], BF16, kind="Internal").ap()
    r2scr = nc.dram_tensor("r2scr", [32, 128], F32, kind="Internal").ap()
    qscr1 = nc.dram_tensor("qscr1", [D, FF], BF16, kind="Internal").ap()
    qscr2 = nc.dram_tensor("qscr2", [FF, D], BF16, kind="Internal").ap()

    es0 = ExitStack()

    def mksem(name):
        return Sem(es0.enter_context(nc.semaphore(name)))

    def mkdma(name):
        return Sem(es0.enter_context(nc.semaphore(name)), dma=True)

    d_x = [mkdma("d_x0"), mkdma("d_x1"), mkdma("d_x2"), mkdma("d_x3")]
    d_st = [mkdma("d_st0"), mkdma("d_st1")]
    d_w = [mkdma("d_w0"), mkdma("d_w1")]
    d_g = [mkdma("d_g0"), mkdma("d_g1")]
    d_s2 = [mkdma("d_s20"), mkdma("d_s21")]
    d_xb = [mkdma("d_xb0"), mkdma("d_xb1")]
    d_ss = mkdma("d_ss")
    DMAS = d_x + d_st + d_w + d_g + d_s2 + d_xb + [d_ss]
    ph = mksem("ph")
    SP = St("sync", mksem("s_sp"))
    GP = St("gpsimd", mksem("s_gp"))
    VE = St("vector", mksem("s_ve"))
    AC = St("scalar", mksem("s_ac"))
    PE = St("tensor", mksem("s_pe"))
    ALL = [SP, GP, VE, AC, PE]

    def barrier():
        for ds in DMAS:
            SP.wait((ds, ds.n))
            GP.wait((ds, ds.n))
        for st in ALL:
            st.raw("nop")
            ph.n += 1
            st.ops[-1][4].append((ph, 1))
        tgt = ph.n
        for st in ALL:
            st.wait((ph, tgt))

    B = {}

    def buf(name):
        if name not in B:
            B[name] = Buf()
        return B[name]

    def sb0(name, shape, dt):
        return es0.enter_context(nc.sbuf_tensor(name, shape, dt))

    ident_f = sb0("ident_f", [128, 128], F32)
    ident_b = sb0("ident_b", [128, 128], BF16)
    zcol = sb0("zcol", [128, 128], BF16)
    ones_row = sb0("ones_row", [1, 128], BF16)
    ones64 = sb0("ones64", [64, 128], BF16)
    madd = sb0("madd", [128, 256], F32)
    madd_t0 = sb0("madd_t0s", [128, 256], F32)
    madd_b = sb0("madd_b", [128, 256], BF16)
    madd_t0_b = sb0("madd_t0b", [128, 256], BF16)
    mask2 = sb0("mask2", [128, 512], BF16)
    mask2_t0 = sb0("mask2_t0", [128, 512], BF16)
    latmadd = sb0("latmadd", [128, 32, 64], BF16)
    tok0col = sb0("tok0col", [128, 1], F32)
    qb_col = sb0("qb_col", [128, 8], F32)
    kb_col = sb0("kb_col", [128, 8], F32)
    vb_colT = sb0("vb_colT", [128, 8], F32)
    rqb_col = sb0("rqb_col", [128, 2], F32)
    rkb_col = sb0("rkb_col", [128, 2], F32)
    rvb_row = sb0("rvb_row", [1, LD], BF16)
    routb_col = sb0("routb_col", [128, 8], F32)
    f1b_col = sb0("f1b_col", [128, 32], F32)
    g_col = sb0("g_col", [128, 8], F32)
    be_col = sb0("be_col", [128, 8], F32)
    glbd_b = sb0("glbd_b", [128, 8, H], BF16)
    grbd_b = sb0("grbd_b", [128, 8, H], BF16)
    sel = sb0("sel", [16, 8, 128], BF16)
    pooled_fm = sb0("pooled_fm", [128, 8, 64], BF16)
    rk_fm = sb0("rk_fm", [128, 2, 64], BF16)
    rv_tm = sb0("rv_tm", [64, 4, 64], BF16)
    xn_prev = sb0("xn_prev", [128, 8, 128], BF16)
    r2sb = sb0("r2sb", [128, 32], F32)
    smalls = sb0("smalls", [128, 32], F32)
    negmax = sb0("negmax", [128, 16], F32)
    sums = sb0("sums", [128, 16], F32)
    recip = sb0("recip", [128, 16], F32)

    NSM = 8

    def sc(i, par):
        j = i + NSM * par
        return smalls[:, j:j + 1]

    S_MU, S_VAR, S_RSTD, S_NB, S_A1, S_T0, S_T1 = range(7)

    with nc.Block() as block:

        def emit():
            for st in ALL:
                ops = st.ops
                st.ops = []
                if not ops:
                    continue

                def run(eng, ops=ops):
                    for o in ops:
                        if o[0] == "w":
                            eng.wait_ge(o[1].h, o[2])
                        else:
                            instr = getattr(eng, o[1])(*o[2], **o[3])
                            for (s, k) in o[4]:
                                instr.then_inc(s.h, k)

                with nc.allow_low_precision(reason="bf16 score psum per stated bf16-matmul bar"):
                    getattr(block, st.name)(run)

        es_w = ExitStack()

        def sbw(name, shape, dt):
            return es_w.enter_context(nc.sbuf_tensor(name, shape, dt))

        qkvw_b = sbw("qkvw_b", [128, 8, 3 * D], BF16)
        outw_b = sbw("outw_b", [128, 8, D], BF16)
        rqw_b = sbw("rqw_b", [128, 8, LD], BF16)
        routw_b = sbw("routw_b", [128, 2, D], BF16)
        rkvw_b = sbw("rkvw_b", [128, 8, 2 * LD], BF16)

        es1 = ExitStack()

        def sb1(name, shape, dt):
            return es1.enter_context(nc.sbuf_tensor(name, shape, dt))

        xt = sb1("xt", [128, 8, D], F32)
        xn_tmb = sb1("xn_tmb", [128, 4, D], BF16)
        s0t = sb1("s0t", [128, 4, D], F32)
        stage = sb1("stage", [128, 2, 8, 512], BF16)
        bnst = sb1("bnst", [128, 8, 12], F32)
        brows = sb1("brows", [32, 4, 128], F32)
        pooled_tm = sb1("pooled_tm", [64, D], BF16)
        p_pool = es1.enter_context(nc.psum_tensor("p_pool", [64, D], F32))
        p_tr = es1.enter_context(nc.psum_tensor("p_tr", [128, 8, 128], BF16))
        p_bias = es1.enter_context(nc.psum_tensor("p_bias", [128, 128], F32))

        dmaq(GP, d_w[0], latmadd[:, :, :], latm_in.partition_broadcast(128),
             writes=[buf("latmadd")])
        dmaq(GP, d_w[0], rvb_row[:, :], brkv[LD:2 * LD].unsqueeze(0),
             writes=[buf("rvb_row")])
        for c in range(8):
            dmaq(GP, d_w[0], glbd_b[:, c, :], glbd[c * 128:(c + 1) * 128, :],
                 writes=[buf("glbd_b")])
            dmaq(GP, d_w[0], grbd_b[:, c, :], grbd[c * 128:(c + 1) * 128, :],
                 writes=[buf("grbd_b")])
        dmaq(GP, d_w[1], madd_t0[:, :], madd_t0_in, writes=[buf("madd_t0s")])
        dmaq(GP, d_w[1], tok0col[:, :], tok0_in, writes=[buf("tok0col")])
        dmaq(GP, d_w[1], brows[0:24, 0, :],
             bqkv.rearrange("(r p) -> r p", p=128)[0:24, :], writes=[buf("brows")])
        dmaq(GP, d_w[1], brows[0:2, 1, :], brq.rearrange("(r p) -> r p", p=128),
             writes=[buf("brows")])
        dmaq(GP, d_w[1], brows[2:4, 1, :],
             brkv.rearrange("(r p) -> r p", p=128)[0:2, :], writes=[buf("brows")])
        dmaq(GP, d_w[1], brows[4:12, 1, :], brout.rearrange("(r p) -> r p", p=128),
             writes=[buf("brows")])
        dmaq(GP, d_w[1], brows[12:20, 1, :], gln.rearrange("(r p) -> r p", p=128),
             writes=[buf("brows")])
        dmaq(GP, d_w[1], brows[20:28, 1, :], bln.rearrange("(r p) -> r p", p=128),
             writes=[buf("brows")])
        dmaq(GP, d_w[1], brows[0:32, 2, :], b1d.rearrange("(r p) -> r p", p=128),
             writes=[buf("brows")])

        def emit_weight_loads():
            GP.wait((d_w[0], d_w[0].n))
            GP.wait((d_w[1], d_w[1].n))
            for c in range(8):
                dmaq(GP, d_w[c % 2], qkvw_b[:, c, :], wqkv[c * 128:(c + 1) * 128, :],
                     writes=[buf("qkvw_b")])
            for c in range(8):
                dmaq(GP, d_w[c % 2], rkvw_b[:, c, :], wrkv[c * 128:(c + 1) * 128, :],
                     writes=[buf("rkvw_b")])
            for c in range(8):
                dmaq(GP, d_w[c % 2], outw_b[:, c, :], wout[c * 128:(c + 1) * 128, :],
                     writes=[buf("outw_b")])
                dmaq(GP, d_w[c % 2], rqw_b[:, c, :], wrq[c * 128:(c + 1) * 128, :],
                     writes=[buf("rqw_b")])
            for c in range(2):
                dmaq(GP, d_w[c % 2], routw_b[:, c, :], wrout[c * 128:(c + 1) * 128, :],
                     writes=[buf("routw_b")])

        for t_ in (ident_f, ident_b):
            do(GP, "memset", t_[:, :], 0.0, hard=True, writes=[buf(t_.name)])
            do(GP, "affine_select", out=t_[:, :], in_=t_[:, :],
               compare_op=ALU.not_equal, fill=1.0, base=0,
               pattern=[[-1, 128]], channel_multiplier=1, hard=True,
               reads=[buf(t_.name)], writes=[buf(t_.name)])
        do(GP, "memset", zcol[:, :], 0.0, hard=True, writes=[buf("zcol")])
        do(GP, "affine_select", out=zcol[:, :], in_=zcol[:, :],
           compare_op=ALU.not_equal, fill=1.0 / 128.0, base=64,
           pattern=[[-1, 128]], channel_multiplier=0, hard=True,
           reads=[buf("zcol")], writes=[buf("zcol")])
        do(GP, "memset", ones_row[:, :], 1.0, writes=[buf("ones_row")])
        do(GP, "memset", ones64[:, :], 1.0, writes=[buf("ones64")])
        do(GP, "memset", madd[:, :], 0.0, hard=True, writes=[buf("madd")])
        do(GP, "affine_select", out=madd[:, :], in_=madd[:, :],
           compare_op=ALU.is_ge, fill=NEG, base=0,
           pattern=[[1, 256]], channel_multiplier=-1, hard=True,
           reads=[buf("madd")], writes=[buf("madd")])
        do(GP, "affine_select", out=madd[:, :], in_=madd[:, :],
           compare_op=ALU.is_ge, fill=NEG, base=127,
           pattern=[[-1, 256]], channel_multiplier=1, hard=True,
           reads=[buf("madd")], writes=[buf("madd")])
        do(GP, "memset", sel[:, :, :], 0.0, hard=True, writes=[buf("sel")])
        for c in range(8):
            do(GP, "affine_select", out=sel[0:16, c, 0:64], in_=sel[0:16, c, 0:64],
               compare_op=ALU.not_equal, fill=1.0, base=-(2 * c),
               pattern=[[0, 64]], channel_multiplier=1, hard=True,
               reads=[buf("sel")], writes=[buf("sel")])
            do(GP, "affine_select", out=sel[0:16, c, 64:128], in_=sel[0:16, c, 64:128],
               compare_op=ALU.not_equal, fill=1.0, base=-(2 * c + 1),
               pattern=[[0, 64]], channel_multiplier=1, hard=True,
               reads=[buf("sel")], writes=[buf("sel")])

        do(GP, "tensor_copy", madd_b[:, :], madd[:, :],
           reads=[buf("madd")], writes=[buf("madd_b")])
        do(GP, "tensor_copy", madd_t0_b[:, :], madd_t0[:, :],
           reads=[buf("madd_t0s")], writes=[buf("madd_t0b")])
        for du in range(2):
            do(GP, "tensor_copy", mask2[:, du * 256:(du + 1) * 256], madd[:, :],
               reads=[buf("madd")], writes=[buf("mask2")])
            do(GP, "tensor_copy", mask2_t0[:, du * 256:(du + 1) * 256], madd_t0[:, :],
               reads=[buf("madd_t0s")], writes=[buf("mask2_t0")])

        mm_group(PE, [("transpose", (p_bias[:, 0:24], brows[0:24, 0, :], ident_f[0:24, 0:24]), {})],
                 reads=[buf("brows"), buf("ident_f")], writes=[buf("p_bias")])
        mm_group(PE, [("transpose", (p_bias[:, 24:52], brows[0:28, 1, :], ident_f[0:28, 0:28]), {})],
                 reads=[buf("brows"), buf("ident_f")], writes=[buf("p_bias")])
        mm_group(PE, [("transpose", (p_bias[:, 52:84], brows[0:32, 2, :], ident_f[0:32, 0:32]), {})],
                 reads=[buf("brows"), buf("ident_f")], writes=[buf("p_bias")])
        for dst, s0_, s1_, scale in ((qb_col, 0, 8, 0.125), (kb_col, 8, 16, 1.0),
                                     (vb_colT, 16, 24, 1.0),
                                     (rqb_col, 24, 26, 0.125), (rkb_col, 26, 28, 1.0),
                                     (routb_col, 28, 36, 1.0), (g_col, 36, 44, 1.0),
                                     (be_col, 44, 52, 1.0), (f1b_col, 52, 84, 1.0)):
            do(AC, "activation", dst[:, :], p_bias[:, s0_:s1_], AF.Copy, scale=scale,
               reads=[buf("p_bias")], writes=[buf(dst.name)])

        XQ = d_x + d_xb

        def st_load(idx):
            if idx % 2 == 1:
                return
            own = idx < 32
            i = idx if own else idx - 32
            src = xo if own else xp
            s = idx % 8
            dmaq(SP, XQ[(idx // 2) % 6], xt[:, s:s + 2, :],
                 src[i * 128:(i + 2) * 128, :].rearrange("(a p) c -> p a c", p=128),
                 writes=[buf(f"xt{s}"), buf(f"xt{s + 1}")])

        def st_stats(idx):
            s = idx % 8
            par = idx % 4
            bx = buf(f"xt{s}")
            do(VE, "bn_stats", bnst[:, s, 0:6], xt[:, s, 0:512], hard=True,
               reads=[bx], writes=[buf(f"bnst{s}")])
            do(VE, "bn_stats", bnst[:, s, 6:12], xt[:, s, 512:1024], hard=True,
               reads=[bx], writes=[buf(f"bnst{s}")])
            do(VE, "bn_aggr", smalls[:, NSM * par + S_MU:NSM * par + S_VAR + 1],
               bnst[:, s, 0:12], hard=True,
               reads=[buf(f"bnst{s}")],
               writes=[buf(f"sm_mu{par}"), buf(f"sm_var{par}")])
            do(VE, "tensor_scalar", hard=True, out=sc(S_T1, par), in0=sc(S_VAR, par),
               scalar1=EPS, scalar2=None, op0=ALU.add,
               reads=[buf(f"sm_var{par}")], writes=[buf(f"sm_t1{par}")])

        def st_rstd(idx):
            par = idx % 4
            do(AC, "activation", sc(S_T0, par), sc(S_T1, par), AF.Ln, hard=True,
               reads=[buf(f"sm_t1{par}")], writes=[buf(f"sm_t0{par}")])
            do(AC, "activation", sc(S_RSTD, par), sc(S_T0, par), AF.Exp, scale=-0.5,
               hard=True, reads=[buf(f"sm_t0{par}")], writes=[buf(f"sm_rstd{par}")])
            do(VE, "scalar_tensor_tensor", hard=True, out=sc(S_NB, par), in0=sc(S_MU, par),
               scalar=-1.0, in1=sc(S_RSTD, par), op0=ALU.mult, op1=ALU.mult,
               reads=[buf(f"sm_mu{par}"), buf(f"sm_rstd{par}")], writes=[buf(f"sm_nb{par}")])
            if idx < 32:
                do(VE, "tensor_scalar", hard=True, out=sc(S_A1, par), in0=sc(S_RSTD, par),
                   scalar1=1.0, scalar2=None, op0=ALU.add,
                   reads=[buf(f"sm_rstd{par}")], writes=[buf(f"sm_a1{par}")])

        def st_main(idx):
            own = idx < 32
            i = idx if own else idx - 32
            s = idx % 4
            sx = idx % 8
            par = idx % 4
            bx = buf(f"xt{sx}")
            do(AC, "activation", xn_tmb[:, s, :], xt[:, sx, :], AF.Identity,
               scale=sc(S_RSTD, par), bias=sc(S_NB, par),
               reads=[bx, buf(f"sm_rstd{par}"), buf(f"sm_nb{par}")],
               writes=[buf(f"xn{s}")])
            c = i if own else 32 + i
            mm_group(PE, [
                ("matmul", (p_pool[:, 0:512], zcol[:, 64 - c:128 - c], xn_tmb[:, s, 0:512]),
                 dict(start=(idx == 0), stop=(idx == 63))),
                ("matmul", (p_pool[:, 512:1024], zcol[:, 64 - c:128 - c], xn_tmb[:, s, 512:1024]),
                 dict(start=(idx == 0), stop=(idx == 63))),
            ], reads=[buf(f"xn{s}"), buf("zcol")], writes=[buf("p_pool")])
            if own:
                do(VE, "tensor_scalar", out=s0t[:, s, :], in0=xt[:, sx, :],
                   scalar1=sc(S_A1, par), scalar2=sc(S_NB, par),
                   op0=ALU.mult, op1=ALU.add,
                   reads=[bx, buf(f"sm_a1{par}"), buf(f"sm_nb{par}")],
                   writes=[buf(f"s0t{s}")])
                if idx % 2 == 1:
                    dmaq(SP, d_st[(idx // 2) % 2],
                         out[(i - 1) * 128:(i + 1) * 128, :]
                         .rearrange("(a p) c -> p a c", p=128),
                         s0t[:, s - 1:s + 1, :],
                         reads=[buf(f"s0t{s - 1}"), buf(f"s0t{s}")])
                do(VE, "reciprocal", sc(S_T1, par), sc(S_T1, par),
                   reads=[buf(f"sm_t1{par}")], writes=[buf(f"sm_t1{par}")])
                do(VE, "tensor_tensor", sc(S_T0, par), sc(S_VAR, par), sc(S_T1, par), hard=True,
                   op=ALU.mult, reads=[buf(f"sm_var{par}"), buf(f"sm_t1{par}")],
                   writes=[buf(f"sm_t0{par}")])
                do(VE, "tensor_scalar", hard=True, out=sc(S_T0, par), in0=sc(S_T0, par),
                   scalar1=EPS, scalar2=None, op0=ALU.add,
                   reads=[buf(f"sm_t0{par}")], writes=[buf(f"sm_t0{par}")])
                mm_group(PE, [("transpose",
                               (p_tr[:, cc, :], xn_tmb[:, s, cc * 128:(cc + 1) * 128],
                                ident_b[:, :]), {}) for cc in range(8)],
                         reads=[buf(f"xn{s}"), buf("ident_b")], writes=[buf("p_tr")])
                ss = (i // 4) % 2
                do(VE, "tensor_copy", stage[:, ss, :, (i % 4) * 128:(i % 4 + 1) * 128],
                   p_tr[:, :, :], reads=[buf("p_tr")], writes=[buf(f"stage{ss}")])
            if (not own) and i == 31:
                mm_group(PE, [("transpose",
                               (p_tr[:, cc, :], xn_tmb[:, s, cc * 128:(cc + 1) * 128],
                                ident_b[:, :]), {}) for cc in range(8)],
                         reads=[buf(f"xn{s}"), buf("ident_b")], writes=[buf("p_tr")])
                do(VE, "tensor_copy", xn_prev[:, :, :], p_tr[:, :, :],
                   reads=[buf("p_tr")], writes=[buf("xn_prev")])

        def st_tail(idx):
            own = idx < 32
            if not own:
                return
            i = idx
            par = idx % 4
            do(AC, "activation", sc(S_T1, par), sc(S_T0, par), AF.Ln, hard=True,
               reads=[buf(f"sm_t0{par}")], writes=[buf(f"sm_t1{par}")])
            do(AC, "activation", r2sb[:, i:i + 1], sc(S_T1, par), AF.Exp, scale=-0.5,
               hard=True, reads=[buf(f"sm_t1{par}")], writes=[buf("r2sb")])
            ss = (i // 4) % 2
            if i % 4 == 3:
                g0 = i // 4
                dma(GP, d_ss,
                    xnscr[:, :, g0 * 512:(g0 + 1) * 512].rearrange("c p t -> p c t"),
                    stage[:, ss, :, :], reads=[buf(f"stage{ss}")])

        STAGES = (st_load, st_stats, st_rstd, st_main, st_tail)
        for w in range(64 + len(STAGES) - 1):
            if w == 12:
                emit_weight_loads()
            for k, fn in enumerate(STAGES):
                t_ = w - k
                if 0 <= t_ < 64:
                    fn(t_)

        mm_group(PE, [("transpose", (p_bias[0:32, 0:128], r2sb[:, :], ident_f[:, :]), {})],
                 reads=[buf("r2sb"), buf("ident_f")], writes=[buf("p_bias")])
        do(VE, "tensor_copy", brows[0:32, 3, :], p_bias[0:32, 0:128],
           reads=[buf("p_bias")], writes=[buf("brows")])
        dma(GP, d_ss, r2scr[:, :], brows[0:32, 3, :], reads=[buf("brows")])

        do(AC, "activation", pooled_tm[:, :], p_pool[:, :], AF.Copy,
           reads=[buf("p_pool")], writes=[buf("pooled_tm")])
        mm_group(PE, [("transpose",
                       (p_tr[:, cc, 0:64], pooled_tm[0:64, cc * 128:(cc + 1) * 128],
                        ident_b[0:64, 0:64]), {}) for cc in range(8)],
                 reads=[buf("pooled_tm"), buf("ident_b")], writes=[buf("p_tr")])
        do(VE, "tensor_copy", pooled_fm[:, :, :], p_tr[:, :, 0:64],
           reads=[buf("p_tr")], writes=[buf("pooled_fm")])

        barrier()
        emit()
        es1.close()

        run_p1 = 'p1' in parts
        es2 = ExitStack()

        def sb2(name, shape, dt):
            return es2.enter_context(nc.sbuf_tensor(name, shape, dt))

        QPQ = [d_g[0], d_g[1], d_s2[0], d_s2[1]]
        for c in range(8):
            dma(GP, QPQ[c % 4], qscr1[c * 128:(c + 1) * 128, :],
                q1d[c * 128:(c + 1) * 128, :])
        for c in range(32):
            dma(GP, QPQ[c % 4], qscr2[c * 128:(c + 1) * 128, :],
                q2d[c * 128:(c + 1) * 128, :])

        big1 = sb2("big1", [128, 8, 2048], BF16)
        xnfm = sb2("xnfm", [128, 2, 8, 512], BF16)
        q_fm = sb2("q_fm", [128, 8, 512], BF16)
        k_fm = sb2("k_fm", [128, 8, 512], BF16)
        k_roll = sb2("k_roll", [128, 8, 128], BF16)
        vring = sb2("vring", [128, 4, D], BF16)
        probs = sb2("probs", [128, 8, 256], BF16)
        probsT = sb2("probsT", [128, 8, 256], BF16)
        s0sub = sb2("s0sub", [128, 2, D], F32)
        remote_fm = sb2("remote_fm", [128, 8, 512], BF16)
        rq_fm = sb2("rq_fm", [128, 2, 512], BF16)
        latprobs = sb2("latprobs", [128, 4, 64], BF16)
        latTs = sb2("latTs", [64, 4, 128], BF16)
        lat_fm = sb2("lat_fm", [128, 2, 512], BF16)
        gate_sbT = sb2("gate_sbT", [16, 512], BF16)
        gexp = sb2("gexp", [16, 512], F32)

        p_qk = es2.enter_context(nc.psum_tensor("p_qk", [128, 512], F32))
        p_v = es2.enter_context(nc.psum_tensor("p_v", [128, D], F32))
        p_scA = es2.enter_context(nc.psum_tensor("p_scA", [128, 4, 256], F32))
        p_scB = es2.enter_context(nc.psum_tensor("p_scB", [128, 4, 256], F32))
        p_av = es2.enter_context(nc.psum_tensor("p_av", [128, 512], F32))

        if run_p1:
            for mc in range(2 if run_p1 else 0):
                mm_group(PE, [("matmul",
                               (p_qk[:, mc * 64:(mc + 1) * 64], rkvw_b[:, kc, mc * 128:(mc + 1) * 128],
                                pooled_fm[:, kc, :]),
                               dict(start=(kc == 0), stop=(kc == 7))) for kc in range(8)],
                         reads=[buf("rkvw_b"), buf("pooled_fm")], writes=[buf("p_qk")])
            if _P1W >= 1:
                do(AC, "activation", rk_fm[:, 0, :], p_qk[:, 0:64], AF.Identity, bias=rkb_col[:, 0:1],
                   reads=[buf("p_qk"), buf("rkb_col")], writes=[buf("rk_fm")])
                do(AC, "activation", rk_fm[:, 1, :], p_qk[:, 64:128], AF.Identity, bias=rkb_col[:, 1:2],
                   reads=[buf("p_qk"), buf("rkb_col")], writes=[buf("rk_fm")])
                mm_group(PE, [("matmul",
                               (p_v[0:64, 0:256], pooled_fm[:, kc, :], rkvw_b[:, kc, 256:512]),
                               dict(start=(kc == 0), stop=False)) for kc in range(8)] +
                         [("matmul", (p_v[0:64, 0:256], ones_row[0:1, 0:64], rvb_row[0:1, :]),
                           dict(start=False, stop=True))],
                         reads=[buf("rkvw_b"), buf("pooled_fm"), buf("ones_row"), buf("rvb_row")],
                         writes=[buf("p_vh0")])
                do(VE, "tensor_copy", rv_tm[0:64, :, :], p_v[0:64, 0:256],
                   reads=[buf("p_vh0")], writes=[buf("rv_tm")])

            if True:
                def atile(Ti):
                    tl = Ti % 4
                    bs = Ti % 2
                    dma(SP, d_x[bs], xnfm[:, bs, :, :],
                        xnscr[:, :, Ti * 512:(Ti + 1) * 512].rearrange("c p t -> p c t"),
                        writes=[buf(f"xnfm{bs}")])
                    if Ti == 0:
                        for mc in range(8):
                            ph_ = mc % 2
                            pv_h = p_v[:, ph_ * 512:ph_ * 512 + 128]
                            mm_group(PE, [("matmul",
                                           (pv_h, qkvw_b[:, kc, D + mc * 128:D + (mc + 1) * 128],
                                            xn_prev[:, kc, :]),
                                           dict(start=(kc == 0), stop=(kc == 7))) for kc in range(8)],
                                     reads=[buf("qkvw_b"), buf("xn_prev")], writes=[buf(f"p_vh{ph_}")])
                            do(AC, "activation", k_roll[:, mc, :], pv_h, AF.Identity,
                               bias=kb_col[:, mc:mc + 1],
                               reads=[buf(f"p_vh{ph_}"), buf("kb_col")], writes=[buf("k_roll")])
                        mm_group(PE, [("matmul",
                                       (p_v[:, nn * 512:(nn + 1) * 512], xn_prev[:, kc, :],
                                        qkvw_b[:, kc, 2 * D + nn * 512:2 * D + (nn + 1) * 512]),
                                       dict(start=(kc == 0), stop=(kc == 7)))
                                      for nn in range(2) for kc in range(8)],
                                 reads=[buf("qkvw_b"), buf("xn_prev")],
                                 writes=[buf("p_vh0"), buf("p_vh1")])
                        do(AC, "activation", vring[:, 3, :], p_v[:, :], AF.Copy,
                           reads=[buf("p_vh0"), buf("p_vh1")], writes=[buf("vring3")])
                    yield
                    for w0, dst, bcol, scl, nbuf in ((0, q_fm, qb_col, 0.125, "q_fm"),
                                                     (D, k_fm, kb_col, 1.0, "k_fm")):
                        for mc in range(8):
                            ph_ = mc % 2
                            pv_h = p_v[:, ph_ * 512:(ph_ + 1) * 512]
                            mm_group(PE, [("matmul",
                                           (pv_h, qkvw_b[:, kc, w0 + mc * 128:w0 + (mc + 1) * 128],
                                            xnfm[:, bs, kc, :]),
                                           dict(start=(kc == 0), stop=(kc == 7))) for kc in range(8)],
                                     reads=[buf("qkvw_b"), buf(f"xnfm{bs}")], writes=[buf(f"p_vh{ph_}")])
                            do(AC, "activation", dst[:, mc, :], pv_h, AF.Identity,
                               scale=scl, bias=bcol[:, mc:mc + 1],
                               reads=[buf(f"p_vh{ph_}"), buf(bcol.name)], writes=[buf(nbuf)])
                            if mc % 2 == 1:
                                yield
                    def vproj(j, bs=bs, Ti=Ti):
                        slot = (Ti * 4 + j) % 4
                        mm_group(PE, [("matmul",
                                       (p_v[:, nn * 512:(nn + 1) * 512],
                                        xnfm[:, bs, kc, j * 128:(j + 1) * 128],
                                        qkvw_b[:, kc, 2 * D + nn * 512:2 * D + (nn + 1) * 512]),
                                       dict(start=(kc == 0), stop=(kc == 7)))
                                      for nn in range(2) for kc in range(8)],
                                 reads=[buf("qkvw_b"), buf(f"xnfm{bs}")],
                                 writes=[buf("p_vh0"), buf("p_vh1")])
                        do(AC, "activation", vring[:, slot, :], p_v[:, :], AF.Copy,
                           reads=[buf("p_vh0"), buf("p_vh1")], writes=[buf(f"vring{slot}")])
                    for j in range(3):
                        vproj(j)
                        yield
                    for j in range(4):
                        if j == 1:
                            vproj(3)
                            yield
                        Gs = Ti * 4 + j
                        slot = Gs % 4
                        pslot = (Gs - 1) % 4
                        mmask = mask2_t0 if Gs == 0 else mask2
                        mmaskb = buf("mask2_t0") if Gs == 0 else buf("mask2")
                        RMAP = (0, 2, 1, 3)
                        RINV = (0, 2, 1, 3)

                        def sc_stage(g4, j=j, Gs=Gs, mmask=mmask, mmaskb=mmaskb):
                            par = g4 % 2
                            psc = p_scA if par == 0 else p_scB
                            pbuf = f"psc{par}"
                            h0 = g4 * 4
                            mms = []
                            for r, hh in enumerate(RMAP):
                                h = h0 + hh
                                po = (h % 2) * 64
                                hc = h // 2
                                qap = q_fm[po:po + 64, hc, j * 128:(j + 1) * 128]
                                first = (r % 2) == 0
                                if j == 0:
                                    mms.append(("matmul",
                                                (psc[:, r, 0:128], qap, k_roll[po:po + 64, hc, :]),
                                                dict(start=first, stop=False,
                                                     skip_group_check=not first)))
                                    mms.append(("matmul",
                                                (psc[:, r, 128:256], qap, k_fm[po:po + 64, hc, 0:128]),
                                                dict(start=False, stop=False, skip_group_check=True)))
                                else:
                                    mms.append(("matmul",
                                                (psc[:, r, :], qap,
                                                 k_fm[po:po + 64, hc, (j - 1) * 128:(j + 1) * 128]),
                                                dict(start=first, stop=False,
                                                     skip_group_check=not first)))
                                if r % 2 == 1:
                                    mms.append(("matmul",
                                                (psc[:, r - 1:r + 1, :]
                                                 .rearrange("p a b -> p (a b)"),
                                                 ident_b[:, :], mmask[:, :]),
                                                dict(start=False, stop=True)))
                            mm_group(PE, mms,
                                     reads=[buf("q_fm"), buf("k_fm"), buf("k_roll"),
                                            mmaskb, buf("ident_b")],
                                     writes=[buf(pbuf)])

                        def th_stage(g4, j=j):
                            par = g4 % 2
                            psc = p_scA if par == 0 else p_scB
                            pbuf = f"psc{par}"
                            h0 = g4 * 4
                            gsl = slice(h0, h0 + 4)
                            pb = par * 4
                            do(AC, "activation", probs[:, pb:pb + 2, :], psc[:, 0:2, :],
                               AF.Exp, reads=[buf(pbuf)],
                               writes=[buf(f"pr{par}{r}") for r in range(2)])
                            do(VE, "reduce_sum", sums[:, h0:h0 + 2],
                               probs[:, pb:pb + 2, :], axis=AX.X,
                               reads=[buf(f"pr{par}{r}") for r in range(2)],
                               writes=[buf(f"sums{par}a")])
                            do(AC, "activation", probs[:, pb + 2:pb + 4, :], psc[:, 2:4, :],
                               AF.Exp, reads=[buf(pbuf)],
                               writes=[buf(f"pr{par}{r}") for r in range(2, 4)])
                            do(VE, "reduce_sum", sums[:, h0 + 2:h0 + 4],
                               probs[:, pb + 2:pb + 4, :], axis=AX.X,
                               reads=[buf(f"pr{par}{r}") for r in range(2, 4)],
                               writes=[buf(f"sums{par}b")])
                            do(VE, "tensor_scalar", hard=True, out=sums[:, gsl],
                               in0=sums[:, gsl], scalar1=1e-30, scalar2=None, op0=ALU.add,
                               reads=[buf(f"sums{par}a"), buf(f"sums{par}b")],
                               writes=[buf(f"sums{par}")])
                            do(VE, "reciprocal", recip[:, gsl], sums[:, gsl],
                               reads=[buf(f"sums{par}")], writes=[buf(f"recip{par}")])
                            for r in range(4):
                                do(VE, "tensor_scalar",
                                   out=probs[:, pb + r, :],
                                   in0=probs[:, pb + r, :],
                                   scalar1=recip[:, h0 + r:h0 + r + 1], scalar2=None,
                                   op0=ALU.mult,
                                   reads=[buf(f"recip{par}")], writes=[buf(f"pr{par}{r}")])

                        def tf_stage(g4, j=j, tl=tl, slot=slot, pslot=pslot):
                            par = g4 % 2
                            psc_o = p_scB if par == 0 else p_scA
                            obuf = f"psc{1 - par}"
                            psc_bf = psc_o.ap().bitcast(BF16)
                            h0 = g4 * 4
                            pb = par * 4
                            mm_group(PE, [("transpose",
                                           (psc_bf[:, r, kk * 128:(kk + 1) * 128],
                                            probs[:, pb + r, kk * 128:(kk + 1) * 128],
                                            ident_b[:, :]), {})
                                          for r in range(4) for kk in range(2)],
                                     reads=[buf(f"pr{par}{r}") for r in range(4)] +
                                           [buf("ident_b")],
                                     writes=[buf(obuf)])
                            if par == 0:
                                do(VE, "tensor_copy", probsT[:, 0:4, :],
                                   psc_bf[:, :, 0:256],
                                   reads=[buf(obuf)], writes=[buf(f"probsT{par}")])
                            else:
                                do(AC, "activation", probsT[:, 4:8, :],
                                   psc_bf[:, :, 0:256], AF.Copy,
                                   reads=[buf(obuf)], writes=[buf(f"probsT{par}")])
                            mms = []
                            for hh in range(4):
                                h = h0 + hh
                                po = (h % 2) * 64
                                oap = psc_o[po:po + 64, hh // 2, 128:256]
                                pt = probsT[:, par * 4 + RINV[hh], :]
                                mms.append(("matmul",
                                            (oap, vring[:, pslot, h * 64:(h + 1) * 64],
                                             pt[:, 0:128]), dict(start=True, stop=False)))
                                mms.append(("matmul",
                                            (oap, vring[:, slot, h * 64:(h + 1) * 64],
                                             pt[:, 128:256]), dict(start=False, stop=True)))
                            mm_group(PE, mms,
                                     reads=[buf(f"probsT{par}"), buf(f"vring{slot}"),
                                            buf(f"vring{pslot}")],
                                     writes=[buf(obuf)])
                            for gg in range(2):
                                do(AC, "activation",
                                   big1[:, 2 * g4 + gg,
                                        tl * 512 + j * 128:tl * 512 + (j + 1) * 128],
                                   psc_o[:, gg, 128:256], AF.Identity,
                                   bias=vb_colT[:, 2 * g4 + gg:2 * g4 + gg + 1],
                                   reads=[buf(obuf), buf("vb_colT")],
                                   writes=[buf(f"big1c{tl}")])

                        sc_stage(0)
                        yield
                        sc_stage(1)
                        yield
                        th_stage(0)
                        yield
                        th_stage(1)
                        yield
                        tf_stage(0)
                        yield
                        sc_stage(2)
                        yield
                        th_stage(2)
                        yield
                        tf_stage(1)
                        yield
                        sc_stage(3)
                        yield
                        th_stage(3)
                        yield
                        tf_stage(2)
                        yield
                        tf_stage(3)
                        yield
                    do(VE, "tensor_copy", k_roll[:, :, :], k_fm[:, :, 384:512],
                       reads=[buf("k_fm")], writes=[buf("k_roll")])
                    if Ti == 0:
                        do(VE, "tensor_scalar", out=big1[:, :, 0:1], in0=big1[:, :, 0:1],
                           scalar1=tok0col[:, 0:1], scalar2=None, op0=ALU.mult,
                           reads=[buf("tok0col")], writes=[buf("big1c0")])

                def btile(Ti):
                    tl = Ti % 4
                    bs = Ti % 2
                    dma(SP, d_x[bs], xnfm[:, bs, :, :],
                        xnscr[:, :, Ti * 512:(Ti + 1) * 512].rearrange("c p t -> p c t"),
                        writes=[buf(f"xnfm{bs}")])
                    for mc in range(2):
                        pp, ppb = (p_qk, "p_qk") if mc % 2 == 0 else (p_av, "p_av")
                        mm_group(PE, [("matmul",
                                       (pp[:, :], rqw_b[:, kc, mc * 128:(mc + 1) * 128],
                                        xnfm[:, bs, kc, :]),
                                       dict(start=(kc == 0), stop=(kc == 7))) for kc in range(8)],
                                 reads=[buf("rqw_b"), buf(f"xnfm{bs}")], writes=[buf(ppb)])
                        do(AC, "activation", rq_fm[:, mc, :], pp[:, :], AF.Identity,
                           scale=0.125, bias=rqb_col[:, mc:mc + 1],
                           reads=[buf(ppb), buf("rqb_col")], writes=[buf("rq_fm")])
                        yield
                    pav_bf = p_av.ap().bitcast(BF16)
                    for j in range(4):
                        Gs = Ti * 4 + j
                        for half in range(2):
                            tk = mm_group(PE, [("matmul",
                                           (p_qk[:, lh * 64:(lh + 1) * 64],
                                            rq_fm[(lh % 2) * 64:(lh % 2) * 64 + 64, lh // 2,
                                                  j * 128:(j + 1) * 128],
                                            rk_fm[(lh % 2) * 64:(lh % 2) * 64 + 64, lh // 2, :]),
                                           dict(start=True, stop=True)) for lh in (half, half + 2)],
                                     reads=[buf("rq_fm"), buf("rk_fm")], writes=[buf("p_qk")])
                            if half == 0:
                                PE.wait(tk, force=True)
                        latview = p_qk[:, 0:256].rearrange("p (a b) -> p a b", b=64)
                        do(VE, "tensor_tensor", latview, latview,
                           latmadd[:, Gs, :].unsqueeze(1).broadcast_to([128, 4, 64]),
                           op=ALU.add, hard=True, reads=[buf("latmadd")], writes=[buf("p_qk")])
                        yield
                        do(AC, "activation", latprobs[:, :, :],
                           p_qk[:, 0:256].rearrange("p (a b) -> p a b", b=64),
                           AF.Exp, reads=[buf("p_qk")], writes=[buf("latprobs")])
                        do(VE, "reduce_sum", sums[:, 8:12], latprobs[:, :, :],
                           axis=AX.X, reads=[buf("latprobs")], writes=[buf("sumsL")])
                        do(VE, "tensor_scalar", hard=True, out=sums[:, 8:12],
                           in0=sums[:, 8:12], scalar1=1e-30, scalar2=None, op0=ALU.add,
                           reads=[buf("sumsL")], writes=[buf("sumsL")])
                        do(VE, "reciprocal", recip[:, 8:12], sums[:, 8:12],
                           reads=[buf("sumsL")], writes=[buf("recipL")])
                        yield
                        for lh in range(LH):
                            do(VE, "tensor_scalar", out=latprobs[:, lh, :], in0=latprobs[:, lh, :],
                               scalar1=recip[:, 8 + lh:9 + lh], scalar2=None, op0=ALU.mult,
                               reads=[buf("recipL")], writes=[buf("latprobs")])
                        mm_group(PE, [("transpose",
                                       (pav_bf[0:64, lh * 128:(lh + 1) * 128], latprobs[:, lh, :],
                                        ident_b[:, :]), {}) for lh in range(LH)],
                                 reads=[buf("latprobs"), buf("ident_b")], writes=[buf("p_av")])
                        do(VE, "tensor_copy", latTs[0:64, :, :],
                           pav_bf[0:64, 0:512].rearrange("p (a b) -> p a b", b=128),
                           reads=[buf("p_av")], writes=[buf("latTs")])
                        mms = []
                        for lh in range(LH):
                            po = (lh % 2) * 64
                            hp = lh // 2
                            oap = p_qk[po:po + 64, 256 + hp * 128:256 + (hp + 1) * 128]
                            mms.append(("matmul", (oap, rv_tm[0:64, lh, :], latTs[0:64, lh, :]),
                                        dict(start=True, stop=True)))
                        mm_group(PE, mms, reads=[buf("rv_tm"), buf("latTs")], writes=[buf("p_qk")])
                        do(AC, "activation", lat_fm[:, :, j * 128:(j + 1) * 128],
                           p_qk[:, 256:512].rearrange("p (a b) -> p a b", a=2),
                           AF.Copy, reads=[buf("p_qk")], writes=[buf("lat_fm")])
                        yield
                    yield
                    if Ti == 0:
                        do(VE, "tensor_scalar", out=lat_fm[:, :, 0:128], in0=lat_fm[:, :, 0:128],
                           scalar1=tok0col[:, 0:1], scalar2=None, op0=ALU.mult,
                           reads=[buf("tok0col")], writes=[buf("lat_fm")])
                    for mc in range(8):
                        pp, ppb = (p_qk, "p_qk") if mc % 2 == 0 else (p_av, "p_av")
                        mm_group(PE, [("matmul",
                                       (pp[:, :], routw_b[:, kc, mc * 128:(mc + 1) * 128],
                                        lat_fm[:, kc, :]),
                                       dict(start=(kc == 0), stop=(kc == 1))) for kc in range(2)],
                                 reads=[buf("routw_b"), buf("lat_fm")], writes=[buf(ppb)])
                        do(AC, "activation", remote_fm[:, mc, :], pp[:, :], AF.Identity,
                           bias=routb_col[:, mc:mc + 1],
                           reads=[buf(ppb), buf("routb_col")], writes=[buf("remote_fm")])
                        if mc % 2 == 1:
                            yield
                    mm_group(PE,
                             [("matmul", (p_av[0:16, :], glbd_b[:, kc, :],
                                          big1[:, kc, tl * 512:(tl + 1) * 512]),
                               dict(start=(kc == 0), stop=False)) for kc in range(8)] +
                             [("matmul", (p_av[0:16, :], grbd_b[:, kc, :], remote_fm[:, kc, :]),
                               dict(start=False, stop=(kc == 7))) for kc in range(8)],
                             reads=[buf("glbd_b"), buf("grbd_b"), buf(f"big1c{tl}"), buf("remote_fm")],
                             writes=[buf("p_av")])
                    do(AC, "activation", gexp[:, :], p_av[0:16, :], AF.Exp, scale=-1.0,
                       reads=[buf("p_av")], writes=[buf("gexp")])
                    do(VE, "tensor_scalar", out=gexp[:, :], in0=gexp[:, :], scalar1=1.0,
                       scalar2=None, op0=ALU.add, reads=[], writes=[buf("gexp")])
                    do(VE, "reciprocal", gate_sbT[:, :], gexp[:, :],
                       reads=[buf("gexp")], writes=[buf("gate_sbT")])
                    yield
                    for cch in range(8):
                        pp, ppb = (p_qk, "p_qk") if cch % 2 == 0 else (p_av, "p_av")
                        mm_group(PE, [("matmul", (pp[:, :], sel[0:16, cch, :], gate_sbT[:, :]),
                                       dict(start=True, stop=True))],
                                 reads=[buf("sel"), buf("gate_sbT")], writes=[buf(ppb)])
                        bcols = slice(tl * 512, (tl + 1) * 512)
                        do(VE, "tensor_tensor", big1[:, cch, bcols], big1[:, cch, bcols],
                           remote_fm[:, cch, :], op=ALU.subtract, writes=[buf(f"big1c{tl}")],
                           reads=[buf("remote_fm")])
                        do(VE, "tensor_tensor", big1[:, cch, bcols], big1[:, cch, bcols],
                           pp[:, :], op=ALU.mult, reads=[buf(ppb)], writes=[buf(f"big1c{tl}")])
                        do(VE, "tensor_tensor", big1[:, cch, bcols], big1[:, cch, bcols],
                           remote_fm[:, cch, :], op=ALU.add, reads=[buf("remote_fm")],
                           writes=[buf(f"big1c{tl}")])
                        if cch % 2 == 1:
                            yield
                    for j in range(4):
                        bs2 = j % 2
                        rows = slice(Ti * 512 + j * 128, Ti * 512 + (j + 1) * 128)
                        if j % 2 == 0:
                            rows2 = slice(Ti * 512 + j * 128, Ti * 512 + (j + 2) * 128)
                            dma(SP, d_x[2 + (j // 2)], s0sub[:, :, :],
                                out[rows2, :].rearrange("(a p) c -> p a c", p=128),
                                writes=[buf("s0sub0"), buf("s0sub1")])
                        mm_group(PE, [("matmul",
                                       (p_qk[:, :],
                                        big1[:, kc, tl * 512 + j * 128:tl * 512 + (j + 1) * 128],
                                        outw_b[:, kc, 0:512]),
                                       dict(start=(kc == 0), stop=(kc == 7)))
                                      for kc in range(8)],
                                 reads=[buf(f"big1c{tl}"), buf("outw_b")], writes=[buf("p_qk")])
                        mm_group(PE, [("matmul",
                                       (p_av[:, :],
                                        big1[:, kc, tl * 512 + j * 128:tl * 512 + (j + 1) * 128],
                                        outw_b[:, kc, 512:1024]),
                                       dict(start=(kc == 0), stop=(kc == 7)))
                                      for kc in range(8)],
                                 reads=[buf(f"big1c{tl}"), buf("outw_b")], writes=[buf("p_av")])
                        do(VE, "tensor_tensor", s0sub[:, bs2, 0:512], s0sub[:, bs2, 0:512],
                           p_qk[:, :], op=ALU.add, reads=[buf("p_qk")], writes=[buf(f"s0sub{bs2}")])
                        do(VE, "tensor_tensor", s0sub[:, bs2, 512:1024], s0sub[:, bs2, 512:1024],
                           p_av[:, :], op=ALU.add, reads=[buf("p_av")], writes=[buf(f"s0sub{bs2}")])
                        if j % 2 == 1:
                            rows2 = slice(Ti * 512 + (j - 1) * 128, Ti * 512 + (j + 1) * 128)
                            dmaq(SP, d_st[j // 2], out[rows2, :]
                                 .rearrange("(a p) c -> p a c", p=128),
                                 s0sub[:, :, :],
                                 reads=[buf("s0sub0"), buf("s0sub1")])
                        yield
                ntiles = 2 * _P1G if _P1T >= 4 else _P1T
                for Tix in range(ntiles):
                    ga = atile(Tix)
                    gb = btile(Tix - 1) if (Tix >= 1 and _P1B) else None
                    alive = True
                    while alive:
                        alive = False
                        try:
                            next(ga)
                            alive = True
                        except StopIteration:
                            pass
                        if gb is not None:
                            try:
                                next(gb)
                                alive = True
                            except StopIteration:
                                gb = None
                    while gb is not None:
                        try:
                            next(gb)
                        except StopIteration:
                            gb = None
                if _P1B and ntiles >= 1:
                    for _ in btile(ntiles - 1):
                        pass

        barrier()
        emit()
        es2.close()
        es_w.close()

        run_p2 = 'p2' in parts
        es3 = ExitStack()

        def sb3(name, shape, dt):
            return es3.enter_context(nc.sbuf_tensor(name, shape, dt))

        w1b = sb3("w1b", [128, 8, FF], BF16)
        w2b = sb3("w2b", [128, 32, D], BF16)
        yb = sb3("yb", [128, 32, 256], BF16)
        xnfm2 = sb3("xnfm2", [128, 2, 8, 256], BF16)
        s1sub = sb3("s1sub", [128, 2, D], F32)
        qstage = sb3("qstage", [128, 4, 1024], BF16)
        scrows_f = sb3("scrows_f", [64, 1024], F32)
        scb = sb3("scb", [64, 2, 1024], BF16)
        onehotb = sb3("onehotb", [64, 2, 128], BF16)
        cbtmp = sb3("cbtmp", [128, D], F32)
        cb_bc = sb3("cb_bc", [128, D], F32)
        r2bc = sb3("r2bc", [128, 2, 256], F32)
        p_f1 = es3.enter_context(nc.psum_tensor("p_f1", [128, 2, 512], F32))
        p_z = es3.enter_context(nc.psum_tensor("p_z", [128, 2, D], F32))
        p_sc = es3.enter_context(nc.psum_tensor("p_sc", [128, 1024], F32))

        if run_p2:
            QQ = [d_g[0], d_g[1], d_s2[0], d_s2[1]]
            dma(GP, d_ss, scrows_f[0:32, :],
                s1d.rearrange("r (w c) -> (r w) c", c=1024), writes=[buf("scrows")])
            dma(GP, d_ss, scrows_f[32:64, :], s2d, writes=[buf("scrows")])
            do(VE, "tensor_copy", scb[:, 0, :], scrows_f[:, :],
               reads=[buf("scrows")], writes=[buf("scb")])
            do(VE, "tensor_tensor", scb[:, 1, :], scrows_f[:, :], scb[:, 0, :],
               op=ALU.subtract, reads=[buf("scrows")], writes=[buf("scb")])

            def deq_piece(i2, kc, wv, qsrc, srow, dstw, bufname):
                sq = i2 % 4
                oh = i2 % 2
                dma(GP, QQ[sq], qstage[:, sq, :],
                    qsrc[kc * 128:(kc + 1) * 128, wv * 1024:(wv + 1) * 1024],
                    writes=[buf(f"qst{sq}")])
                do(GP, "tensor_copy", onehotb[:, oh, :],
                   ident_b[0:64, srow:srow + 1].broadcast_to([64, 128]),
                   reads=[buf("ident_b")], writes=[buf(f"oh{oh}")])
                mm_group(PE, [
                    ("matmul", (p_sc[:, hf * 512:(hf + 1) * 512], onehotb[0:64, oh, :],
                                scb[0:64, hl, hf * 512:(hf + 1) * 512]),
                     dict(start=(hl == 0), stop=(hl == 1)))
                    for hf in range(2) for hl in range(2)
                ], reads=[buf(f"oh{oh}"), buf("scb")], writes=[buf("p_sc")])
                do(VE, "scalar_tensor_tensor",
                   out=dstw[:, kc, wv * 1024:(wv + 1) * 1024],
                   in0=qstage[:, sq, :], scalar=-128.0, in1=p_sc[:, :],
                   op0=ALU.add, op1=ALU.mult,
                   reads=[buf(f"qst{sq}"), buf("p_sc")],
                   writes=[buf(bufname)])

            for i2, (kc, wv) in enumerate((kc, wv) for wv in range(4) for kc in range(8)):
                deq_piece(i2, kc, wv, qscr1, kc * 4 + wv, w1b, f"w1b{wv}")

            dma(GP, d_w[0], cbtmp[:, :],
                b2d.unsqueeze(0).partition_broadcast(128), writes=[buf("cbt")])
            dma(GP, d_w[1], cb_bc[:, :],
                bout.unsqueeze(0).partition_broadcast(128), writes=[buf("cb_bc")])
            do(VE, "scalar_tensor_tensor", out=cb_bc[:, :],
               in0=cbtmp[:, :], scalar=0.5,
               in1=cb_bc[:, :], op0=ALU.mult, op1=ALU.add,
               reads=[buf("cbt"), buf("cb_bc")],
               writes=[buf("cb_bc")])

            def deq2_gen():
                for i2, kc in enumerate(range(32)):
                    deq_piece(32 + i2, kc, 0, qscr2, 32 + kc, w2b, f"w2b{kc // 8}")
                    yield

            def p2_load(Ti):
                bs = Ti % 2
                cols = slice(Ti * 256, (Ti + 1) * 256)
                dma(SP, d_x[bs], xnfm2[:, bs, :, :],
                    xnscr[:, :, cols].rearrange("c p t -> p c t"), writes=[buf(f"xnfm2{bs}")])
                dma(GP, d_w[bs], r2bc[:, bs, :],
                    r2scr.rearrange("a b -> (a b)")[Ti * 256:(Ti + 1) * 256]
                    .unsqueeze(0).partition_broadcast(128)[:, 0, :],
                    writes=[buf(f"r2bc{bs}")])

            def p2_norm(Ti):
                bs = Ti % 2
                for cch in range(8):
                    do(GP, "tensor_tensor", xnfm2[:, bs, cch, :], xnfm2[:, bs, cch, :],
                       r2bc[:, bs, :],
                       op=ALU.mult, reads=[buf(f"r2bc{bs}")], writes=[buf(f"xnfm2{bs}")])
                    do(AC, "activation", xnfm2[:, bs, cch, :], xnfm2[:, bs, cch, :], AF.Identity,
                       scale=g_col[:, cch:cch + 1], bias=be_col[:, cch:cch + 1],
                       reads=[buf("g_col"), buf("be_col")], writes=[buf(f"xnfm2{bs}")])

            def p2_fc1(Ti, deq_iter=None):
                bs = Ti % 2
                for mc in range(32):
                    pf = mc % 2
                    mm_group(PE, [("matmul",
                                   (p_f1[:, pf, 0:256], w1b[:, kc, mc * 128:(mc + 1) * 128],
                                    xnfm2[:, bs, kc, :]),
                                   dict(start=(kc == 0), stop=(kc == 7))) for kc in range(8)],
                             reads=[buf(f"w1b{mc // 8}"), buf(f"xnfm2{bs}")], writes=[buf(f"p_f1{pf}")])
                    do(AC, "activation", yb[:, mc, :], p_f1[:, pf, 0:256], AF.Silu,
                       bias=f1b_col[:, mc:mc + 1],
                       reads=[buf(f"p_f1{pf}"), buf("f1b_col")], writes=[buf("yb")])
                    if deq_iter is not None:
                        next(deq_iter, None)

            def p2_fc2(Ti):
                rows2 = slice(Ti * 256, (Ti + 1) * 256)
                dma(SP, d_x[2], s1sub[:, :, :],
                    out[rows2, :].rearrange("(a p) c -> p a c", p=128),
                    writes=[buf("s1s0"), buf("s1s1")])
                for j in range(2):
                    rows = slice(Ti * 256 + j * 128, Ti * 256 + (j + 1) * 128)
                    for gi in range(4):
                        mm_group(PE, [("matmul",
                                       (p_z[:, j, nn * 512:(nn + 1) * 512],
                                        yb[:, kc, j * 128:(j + 1) * 128],
                                        w2b[:, kc, nn * 512:(nn + 1) * 512]),
                                       dict(start=(kc == 0), stop=(kc == 31)))
                                      for nn in range(2) for kc in range(gi * 8, (gi + 1) * 8)],
                                 reads=[buf("yb"), buf(f"w2b{gi}")], writes=[buf(f"p_z{j}")])
                    do(VE, "scalar_tensor_tensor", out=p_z[:, j, :], in0=p_z[:, j, :],
                       scalar=0.5, in1=cb_bc[:, :], op0=ALU.mult, op1=ALU.add,
                       reads=[buf("cb_bc")], writes=[buf(f"p_z{j}")], hard=True)
                    do(VE, "tensor_tensor", s1sub[:, j, :], s1sub[:, j, :], p_z[:, j, :],
                       op=ALU.add, reads=[buf(f"p_z{j}")], writes=[buf(f"s1s{j}")])
                    if j == 1:
                        dmaq(SP, d_st[Ti % 2], out[rows2, :]
                             .rearrange("(a p) c -> p a c", p=128),
                             s1sub[:, :, :],
                             reads=[buf("s1s0"), buf("s1s1")])

            p2_load(0)
            p2_norm(0)
            dq2 = deq2_gen()
            for Ti in range(16):
                if Ti + 1 < 16:
                    p2_load(Ti + 1)
                p2_fc1(Ti, dq2 if Ti == 0 else None)
                if Ti == 0:
                    for _ in dq2:
                        pass
                if Ti + 1 < 16:
                    p2_norm(Ti + 1)
                p2_fc2(Ti)

        barrier()
        emit()
        es3.close()
    es0.close()
    return nc


_CACHED = {}


def _consts(half):
    a = np.arange(128)[:, None]
    b = np.arange(256)[None, :]
    if half == 0:
        madd_t0 = np.where((b >= 128) & (b - 128 < a), 0.0, NEG).astype(np.float32)
        vrow_t0 = np.ones((1, 128), np.float32)
        vrow_t0[0, 0] = 0.0
        tok0 = np.zeros((128, 1), np.float32)
    else:
        madd_t0 = np.where((b >= a) & (b <= a + 127), 0.0, NEG).astype(np.float32)
        vrow_t0 = np.ones((1, 128), np.float32)
        tok0 = np.ones((128, 1), np.float32)
    c0 = 32 * half
    latm = np.full((32, 64), NEG, np.float32)
    for s in range(32):
        c_abs = c0 + s
        for l_loc in range(64):
            l_glob = (c0 + l_loc) if l_loc < 32 else ((32 - c0) + (l_loc - 32))
            if l_glob < c_abs:
                latm[s, l_loc] = 0.0
    return madd_t0, vrow_t0, tok0, latm


def kernel(L0_attn_x, L0_attn_qkv_w, L0_attn_qkv_b, L0_attn_rq_w, L0_attn_rq_b,
           L0_attn_rkv_w, L0_attn_rkv_b, L0_attn_rout_w, L0_attn_rout_b,
           L0_attn_out_w, L0_attn_out_b, L0_attn_gl_w, L0_attn_gr_w,
           L0_ffn_ln_gamma, L0_ffn_ln_beta, L0_ffn_fc1_q, L0_ffn_fc1_scales,
           L0_ffn_fc1_bias, L0_ffn_fc2_q, L0_ffn_fc2_scales, L0_ffn_fc2_bias):
    x = np.ascontiguousarray(np.asarray(L0_attn_x, dtype=np.float32))
    c32 = lambda v: np.ascontiguousarray(np.asarray(v, dtype=np.float32))
    ci32 = lambda v: np.ascontiguousarray(np.asarray(v, dtype=np.int32))

    glbd = np.zeros((D, H), np.float32)
    grbd = np.zeros((D, H), np.float32)
    gl = np.asarray(L0_attn_gl_w, np.float32)[:, 0]
    gr = np.asarray(L0_attn_gr_w, np.float32)[:, 0]
    for h in range(H):
        glbd[h * HD:(h + 1) * HD, h] = gl
        grbd[h * HD:(h + 1) * HD, h] = gr

    common = dict(
        wqkv=c32(L0_attn_qkv_w), bqkv=c32(L0_attn_qkv_b),
        wrq=c32(L0_attn_rq_w), brq=c32(L0_attn_rq_b),
        wrkv=c32(L0_attn_rkv_w), brkv=c32(L0_attn_rkv_b),
        wrout=c32(L0_attn_rout_w), brout=c32(L0_attn_rout_b),
        wout=c32(L0_attn_out_w), bout=c32(L0_attn_out_b),
        glbd=glbd, grbd=grbd,
        gln=c32(L0_ffn_ln_gamma), bln=c32(L0_ffn_ln_beta),
        q1d=ci32(L0_ffn_fc1_q), s1d=c32(L0_ffn_fc1_scales), b1d=c32(L0_ffn_fc1_bias),
        q2d=ci32(L0_ffn_fc2_q), s2d=c32(L0_ffn_fc2_scales), b2d=c32(L0_ffn_fc2_bias),
    )
    halves = [_consts(0), _consts(1)]

    if "nc" not in _CACHED:
        _CACHED["nc"] = build_nc()
    nc = _CACHED["nc"]

    in_maps = []
    for core in range(8):
        bb = core // 2
        half = core % 2
        madd_t0, vrow_t0, tok0, latm = halves[half]
        m = dict(common)
        m["xo"] = np.ascontiguousarray(x[bb, half * T:(half + 1) * T])
        m["xp"] = np.ascontiguousarray(x[bb, (1 - half) * T:(2 - half) * T])
        m["madd_t0"] = madd_t0
        m["vrow_t0"] = vrow_t0
        m["tok0"] = tok0
        m["latm"] = latm
        in_maps.append(m)

    res = run_bass_kernel_spmd(nc, in_maps, core_ids=list(range(8)))
    outs = [np.asarray(r["out"]) for r in res.results]
    full = np.zeros((4, 2 * T, D), np.float32)
    for core in range(8):
        bb = core // 2
        half = core % 2
        full[bb, half * T:(half + 1) * T] = outs[core]
    return full


# version 43
# speedup vs baseline: 1.0093x; 1.0093x over previous
import os
import numpy as np
from contextlib import ExitStack
_KDBG = os.environ.get('KDBG', '')
_P1G = int(os.environ.get('P1G', '2'))
_P1B = os.environ.get('P1B', '1') == '1'
_P1T = int(os.environ.get('P1T', '4'))
_P1S = int(os.environ.get('P1S', '4'))
_P1W = int(os.environ.get('P1W', '2'))
_P1Q = int(os.environ.get('P1Q', '2'))

import concourse.bass as bass
import concourse.mybir as mybir
from concourse.bass_utils import run_bass_kernel_spmd

F32 = mybir.dt.float32
BF16 = mybir.dt.bfloat16
I32 = mybir.dt.int32
AF = mybir.ActivationFunctionType
ALU = mybir.AluOpType
AX = mybir.AxisListType

T = 4096
D = 1024
H = 16
HD = 64
LD = 256
LH = 4
FF = 4096
EPS = 1e-5
NEG = -1e9


class Sem:
    __slots__ = ("h", "n", "dma")

    def __init__(self, h, dma=False):
        self.h = h
        self.n = dma
        self.n = 0
        self.dma = dma


HARD_TOKS = set()


class St:
    def __init__(self, name, sem):
        self.name = name
        self.sem = sem
        self.ops = []
        self.wm = {}

    def wait(self, tok, force=False):
        if tok is None:
            return
        sem, val = tok
        if sem.dma:
            val = sem.n
        if sem is self.sem and not force and (id(sem), val) not in HARD_TOKS:
            return
        if self.wm.get(id(sem), 0) >= val:
            return
        self.wm[id(sem)] = val
        self.ops.append(("w", sem, val))

    def raw(self, meth, *a, **kw):
        self.ops.append(["o", meth, a, kw, []])

    def mark(self):
        self.sem.n += 1
        self.ops[-1][4].append((self.sem, 1))
        return (self.sem, self.sem.n)


class Buf:
    __slots__ = ("w", "r")

    def __init__(self):
        self.w = None
        self.r = []


_FENCED = ("vector", "scalar", "gpsimd")


def _pre(st, reads, writes):
    f = st.name in _FENCED
    for b in reads:
        if b.w is not None:
            st.wait(b.w, force=f)
    for b in writes:
        if b.w is not None:
            st.wait(b.w, force=f)
        for t in b.r:
            st.wait(t, force=f)


def _post(tok, reads, writes):
    for b in reads:
        b.r.append(tok)
    for b in writes:
        b.w = tok
        b.r = []


HARD_METHS = ("reduce_sum", "reduce_max", "tensor_reduce", "reciprocal")


def do(st, meth, *a, reads=(), writes=(), hard=False, **kw):
    _pre(st, reads, writes)
    st.raw(meth, *a, **kw)
    tok = st.mark()
    if hard or meth in HARD_METHS or kw.get("accum_out") is not None:
        HARD_TOKS.add((id(tok[0]), tok[1]))
    _post(tok, reads, writes)
    return tok


def dma(st, dsem, out, in_, reads=(), writes=(), **kw):
    _pre(st, reads, writes)
    st.wait((dsem, dsem.n))
    st.raw("dma_start", out=out, in_=in_, **kw)
    dsem.n += 16
    st.ops[-1][4].append((dsem, 16))
    tok = (dsem, dsem.n)
    _post(tok, reads, writes)
    return tok


def dmaq(st, dsem, out, in_, reads=(), writes=(), **kw):
    _pre(st, reads, writes)
    st.raw("dma_start", out=out, in_=in_, **kw)
    dsem.n += 16
    st.ops[-1][4].append((dsem, 16))
    tok = (dsem, dsem.n)
    _post(tok, reads, writes)
    return tok


def mm_group(st, mms, reads=(), writes=()):
    _pre(st, reads, writes)
    for meth, a, kw in mms:
        st.raw(meth, *a, **kw)
    tok = st.mark()
    _post(tok, reads, writes)
    return tok


def build_nc(parts=('p1', 'p2')):
    nc = bass.Bass("TRN2", target_bir_lowering=False)

    xo = nc.dram_tensor("xo", [T, D], F32, kind="ExternalInput").ap()
    xp = nc.dram_tensor("xp", [T, D], F32, kind="ExternalInput").ap()
    wqkv = nc.dram_tensor("wqkv", [D, 3 * D], F32, kind="ExternalInput").ap()
    bqkv = nc.dram_tensor("bqkv", [3 * D], F32, kind="ExternalInput").ap()
    wrq = nc.dram_tensor("wrq", [D, LD], F32, kind="ExternalInput").ap()
    brq = nc.dram_tensor("brq", [LD], F32, kind="ExternalInput").ap()
    wrkv = nc.dram_tensor("wrkv", [D, 2 * LD], F32, kind="ExternalInput").ap()
    brkv = nc.dram_tensor("brkv", [2 * LD], F32, kind="ExternalInput").ap()
    wrout = nc.dram_tensor("wrout", [LD, D], F32, kind="ExternalInput").ap()
    brout = nc.dram_tensor("brout", [D], F32, kind="ExternalInput").ap()
    wout = nc.dram_tensor("wout", [D, D], F32, kind="ExternalInput").ap()
    bout = nc.dram_tensor("bout", [D], F32, kind="ExternalInput").ap()
    glbd = nc.dram_tensor("glbd", [D, H], F32, kind="ExternalInput").ap()
    grbd = nc.dram_tensor("grbd", [D, H], F32, kind="ExternalInput").ap()
    gln = nc.dram_tensor("gln", [D], F32, kind="ExternalInput").ap()
    bln = nc.dram_tensor("bln", [D], F32, kind="ExternalInput").ap()
    q1d = nc.dram_tensor("q1d", [D, FF], I32, kind="ExternalInput").ap()
    s1d = nc.dram_tensor("s1d", [8, FF], F32, kind="ExternalInput").ap()
    b1d = nc.dram_tensor("b1d", [FF], F32, kind="ExternalInput").ap()
    q2d = nc.dram_tensor("q2d", [FF, D], I32, kind="ExternalInput").ap()
    s2d = nc.dram_tensor("s2d", [32, D], F32, kind="ExternalInput").ap()
    b2d = nc.dram_tensor("b2d", [D], F32, kind="ExternalInput").ap()
    madd_t0_in = nc.dram_tensor("madd_t0", [128, 256], F32, kind="ExternalInput").ap()
    vrow_t0_in = nc.dram_tensor("vrow_t0", [1, 128], F32, kind="ExternalInput").ap()
    tok0_in = nc.dram_tensor("tok0", [128, 1], F32, kind="ExternalInput").ap()
    latm_in = nc.dram_tensor("latm", [32, 64], F32, kind="ExternalInput").ap()

    out = nc.dram_tensor("out", [T, D], F32, kind="ExternalOutput").ap()
    xnscr = nc.dram_tensor("xnscr", [8, 128, T], BF16, kind="Internal").ap()
    r2scr = nc.dram_tensor("r2scr", [32, 128], F32, kind="Internal").ap()
    qscr1 = nc.dram_tensor("qscr1", [D, FF], BF16, kind="Internal").ap()
    qscr2 = nc.dram_tensor("qscr2", [FF, D], BF16, kind="Internal").ap()

    es0 = ExitStack()

    def mksem(name):
        return Sem(es0.enter_context(nc.semaphore(name)))

    def mkdma(name):
        return Sem(es0.enter_context(nc.semaphore(name)), dma=True)

    d_x = [mkdma("d_x0"), mkdma("d_x1"), mkdma("d_x2"), mkdma("d_x3")]
    d_st = [mkdma("d_st0"), mkdma("d_st1")]
    d_w = [mkdma("d_w0"), mkdma("d_w1")]
    d_g = [mkdma("d_g0"), mkdma("d_g1")]
    d_s2 = [mkdma("d_s20"), mkdma("d_s21")]
    d_xb = [mkdma("d_xb0"), mkdma("d_xb1")]
    d_ss = mkdma("d_ss")
    DMAS = d_x + d_st + d_w + d_g + d_s2 + d_xb + [d_ss]
    ph = mksem("ph")
    SP = St("sync", mksem("s_sp"))
    GP = St("gpsimd", mksem("s_gp"))
    VE = St("vector", mksem("s_ve"))
    AC = St("scalar", mksem("s_ac"))
    PE = St("tensor", mksem("s_pe"))
    ALL = [SP, GP, VE, AC, PE]

    def barrier():
        for ds in DMAS:
            SP.wait((ds, ds.n))
            GP.wait((ds, ds.n))
        for st in ALL:
            st.raw("nop")
            ph.n += 1
            st.ops[-1][4].append((ph, 1))
        tgt = ph.n
        for st in ALL:
            st.wait((ph, tgt))

    B = {}

    def buf(name):
        if name not in B:
            B[name] = Buf()
        return B[name]

    def sb0(name, shape, dt):
        return es0.enter_context(nc.sbuf_tensor(name, shape, dt))

    ident_f = sb0("ident_f", [128, 128], F32)
    ident_b = sb0("ident_b", [128, 128], BF16)
    zcol = sb0("zcol", [128, 128], BF16)
    ones_row = sb0("ones_row", [1, 128], BF16)
    ones64 = sb0("ones64", [64, 128], BF16)
    madd = sb0("madd", [128, 256], F32)
    madd_t0 = sb0("madd_t0s", [128, 256], F32)
    madd_b = sb0("madd_b", [128, 256], BF16)
    madd_t0_b = sb0("madd_t0b", [128, 256], BF16)
    mask2 = sb0("mask2", [128, 512], BF16)
    mask2_t0 = sb0("mask2_t0", [128, 512], BF16)
    latmadd = sb0("latmadd", [128, 32, 64], BF16)
    tok0col = sb0("tok0col", [128, 1], F32)
    qb_col = sb0("qb_col", [128, 8], F32)
    kb_col = sb0("kb_col", [128, 8], F32)
    vb_colT = sb0("vb_colT", [128, 8], F32)
    rqb_col = sb0("rqb_col", [128, 2], F32)
    rkb_col = sb0("rkb_col", [128, 2], F32)
    rvb_row = sb0("rvb_row", [1, LD], BF16)
    routb_col = sb0("routb_col", [128, 8], F32)
    f1b_col = sb0("f1b_col", [128, 32], F32)
    g_col = sb0("g_col", [128, 8], F32)
    be_col = sb0("be_col", [128, 8], F32)
    glbd_b = sb0("glbd_b", [128, 8, H], BF16)
    grbd_b = sb0("grbd_b", [128, 8, H], BF16)
    sel = sb0("sel", [16, 8, 128], BF16)
    pooled_fm = sb0("pooled_fm", [128, 8, 64], BF16)
    rk_fm = sb0("rk_fm", [128, 2, 64], BF16)
    rv_tm = sb0("rv_tm", [64, 4, 64], BF16)
    xn_prev = sb0("xn_prev", [128, 8, 128], BF16)
    r2sb = sb0("r2sb", [128, 32], F32)
    smalls = sb0("smalls", [128, 32], F32)
    negmax = sb0("negmax", [128, 16], F32)
    sums = sb0("sums", [128, 16], F32)
    recip = sb0("recip", [128, 16], F32)

    NSM = 8

    def sc(i, par):
        j = i + NSM * par
        return smalls[:, j:j + 1]

    S_MU, S_VAR, S_RSTD, S_NB, S_A1, S_T0, S_T1 = range(7)

    with nc.Block() as block:

        def emit():
            for st in ALL:
                ops = st.ops
                st.ops = []
                if not ops:
                    continue

                def run(eng, ops=ops):
                    for o in ops:
                        if o[0] == "w":
                            eng.wait_ge(o[1].h, o[2])
                        else:
                            instr = getattr(eng, o[1])(*o[2], **o[3])
                            for (s, k) in o[4]:
                                instr.then_inc(s.h, k)

                with nc.allow_low_precision(reason="bf16 score psum per stated bf16-matmul bar"):
                    getattr(block, st.name)(run)

        es_w = ExitStack()

        def sbw(name, shape, dt):
            return es_w.enter_context(nc.sbuf_tensor(name, shape, dt))

        qkvw_b = sbw("qkvw_b", [128, 8, 3 * D], BF16)
        outw_b = sbw("outw_b", [128, 8, D], BF16)
        rqw_b = sbw("rqw_b", [128, 8, LD], BF16)
        routw_b = sbw("routw_b", [128, 2, D], BF16)
        rkvw_b = sbw("rkvw_b", [128, 8, 2 * LD], BF16)

        es1 = ExitStack()

        def sb1(name, shape, dt):
            return es1.enter_context(nc.sbuf_tensor(name, shape, dt))

        xt = sb1("xt", [128, 8, D], F32)
        xn_tmb = sb1("xn_tmb", [128, 4, D], BF16)
        s0t = sb1("s0t", [128, 4, D], F32)
        stage = sb1("stage", [128, 2, 8, 512], BF16)
        bnst = sb1("bnst", [128, 8, 12], F32)
        brows = sb1("brows", [32, 4, 128], F32)
        pooled_tm = sb1("pooled_tm", [64, D], BF16)
        p_pool = es1.enter_context(nc.psum_tensor("p_pool", [64, D], F32))
        p_tr = es1.enter_context(nc.psum_tensor("p_tr", [128, 8, 128], BF16))
        p_bias = es1.enter_context(nc.psum_tensor("p_bias", [128, 128], F32))

        dmaq(GP, d_w[0], latmadd[:, :, :], latm_in.partition_broadcast(128),
             writes=[buf("latmadd")])
        dmaq(GP, d_w[0], rvb_row[:, :], brkv[LD:2 * LD].unsqueeze(0),
             writes=[buf("rvb_row")])
        for c in range(8):
            dmaq(GP, d_w[0], glbd_b[:, c, :], glbd[c * 128:(c + 1) * 128, :],
                 writes=[buf("glbd_b")])
            dmaq(GP, d_w[0], grbd_b[:, c, :], grbd[c * 128:(c + 1) * 128, :],
                 writes=[buf("grbd_b")])
        dmaq(GP, d_w[1], madd_t0[:, :], madd_t0_in, writes=[buf("madd_t0s")])
        dmaq(GP, d_w[1], tok0col[:, :], tok0_in, writes=[buf("tok0col")])
        dmaq(GP, d_w[1], brows[0:24, 0, :],
             bqkv.rearrange("(r p) -> r p", p=128)[0:24, :], writes=[buf("brows")])
        dmaq(GP, d_w[1], brows[0:2, 1, :], brq.rearrange("(r p) -> r p", p=128),
             writes=[buf("brows")])
        dmaq(GP, d_w[1], brows[2:4, 1, :],
             brkv.rearrange("(r p) -> r p", p=128)[0:2, :], writes=[buf("brows")])
        dmaq(GP, d_w[1], brows[4:12, 1, :], brout.rearrange("(r p) -> r p", p=128),
             writes=[buf("brows")])
        dmaq(GP, d_w[1], brows[12:20, 1, :], gln.rearrange("(r p) -> r p", p=128),
             writes=[buf("brows")])
        dmaq(GP, d_w[1], brows[20:28, 1, :], bln.rearrange("(r p) -> r p", p=128),
             writes=[buf("brows")])
        dmaq(GP, d_w[1], brows[0:32, 2, :], b1d.rearrange("(r p) -> r p", p=128),
             writes=[buf("brows")])

        def emit_weight_loads():
            GP.wait((d_w[0], d_w[0].n))
            GP.wait((d_w[1], d_w[1].n))
            for c in range(8):
                dmaq(GP, d_w[c % 2], qkvw_b[:, c, :], wqkv[c * 128:(c + 1) * 128, :],
                     writes=[buf("qkvw_b")])
            for c in range(8):
                dmaq(GP, d_w[c % 2], rkvw_b[:, c, :], wrkv[c * 128:(c + 1) * 128, :],
                     writes=[buf("rkvw_b")])
            for c in range(8):
                dmaq(GP, d_w[c % 2], outw_b[:, c, :], wout[c * 128:(c + 1) * 128, :],
                     writes=[buf("outw_b")])
                dmaq(GP, d_w[c % 2], rqw_b[:, c, :], wrq[c * 128:(c + 1) * 128, :],
                     writes=[buf("rqw_b")])
            for c in range(2):
                dmaq(GP, d_w[c % 2], routw_b[:, c, :], wrout[c * 128:(c + 1) * 128, :],
                     writes=[buf("routw_b")])

        for t_ in (ident_f, ident_b):
            do(GP, "memset", t_[:, :], 0.0, hard=True, writes=[buf(t_.name)])
            do(GP, "affine_select", out=t_[:, :], in_=t_[:, :],
               compare_op=ALU.not_equal, fill=1.0, base=0,
               pattern=[[-1, 128]], channel_multiplier=1, hard=True,
               reads=[buf(t_.name)], writes=[buf(t_.name)])
        do(GP, "memset", zcol[:, :], 0.0, hard=True, writes=[buf("zcol")])
        do(GP, "affine_select", out=zcol[:, :], in_=zcol[:, :],
           compare_op=ALU.not_equal, fill=1.0 / 128.0, base=64,
           pattern=[[-1, 128]], channel_multiplier=0, hard=True,
           reads=[buf("zcol")], writes=[buf("zcol")])
        do(GP, "memset", ones_row[:, :], 1.0, writes=[buf("ones_row")])
        do(GP, "memset", ones64[:, :], 1.0, writes=[buf("ones64")])
        do(GP, "memset", madd[:, :], 0.0, hard=True, writes=[buf("madd")])
        do(GP, "affine_select", out=madd[:, :], in_=madd[:, :],
           compare_op=ALU.is_ge, fill=NEG, base=0,
           pattern=[[1, 256]], channel_multiplier=-1, hard=True,
           reads=[buf("madd")], writes=[buf("madd")])
        do(GP, "affine_select", out=madd[:, :], in_=madd[:, :],
           compare_op=ALU.is_ge, fill=NEG, base=127,
           pattern=[[-1, 256]], channel_multiplier=1, hard=True,
           reads=[buf("madd")], writes=[buf("madd")])
        do(GP, "memset", sel[:, :, :], 0.0, hard=True, writes=[buf("sel")])
        for c in range(8):
            do(GP, "affine_select", out=sel[0:16, c, 0:64], in_=sel[0:16, c, 0:64],
               compare_op=ALU.not_equal, fill=1.0, base=-(2 * c),
               pattern=[[0, 64]], channel_multiplier=1, hard=True,
               reads=[buf("sel")], writes=[buf("sel")])
            do(GP, "affine_select", out=sel[0:16, c, 64:128], in_=sel[0:16, c, 64:128],
               compare_op=ALU.not_equal, fill=1.0, base=-(2 * c + 1),
               pattern=[[0, 64]], channel_multiplier=1, hard=True,
               reads=[buf("sel")], writes=[buf("sel")])

        do(GP, "tensor_copy", madd_b[:, :], madd[:, :],
           reads=[buf("madd")], writes=[buf("madd_b")])
        do(GP, "tensor_copy", madd_t0_b[:, :], madd_t0[:, :],
           reads=[buf("madd_t0s")], writes=[buf("madd_t0b")])
        for du in range(2):
            do(GP, "tensor_copy", mask2[:, du * 256:(du + 1) * 256], madd[:, :],
               reads=[buf("madd")], writes=[buf("mask2")])
            do(GP, "tensor_copy", mask2_t0[:, du * 256:(du + 1) * 256], madd_t0[:, :],
               reads=[buf("madd_t0s")], writes=[buf("mask2_t0")])

        mm_group(PE, [("transpose", (p_bias[:, 0:24], brows[0:24, 0, :], ident_f[0:24, 0:24]), {})],
                 reads=[buf("brows"), buf("ident_f")], writes=[buf("p_bias")])
        mm_group(PE, [("transpose", (p_bias[:, 24:52], brows[0:28, 1, :], ident_f[0:28, 0:28]), {})],
                 reads=[buf("brows"), buf("ident_f")], writes=[buf("p_bias")])
        mm_group(PE, [("transpose", (p_bias[:, 52:84], brows[0:32, 2, :], ident_f[0:32, 0:32]), {})],
                 reads=[buf("brows"), buf("ident_f")], writes=[buf("p_bias")])
        for dst, s0_, s1_, scale in ((qb_col, 0, 8, 0.125), (kb_col, 8, 16, 1.0),
                                     (vb_colT, 16, 24, 1.0),
                                     (rqb_col, 24, 26, 0.125), (rkb_col, 26, 28, 1.0),
                                     (routb_col, 28, 36, 1.0), (g_col, 36, 44, 1.0),
                                     (be_col, 44, 52, 1.0), (f1b_col, 52, 84, 1.0)):
            do(AC, "activation", dst[:, :], p_bias[:, s0_:s1_], AF.Copy, scale=scale,
               reads=[buf("p_bias")], writes=[buf(dst.name)])

        XQ = d_x + d_xb

        def st_load(idx):
            if idx % 2 == 1:
                return
            own = idx < 32
            i = idx if own else idx - 32
            src = xo if own else xp
            s = idx % 8
            dmaq(SP, XQ[(idx // 2) % 6], xt[:, s:s + 2, :],
                 src[i * 128:(i + 2) * 128, :].rearrange("(a p) c -> p a c", p=128),
                 writes=[buf(f"xt{s}"), buf(f"xt{s + 1}")])

        def st_stats(idx):
            s = idx % 8
            par = idx % 4
            bx = buf(f"xt{s}")
            do(VE, "bn_stats", bnst[:, s, 0:6], xt[:, s, 0:512], hard=True,
               reads=[bx], writes=[buf(f"bnst{s}")])
            do(VE, "bn_stats", bnst[:, s, 6:12], xt[:, s, 512:1024], hard=True,
               reads=[bx], writes=[buf(f"bnst{s}")])
            do(VE, "bn_aggr", smalls[:, NSM * par + S_MU:NSM * par + S_VAR + 1],
               bnst[:, s, 0:12], hard=True,
               reads=[buf(f"bnst{s}")],
               writes=[buf(f"sm_mu{par}"), buf(f"sm_var{par}")])
            do(VE, "tensor_scalar", hard=True, out=sc(S_T1, par), in0=sc(S_VAR, par),
               scalar1=EPS, scalar2=None, op0=ALU.add,
               reads=[buf(f"sm_var{par}")], writes=[buf(f"sm_t1{par}")])

        def st_rstd(idx):
            par = idx % 4
            do(AC, "activation", sc(S_T0, par), sc(S_T1, par), AF.Ln, hard=True,
               reads=[buf(f"sm_t1{par}")], writes=[buf(f"sm_t0{par}")])
            do(AC, "activation", sc(S_RSTD, par), sc(S_T0, par), AF.Exp, scale=-0.5,
               hard=True, reads=[buf(f"sm_t0{par}")], writes=[buf(f"sm_rstd{par}")])
            do(VE, "scalar_tensor_tensor", hard=True, out=sc(S_NB, par), in0=sc(S_MU, par),
               scalar=-1.0, in1=sc(S_RSTD, par), op0=ALU.mult, op1=ALU.mult,
               reads=[buf(f"sm_mu{par}"), buf(f"sm_rstd{par}")], writes=[buf(f"sm_nb{par}")])
            if idx < 32:
                do(VE, "tensor_scalar", hard=True, out=sc(S_A1, par), in0=sc(S_RSTD, par),
                   scalar1=1.0, scalar2=None, op0=ALU.add,
                   reads=[buf(f"sm_rstd{par}")], writes=[buf(f"sm_a1{par}")])

        def st_main(idx):
            own = idx < 32
            i = idx if own else idx - 32
            s = idx % 4
            sx = idx % 8
            par = idx % 4
            bx = buf(f"xt{sx}")
            do(AC, "activation", xn_tmb[:, s, :], xt[:, sx, :], AF.Identity,
               scale=sc(S_RSTD, par), bias=sc(S_NB, par),
               reads=[bx, buf(f"sm_rstd{par}"), buf(f"sm_nb{par}")],
               writes=[buf(f"xn{s}")])
            c = i if own else 32 + i
            mm_group(PE, [
                ("matmul", (p_pool[:, 0:512], zcol[:, 64 - c:128 - c], xn_tmb[:, s, 0:512]),
                 dict(start=(idx == 0), stop=(idx == 63))),
                ("matmul", (p_pool[:, 512:1024], zcol[:, 64 - c:128 - c], xn_tmb[:, s, 512:1024]),
                 dict(start=(idx == 0), stop=(idx == 63))),
            ], reads=[buf(f"xn{s}"), buf("zcol")], writes=[buf("p_pool")])
            if own:
                do(VE, "tensor_scalar", out=s0t[:, s, :], in0=xt[:, sx, :],
                   scalar1=sc(S_A1, par), scalar2=sc(S_NB, par),
                   op0=ALU.mult, op1=ALU.add,
                   reads=[bx, buf(f"sm_a1{par}"), buf(f"sm_nb{par}")],
                   writes=[buf(f"s0t{s}")])
                if idx % 2 == 1:
                    dmaq(SP, d_st[(idx // 2) % 2],
                         out[(i - 1) * 128:(i + 1) * 128, :]
                         .rearrange("(a p) c -> p a c", p=128),
                         s0t[:, s - 1:s + 1, :],
                         reads=[buf(f"s0t{s - 1}"), buf(f"s0t{s}")])
                do(VE, "reciprocal", sc(S_T1, par), sc(S_T1, par),
                   reads=[buf(f"sm_t1{par}")], writes=[buf(f"sm_t1{par}")])
                do(VE, "tensor_tensor", sc(S_T0, par), sc(S_VAR, par), sc(S_T1, par), hard=True,
                   op=ALU.mult, reads=[buf(f"sm_var{par}"), buf(f"sm_t1{par}")],
                   writes=[buf(f"sm_t0{par}")])
                do(VE, "tensor_scalar", hard=True, out=sc(S_T0, par), in0=sc(S_T0, par),
                   scalar1=EPS, scalar2=None, op0=ALU.add,
                   reads=[buf(f"sm_t0{par}")], writes=[buf(f"sm_t0{par}")])
                mm_group(PE, [("transpose",
                               (p_tr[:, cc, :], xn_tmb[:, s, cc * 128:(cc + 1) * 128],
                                ident_b[:, :]), {}) for cc in range(8)],
                         reads=[buf(f"xn{s}"), buf("ident_b")], writes=[buf("p_tr")])
                ss = (i // 4) % 2
                do(VE, "tensor_copy", stage[:, ss, :, (i % 4) * 128:(i % 4 + 1) * 128],
                   p_tr[:, :, :], reads=[buf("p_tr")], writes=[buf(f"stage{ss}")])
            if (not own) and i == 31:
                mm_group(PE, [("transpose",
                               (p_tr[:, cc, :], xn_tmb[:, s, cc * 128:(cc + 1) * 128],
                                ident_b[:, :]), {}) for cc in range(8)],
                         reads=[buf(f"xn{s}"), buf("ident_b")], writes=[buf("p_tr")])
                do(VE, "tensor_copy", xn_prev[:, :, :], p_tr[:, :, :],
                   reads=[buf("p_tr")], writes=[buf("xn_prev")])

        def st_tail(idx):
            own = idx < 32
            if not own:
                return
            i = idx
            par = idx % 4
            do(AC, "activation", sc(S_T1, par), sc(S_T0, par), AF.Ln, hard=True,
               reads=[buf(f"sm_t0{par}")], writes=[buf(f"sm_t1{par}")])
            do(AC, "activation", r2sb[:, i:i + 1], sc(S_T1, par), AF.Exp, scale=-0.5,
               hard=True, reads=[buf(f"sm_t1{par}")], writes=[buf("r2sb")])
            ss = (i // 4) % 2
            if i % 4 == 3:
                g0 = i // 4
                dma(GP, d_ss,
                    xnscr[:, :, g0 * 512:(g0 + 1) * 512].rearrange("c p t -> p c t"),
                    stage[:, ss, :, :], reads=[buf(f"stage{ss}")])

        STAGES = (st_load, st_stats, st_rstd, st_main, st_tail)
        for w in range(64 + len(STAGES) - 1):
            if w == 12:
                emit_weight_loads()
            for k, fn in enumerate(STAGES):
                t_ = w - k
                if 0 <= t_ < 64:
                    fn(t_)

        mm_group(PE, [("transpose", (p_bias[0:32, 0:128], r2sb[:, :], ident_f[:, :]), {})],
                 reads=[buf("r2sb"), buf("ident_f")], writes=[buf("p_bias")])
        do(VE, "tensor_copy", brows[0:32, 3, :], p_bias[0:32, 0:128],
           reads=[buf("p_bias")], writes=[buf("brows")])
        dma(GP, d_ss, r2scr[:, :], brows[0:32, 3, :], reads=[buf("brows")])

        do(AC, "activation", pooled_tm[:, :], p_pool[:, :], AF.Copy,
           reads=[buf("p_pool")], writes=[buf("pooled_tm")])
        mm_group(PE, [("transpose",
                       (p_tr[:, cc, 0:64], pooled_tm[0:64, cc * 128:(cc + 1) * 128],
                        ident_b[0:64, 0:64]), {}) for cc in range(8)],
                 reads=[buf("pooled_tm"), buf("ident_b")], writes=[buf("p_tr")])
        do(VE, "tensor_copy", pooled_fm[:, :, :], p_tr[:, :, 0:64],
           reads=[buf("p_tr")], writes=[buf("pooled_fm")])

        barrier()
        emit()
        es1.close()

        run_p1 = 'p1' in parts
        es2 = ExitStack()

        def sb2(name, shape, dt):
            return es2.enter_context(nc.sbuf_tensor(name, shape, dt))

        QPQ = [d_g[0], d_g[1], d_s2[0], d_s2[1]]
        for c in range(8):
            dma(GP, QPQ[c % 4], qscr1[c * 128:(c + 1) * 128, :],
                q1d[c * 128:(c + 1) * 128, :])
        for c in range(32):
            dma(GP, QPQ[c % 4], qscr2[c * 128:(c + 1) * 128, :],
                q2d[c * 128:(c + 1) * 128, :])

        big1 = sb2("big1", [128, 8, 2048], BF16)
        xnfm = sb2("xnfm", [128, 2, 8, 512], BF16)
        q_fm = sb2("q_fm", [128, 8, 512], BF16)
        k_fm = sb2("k_fm", [128, 8, 512], BF16)
        k_roll = sb2("k_roll", [128, 8, 128], BF16)
        vring = sb2("vring", [128, 4, D], BF16)
        probs = sb2("probs", [128, 8, 256], BF16)
        probsT = sb2("probsT", [128, 8, 256], BF16)
        s0sub = sb2("s0sub", [128, 2, D], F32)
        remote_fm = sb2("remote_fm", [128, 8, 512], BF16)
        rq_fm = sb2("rq_fm", [128, 2, 512], BF16)
        latprobs = sb2("latprobs", [128, 4, 64], BF16)
        latTs = sb2("latTs", [64, 4, 128], BF16)
        lat_fm = sb2("lat_fm", [128, 2, 512], BF16)
        gate_sbT = sb2("gate_sbT", [16, 512], BF16)
        gexp = sb2("gexp", [16, 512], F32)

        p_qk = es2.enter_context(nc.psum_tensor("p_qk", [128, 512], F32))
        p_v = es2.enter_context(nc.psum_tensor("p_v", [128, D], F32))
        p_scA = es2.enter_context(nc.psum_tensor("p_scA", [128, 4, 256], F32))
        p_scB = es2.enter_context(nc.psum_tensor("p_scB", [128, 4, 256], F32))
        p_av = es2.enter_context(nc.psum_tensor("p_av", [128, 512], F32))

        if run_p1:
            for mc in range(2 if run_p1 else 0):
                mm_group(PE, [("matmul",
                               (p_qk[:, mc * 64:(mc + 1) * 64], rkvw_b[:, kc, mc * 128:(mc + 1) * 128],
                                pooled_fm[:, kc, :]),
                               dict(start=(kc == 0), stop=(kc == 7))) for kc in range(8)],
                         reads=[buf("rkvw_b"), buf("pooled_fm")], writes=[buf("p_qk")])
            if _P1W >= 1:
                do(AC, "activation", rk_fm[:, 0, :], p_qk[:, 0:64], AF.Identity, bias=rkb_col[:, 0:1],
                   reads=[buf("p_qk"), buf("rkb_col")], writes=[buf("rk_fm")])
                do(AC, "activation", rk_fm[:, 1, :], p_qk[:, 64:128], AF.Identity, bias=rkb_col[:, 1:2],
                   reads=[buf("p_qk"), buf("rkb_col")], writes=[buf("rk_fm")])
                mm_group(PE, [("matmul",
                               (p_v[0:64, 0:256], pooled_fm[:, kc, :], rkvw_b[:, kc, 256:512]),
                               dict(start=(kc == 0), stop=False)) for kc in range(8)] +
                         [("matmul", (p_v[0:64, 0:256], ones_row[0:1, 0:64], rvb_row[0:1, :]),
                           dict(start=False, stop=True))],
                         reads=[buf("rkvw_b"), buf("pooled_fm"), buf("ones_row"), buf("rvb_row")],
                         writes=[buf("p_vh0")])
                do(VE, "tensor_copy", rv_tm[0:64, :, :], p_v[0:64, 0:256],
                   reads=[buf("p_vh0")], writes=[buf("rv_tm")])

            if True:
                def atile(Ti):
                    tl = Ti % 4
                    bs = Ti % 2
                    dma(SP, d_x[bs], xnfm[:, bs, :, :],
                        xnscr[:, :, Ti * 512:(Ti + 1) * 512].rearrange("c p t -> p c t"),
                        writes=[buf(f"xnfm{bs}")])
                    if Ti == 0:
                        for mc in range(8):
                            ph_ = mc % 2
                            pv_h = p_v[:, ph_ * 512:ph_ * 512 + 128]
                            mm_group(PE, [("matmul",
                                           (pv_h, qkvw_b[:, kc, D + mc * 128:D + (mc + 1) * 128],
                                            xn_prev[:, kc, :]),
                                           dict(start=(kc == 0), stop=(kc == 7))) for kc in range(8)],
                                     reads=[buf("qkvw_b"), buf("xn_prev")], writes=[buf(f"p_vh{ph_}")])
                            do(AC, "activation", k_roll[:, mc, :], pv_h, AF.Identity,
                               bias=kb_col[:, mc:mc + 1],
                               reads=[buf(f"p_vh{ph_}"), buf("kb_col")], writes=[buf("k_roll")])
                        mm_group(PE, [("matmul",
                                       (p_v[:, nn * 512:(nn + 1) * 512], xn_prev[:, kc, :],
                                        qkvw_b[:, kc, 2 * D + nn * 512:2 * D + (nn + 1) * 512]),
                                       dict(start=(kc == 0), stop=(kc == 7)))
                                      for nn in range(2) for kc in range(8)],
                                 reads=[buf("qkvw_b"), buf("xn_prev")],
                                 writes=[buf("p_vh0"), buf("p_vh1")])
                        do(AC, "activation", vring[:, 3, :], p_v[:, :], AF.Copy,
                           reads=[buf("p_vh0"), buf("p_vh1")], writes=[buf("vring3")])
                    yield
                    for w0, dst, bcol, scl, nbuf in ((0, q_fm, qb_col, 0.125, "q_fm"),
                                                     (D, k_fm, kb_col, 1.0, "k_fm")):
                        for mc in range(8):
                            ph_ = mc % 2
                            pv_h = p_v[:, ph_ * 512:(ph_ + 1) * 512]
                            mm_group(PE, [("matmul",
                                           (pv_h, qkvw_b[:, kc, w0 + mc * 128:w0 + (mc + 1) * 128],
                                            xnfm[:, bs, kc, :]),
                                           dict(start=(kc == 0), stop=(kc == 7))) for kc in range(8)],
                                     reads=[buf("qkvw_b"), buf(f"xnfm{bs}")], writes=[buf(f"p_vh{ph_}")])
                            do(AC, "activation", dst[:, mc, :], pv_h, AF.Identity,
                               scale=scl, bias=bcol[:, mc:mc + 1],
                               reads=[buf(f"p_vh{ph_}"), buf(bcol.name)], writes=[buf(nbuf)])
                            if mc % 2 == 1:
                                yield
                    def vproj(j, bs=bs, Ti=Ti):
                        slot = (Ti * 4 + j) % 4
                        mm_group(PE, [("matmul",
                                       (p_v[:, nn * 512:(nn + 1) * 512],
                                        xnfm[:, bs, kc, j * 128:(j + 1) * 128],
                                        qkvw_b[:, kc, 2 * D + nn * 512:2 * D + (nn + 1) * 512]),
                                       dict(start=(kc == 0), stop=(kc == 7)))
                                      for nn in range(2) for kc in range(8)],
                                 reads=[buf("qkvw_b"), buf(f"xnfm{bs}")],
                                 writes=[buf("p_vh0"), buf("p_vh1")])
                        do(AC, "activation", vring[:, slot, :], p_v[:, :], AF.Copy,
                           reads=[buf("p_vh0"), buf("p_vh1")], writes=[buf(f"vring{slot}")])
                    for j in range(3):
                        vproj(j)
                        yield
                    for j in range(4):
                        if j == 1:
                            vproj(3)
                            yield
                        Gs = Ti * 4 + j
                        slot = Gs % 4
                        pslot = (Gs - 1) % 4
                        mmask = mask2_t0 if Gs == 0 else mask2
                        mmaskb = buf("mask2_t0") if Gs == 0 else buf("mask2")
                        RMAP = (0, 2, 1, 3)
                        RINV = (0, 2, 1, 3)

                        def sc_stage(g4, j=j, Gs=Gs, mmask=mmask, mmaskb=mmaskb):
                            par = g4 % 2
                            psc = p_scA if par == 0 else p_scB
                            pbuf = f"psc{par}"
                            h0 = g4 * 4
                            mms = []
                            for r, hh in enumerate(RMAP):
                                h = h0 + hh
                                po = (h % 2) * 64
                                hc = h // 2
                                qap = q_fm[po:po + 64, hc, j * 128:(j + 1) * 128]
                                first = (r % 2) == 0
                                if j == 0:
                                    mms.append(("matmul",
                                                (psc[:, r, 0:128], qap, k_roll[po:po + 64, hc, :]),
                                                dict(start=first, stop=False,
                                                     skip_group_check=not first)))
                                    mms.append(("matmul",
                                                (psc[:, r, 128:256], qap, k_fm[po:po + 64, hc, 0:128]),
                                                dict(start=False, stop=False, skip_group_check=True)))
                                else:
                                    mms.append(("matmul",
                                                (psc[:, r, :], qap,
                                                 k_fm[po:po + 64, hc, (j - 1) * 128:(j + 1) * 128]),
                                                dict(start=first, stop=False,
                                                     skip_group_check=not first)))
                                if r % 2 == 1:
                                    mms.append(("matmul",
                                                (psc[:, r - 1:r + 1, :]
                                                 .rearrange("p a b -> p (a b)"),
                                                 ident_b[:, :], mmask[:, :]),
                                                dict(start=False, stop=True)))
                            mm_group(PE, mms,
                                     reads=[buf("q_fm"), buf("k_fm"), buf("k_roll"),
                                            mmaskb, buf("ident_b")],
                                     writes=[buf(pbuf)])

                        def th_stage(g4, j=j):
                            par = g4 % 2
                            psc = p_scA if par == 0 else p_scB
                            pbuf = f"psc{par}"
                            h0 = g4 * 4
                            gsl = slice(h0, h0 + 4)
                            pb = par * 4
                            do(AC, "activation", probs[:, pb:pb + 2, :], psc[:, 0:2, :],
                               AF.Exp, reads=[buf(pbuf)],
                               writes=[buf(f"pr{par}{r}") for r in range(2)])
                            do(VE, "reduce_sum", sums[:, h0:h0 + 2],
                               probs[:, pb:pb + 2, :], axis=AX.X,
                               reads=[buf(f"pr{par}{r}") for r in range(2)],
                               writes=[buf(f"sums{par}a")])
                            do(AC, "activation", probs[:, pb + 2:pb + 4, :], psc[:, 2:4, :],
                               AF.Exp, reads=[buf(pbuf)],
                               writes=[buf(f"pr{par}{r}") for r in range(2, 4)])
                            do(VE, "reduce_sum", sums[:, h0 + 2:h0 + 4],
                               probs[:, pb + 2:pb + 4, :], axis=AX.X,
                               reads=[buf(f"pr{par}{r}") for r in range(2, 4)],
                               writes=[buf(f"sums{par}b")])
                            do(VE, "tensor_scalar", hard=True, out=sums[:, gsl],
                               in0=sums[:, gsl], scalar1=1e-30, scalar2=None, op0=ALU.add,
                               reads=[buf(f"sums{par}a"), buf(f"sums{par}b")],
                               writes=[buf(f"sums{par}")])
                            do(VE, "reciprocal", recip[:, gsl], sums[:, gsl],
                               reads=[buf(f"sums{par}")], writes=[buf(f"recip{par}")])
                            for r in range(4):
                                do(VE, "tensor_scalar",
                                   out=probs[:, pb + r, :],
                                   in0=probs[:, pb + r, :],
                                   scalar1=recip[:, h0 + r:h0 + r + 1], scalar2=None,
                                   op0=ALU.mult,
                                   reads=[buf(f"recip{par}")], writes=[buf(f"pr{par}{r}")])

                        def tf_stage(g4, j=j, tl=tl, slot=slot, pslot=pslot):
                            par = g4 % 2
                            psc_o = p_scB if par == 0 else p_scA
                            obuf = f"psc{1 - par}"
                            psc_bf = psc_o.ap().bitcast(BF16)
                            h0 = g4 * 4
                            pb = par * 4
                            mm_group(PE, [("transpose",
                                           (psc_bf[:, r, kk * 128:(kk + 1) * 128],
                                            probs[:, pb + r, kk * 128:(kk + 1) * 128],
                                            ident_b[:, :]), {})
                                          for r in range(4) for kk in range(2)],
                                     reads=[buf(f"pr{par}{r}") for r in range(4)] +
                                           [buf("ident_b")],
                                     writes=[buf(obuf)])
                            if par == 0:
                                do(VE, "tensor_copy", probsT[:, 0:4, :],
                                   psc_bf[:, :, 0:256],
                                   reads=[buf(obuf)], writes=[buf(f"probsT{par}")])
                            else:
                                do(AC, "activation", probsT[:, 4:8, :],
                                   psc_bf[:, :, 0:256], AF.Copy,
                                   reads=[buf(obuf)], writes=[buf(f"probsT{par}")])
                            mms = []
                            for hh in range(4):
                                h = h0 + hh
                                po = (h % 2) * 64
                                oap = psc_o[po:po + 64, hh // 2, 128:256]
                                pt = probsT[:, par * 4 + RINV[hh], :]
                                mms.append(("matmul",
                                            (oap, vring[:, pslot, h * 64:(h + 1) * 64],
                                             pt[:, 0:128]), dict(start=True, stop=False)))
                                mms.append(("matmul",
                                            (oap, vring[:, slot, h * 64:(h + 1) * 64],
                                             pt[:, 128:256]), dict(start=False, stop=True)))
                            mm_group(PE, mms,
                                     reads=[buf(f"probsT{par}"), buf(f"vring{slot}"),
                                            buf(f"vring{pslot}")],
                                     writes=[buf(obuf)])
                            for gg in range(2):
                                do(AC, "activation",
                                   big1[:, 2 * g4 + gg,
                                        tl * 512 + j * 128:tl * 512 + (j + 1) * 128],
                                   psc_o[:, gg, 128:256], AF.Identity,
                                   bias=vb_colT[:, 2 * g4 + gg:2 * g4 + gg + 1],
                                   reads=[buf(obuf), buf("vb_colT")],
                                   writes=[buf(f"big1c{tl}")])

                        sc_stage(0)
                        yield
                        sc_stage(1)
                        yield
                        th_stage(0)
                        yield
                        th_stage(1)
                        yield
                        tf_stage(0)
                        yield
                        sc_stage(2)
                        yield
                        th_stage(2)
                        yield
                        tf_stage(1)
                        yield
                        sc_stage(3)
                        yield
                        th_stage(3)
                        yield
                        tf_stage(2)
                        yield
                        tf_stage(3)
                        yield
                    do(VE, "tensor_copy", k_roll[:, :, :], k_fm[:, :, 384:512],
                       reads=[buf("k_fm")], writes=[buf("k_roll")])
                    if Ti == 0:
                        do(VE, "tensor_scalar", out=big1[:, :, 0:1], in0=big1[:, :, 0:1],
                           scalar1=tok0col[:, 0:1], scalar2=None, op0=ALU.mult,
                           reads=[buf("tok0col")], writes=[buf("big1c0")])

                def btile(Ti):
                    tl = Ti % 4
                    bs = Ti % 2
                    dma(SP, d_x[bs], xnfm[:, bs, :, :],
                        xnscr[:, :, Ti * 512:(Ti + 1) * 512].rearrange("c p t -> p c t"),
                        writes=[buf(f"xnfm{bs}")])
                    for mc in range(2):
                        pp, ppb = (p_qk, "p_qk") if mc % 2 == 0 else (p_av, "p_av")
                        mm_group(PE, [("matmul",
                                       (pp[:, :], rqw_b[:, kc, mc * 128:(mc + 1) * 128],
                                        xnfm[:, bs, kc, :]),
                                       dict(start=(kc == 0), stop=(kc == 7))) for kc in range(8)],
                                 reads=[buf("rqw_b"), buf(f"xnfm{bs}")], writes=[buf(ppb)])
                        do(AC, "activation", rq_fm[:, mc, :], pp[:, :], AF.Identity,
                           scale=0.125, bias=rqb_col[:, mc:mc + 1],
                           reads=[buf(ppb), buf("rqb_col")], writes=[buf("rq_fm")])
                        yield
                    pav_bf = p_av.ap().bitcast(BF16)
                    for j in range(4):
                        Gs = Ti * 4 + j
                        for half in range(2):
                            tk = mm_group(PE, [("matmul",
                                           (p_qk[:, lh * 64:(lh + 1) * 64],
                                            rq_fm[(lh % 2) * 64:(lh % 2) * 64 + 64, lh // 2,
                                                  j * 128:(j + 1) * 128],
                                            rk_fm[(lh % 2) * 64:(lh % 2) * 64 + 64, lh // 2, :]),
                                           dict(start=True, stop=True)) for lh in (half, half + 2)],
                                     reads=[buf("rq_fm"), buf("rk_fm")], writes=[buf("p_qk")])
                            if half == 0:
                                PE.wait(tk, force=True)
                        latview = p_qk[:, 0:256].rearrange("p (a b) -> p a b", b=64)
                        do(VE, "tensor_tensor", latview, latview,
                           latmadd[:, Gs, :].unsqueeze(1).broadcast_to([128, 4, 64]),
                           op=ALU.add, hard=True, reads=[buf("latmadd")], writes=[buf("p_qk")])
                        yield
                        do(AC, "activation", latprobs[:, :, :],
                           p_qk[:, 0:256].rearrange("p (a b) -> p a b", b=64),
                           AF.Exp, reads=[buf("p_qk")], writes=[buf("latprobs")])
                        do(VE, "reduce_sum", sums[:, 8:12], latprobs[:, :, :],
                           axis=AX.X, reads=[buf("latprobs")], writes=[buf("sumsL")])
                        do(VE, "tensor_scalar", hard=True, out=sums[:, 8:12],
                           in0=sums[:, 8:12], scalar1=1e-30, scalar2=None, op0=ALU.add,
                           reads=[buf("sumsL")], writes=[buf("sumsL")])
                        do(VE, "reciprocal", recip[:, 8:12], sums[:, 8:12],
                           reads=[buf("sumsL")], writes=[buf("recipL")])
                        yield
                        for lh in range(LH):
                            do(VE, "tensor_scalar", out=latprobs[:, lh, :], in0=latprobs[:, lh, :],
                               scalar1=recip[:, 8 + lh:9 + lh], scalar2=None, op0=ALU.mult,
                               reads=[buf("recipL")], writes=[buf("latprobs")])
                        mm_group(PE, [("transpose",
                                       (pav_bf[0:64, lh * 128:(lh + 1) * 128], latprobs[:, lh, :],
                                        ident_b[:, :]), {}) for lh in range(LH)],
                                 reads=[buf("latprobs"), buf("ident_b")], writes=[buf("p_av")])
                        do(VE, "tensor_copy", latTs[0:64, :, :],
                           pav_bf[0:64, 0:512].rearrange("p (a b) -> p a b", b=128),
                           reads=[buf("p_av")], writes=[buf("latTs")])
                        mms = []
                        for lh in range(LH):
                            po = (lh % 2) * 64
                            hp = lh // 2
                            oap = p_qk[po:po + 64, 256 + hp * 128:256 + (hp + 1) * 128]
                            mms.append(("matmul", (oap, rv_tm[0:64, lh, :], latTs[0:64, lh, :]),
                                        dict(start=True, stop=True)))
                        mm_group(PE, mms, reads=[buf("rv_tm"), buf("latTs")], writes=[buf("p_qk")])
                        do(AC, "activation", lat_fm[:, :, j * 128:(j + 1) * 128],
                           p_qk[:, 256:512].rearrange("p (a b) -> p a b", a=2),
                           AF.Copy, reads=[buf("p_qk")], writes=[buf("lat_fm")])
                        yield
                    yield
                    if Ti == 0:
                        do(VE, "tensor_scalar", out=lat_fm[:, :, 0:128], in0=lat_fm[:, :, 0:128],
                           scalar1=tok0col[:, 0:1], scalar2=None, op0=ALU.mult,
                           reads=[buf("tok0col")], writes=[buf("lat_fm")])
                    for mc in range(8):
                        pp, ppb = (p_qk, "p_qk") if mc % 2 == 0 else (p_av, "p_av")
                        mm_group(PE, [("matmul",
                                       (pp[:, :], routw_b[:, kc, mc * 128:(mc + 1) * 128],
                                        lat_fm[:, kc, :]),
                                       dict(start=(kc == 0), stop=(kc == 1))) for kc in range(2)],
                                 reads=[buf("routw_b"), buf("lat_fm")], writes=[buf(ppb)])
                        do(AC, "activation", remote_fm[:, mc, :], pp[:, :], AF.Identity,
                           bias=routb_col[:, mc:mc + 1],
                           reads=[buf(ppb), buf("routb_col")], writes=[buf("remote_fm")])
                        if mc % 2 == 1:
                            yield
                    mm_group(PE,
                             [("matmul", (p_av[0:16, :], glbd_b[:, kc, :],
                                          big1[:, kc, tl * 512:(tl + 1) * 512]),
                               dict(start=(kc == 0), stop=False)) for kc in range(8)] +
                             [("matmul", (p_av[0:16, :], grbd_b[:, kc, :], remote_fm[:, kc, :]),
                               dict(start=False, stop=(kc == 7))) for kc in range(8)],
                             reads=[buf("glbd_b"), buf("grbd_b"), buf(f"big1c{tl}"), buf("remote_fm")],
                             writes=[buf("p_av")])
                    do(AC, "activation", gexp[:, :], p_av[0:16, :], AF.Exp, scale=-1.0,
                       reads=[buf("p_av")], writes=[buf("gexp")])
                    do(VE, "tensor_scalar", out=gexp[:, :], in0=gexp[:, :], scalar1=1.0,
                       scalar2=None, op0=ALU.add, reads=[], writes=[buf("gexp")])
                    do(VE, "reciprocal", gate_sbT[:, :], gexp[:, :],
                       reads=[buf("gexp")], writes=[buf("gate_sbT")])
                    yield
                    for cch in range(8):
                        pp, ppb = (p_qk, "p_qk") if cch % 2 == 0 else (p_av, "p_av")
                        mm_group(PE, [("matmul", (pp[:, :], sel[0:16, cch, :], gate_sbT[:, :]),
                                       dict(start=True, stop=True))],
                                 reads=[buf("sel"), buf("gate_sbT")], writes=[buf(ppb)])
                        bcols = slice(tl * 512, (tl + 1) * 512)
                        do(VE, "tensor_tensor", big1[:, cch, bcols], big1[:, cch, bcols],
                           remote_fm[:, cch, :], op=ALU.subtract, writes=[buf(f"big1c{tl}")],
                           reads=[buf("remote_fm")])
                        do(VE, "tensor_tensor", big1[:, cch, bcols], big1[:, cch, bcols],
                           pp[:, :], op=ALU.mult, reads=[buf(ppb)], writes=[buf(f"big1c{tl}")])
                        do(VE, "tensor_tensor", big1[:, cch, bcols], big1[:, cch, bcols],
                           remote_fm[:, cch, :], op=ALU.add, reads=[buf("remote_fm")],
                           writes=[buf(f"big1c{tl}")])
                        if cch % 2 == 1:
                            yield
                    for j in range(4):
                        bs2 = j % 2
                        rows = slice(Ti * 512 + j * 128, Ti * 512 + (j + 1) * 128)
                        dma(SP, d_x[2 + bs2], s0sub[:, bs2, :], out[rows, :],
                            writes=[buf(f"s0sub{bs2}")])
                        mm_group(PE, [("matmul",
                                       (p_qk[:, :],
                                        big1[:, kc, tl * 512 + j * 128:tl * 512 + (j + 1) * 128],
                                        outw_b[:, kc, 0:512]),
                                       dict(start=(kc == 0), stop=(kc == 7)))
                                      for kc in range(8)],
                                 reads=[buf(f"big1c{tl}"), buf("outw_b")], writes=[buf("p_qk")])
                        mm_group(PE, [("matmul",
                                       (p_av[:, :],
                                        big1[:, kc, tl * 512 + j * 128:tl * 512 + (j + 1) * 128],
                                        outw_b[:, kc, 512:1024]),
                                       dict(start=(kc == 0), stop=(kc == 7)))
                                      for kc in range(8)],
                                 reads=[buf(f"big1c{tl}"), buf("outw_b")], writes=[buf("p_av")])
                        do(VE, "tensor_tensor", s0sub[:, bs2, 0:512], s0sub[:, bs2, 0:512],
                           p_qk[:, :], op=ALU.add, reads=[buf("p_qk")], writes=[buf(f"s0sub{bs2}")])
                        do(VE, "tensor_tensor", s0sub[:, bs2, 512:1024], s0sub[:, bs2, 512:1024],
                           p_av[:, :], op=ALU.add, reads=[buf("p_av")], writes=[buf(f"s0sub{bs2}")])
                        dmaq(SP, d_st[bs2], out[rows, :], s0sub[:, bs2, :],
                             reads=[buf(f"s0sub{bs2}")])
                        yield
                ntiles = 2 * _P1G if _P1T >= 4 else _P1T
                for Tix in range(ntiles):
                    ga = atile(Tix)
                    gb = btile(Tix - 1) if (Tix >= 1 and _P1B) else None
                    alive = True
                    while alive:
                        alive = False
                        try:
                            next(ga)
                            alive = True
                        except StopIteration:
                            pass
                        if gb is not None:
                            try:
                                next(gb)
                                alive = True
                            except StopIteration:
                                gb = None
                    while gb is not None:
                        try:
                            next(gb)
                        except StopIteration:
                            gb = None
                if _P1B and ntiles >= 1:
                    for _ in btile(ntiles - 1):
                        pass

        barrier()
        emit()
        es2.close()
        es_w.close()

        run_p2 = 'p2' in parts
        es3 = ExitStack()

        def sb3(name, shape, dt):
            return es3.enter_context(nc.sbuf_tensor(name, shape, dt))

        w1b = sb3("w1b", [128, 8, FF], BF16)
        w2b = sb3("w2b", [128, 32, D], BF16)
        yb = sb3("yb", [128, 32, 256], BF16)
        xnfm2 = sb3("xnfm2", [128, 2, 8, 256], BF16)
        s1sub = sb3("s1sub", [128, 2, D], F32)
        qstage = sb3("qstage", [128, 5, 1024], BF16)
        scrows_f = sb3("scrows_f", [64, 1024], F32)
        scb = sb3("scb", [64, 2, 1024], BF16)
        onehotb = sb3("onehotb", [64, 2, 128], BF16)
        cbtmp = sb3("cbtmp", [128, D], F32)
        cb_bc = sb3("cb_bc", [128, D], F32)
        r2bc = sb3("r2bc", [128, 2, 256], F32)
        p_f1 = es3.enter_context(nc.psum_tensor("p_f1", [128, 2, 512], F32))
        p_z = es3.enter_context(nc.psum_tensor("p_z", [128, 2, D], F32))
        p_sc = es3.enter_context(nc.psum_tensor("p_sc", [128, 1024], F32))

        if run_p2:
            QQ = [d_g[0], d_g[1], d_s2[0], d_s2[1], d_w[0], d_w[1]]
            dma(GP, d_ss, scrows_f[0:32, :],
                s1d.rearrange("r (w c) -> (r w) c", c=1024), writes=[buf("scrows")])
            dma(GP, d_ss, scrows_f[32:64, :], s2d, writes=[buf("scrows")])
            do(VE, "tensor_copy", scb[:, 0, :], scrows_f[:, :],
               reads=[buf("scrows")], writes=[buf("scb")])
            do(VE, "tensor_tensor", scb[:, 1, :], scrows_f[:, :], scb[:, 0, :],
               op=ALU.subtract, reads=[buf("scrows")], writes=[buf("scb")])

            def deq_piece(i2, kc, wv, qsrc, srow, dstw, bufname):
                sq = i2 % 5
                oh = i2 % 2
                dma(GP, QQ[sq], qstage[:, sq, :],
                    qsrc[kc * 128:(kc + 1) * 128, wv * 1024:(wv + 1) * 1024],
                    writes=[buf(f"qst{sq}")])
                do(GP, "tensor_copy", onehotb[:, oh, :],
                   ident_b[0:64, srow:srow + 1].broadcast_to([64, 128]),
                   reads=[buf("ident_b")], writes=[buf(f"oh{oh}")])
                mm_group(PE, [
                    ("matmul", (p_sc[:, hf * 512:(hf + 1) * 512], onehotb[0:64, oh, :],
                                scb[0:64, hl, hf * 512:(hf + 1) * 512]),
                     dict(start=(hl == 0), stop=(hl == 1)))
                    for hf in range(2) for hl in range(2)
                ], reads=[buf(f"oh{oh}"), buf("scb")], writes=[buf("p_sc")])
                do(VE, "scalar_tensor_tensor",
                   out=dstw[:, kc, wv * 1024:(wv + 1) * 1024],
                   in0=qstage[:, sq, :], scalar=-128.0, in1=p_sc[:, :],
                   op0=ALU.add, op1=ALU.mult,
                   reads=[buf(f"qst{sq}"), buf("p_sc")],
                   writes=[buf(bufname)])

            for i2, (kc, wv) in enumerate((kc, wv) for wv in range(4) for kc in range(8)):
                deq_piece(i2, kc, wv, qscr1, kc * 4 + wv, w1b, f"w1b{wv}")

            dma(GP, d_w[0], cbtmp[:, :],
                b2d.unsqueeze(0).partition_broadcast(128), writes=[buf("cbt")])
            dma(GP, d_w[1], cb_bc[:, :],
                bout.unsqueeze(0).partition_broadcast(128), writes=[buf("cb_bc")])
            do(VE, "scalar_tensor_tensor", out=cb_bc[:, :],
               in0=cbtmp[:, :], scalar=0.5,
               in1=cb_bc[:, :], op0=ALU.mult, op1=ALU.add,
               reads=[buf("cbt"), buf("cb_bc")],
               writes=[buf("cb_bc")])

            def deq2_gen():
                for i2, kc in enumerate(range(32)):
                    deq_piece(32 + i2, kc, 0, qscr2, 32 + kc, w2b, f"w2b{kc // 8}")
                    yield

            def p2_load(Ti):
                bs = Ti % 2
                cols = slice(Ti * 256, (Ti + 1) * 256)
                dma(SP, d_x[bs], xnfm2[:, bs, :, :],
                    xnscr[:, :, cols].rearrange("c p t -> p c t"), writes=[buf(f"xnfm2{bs}")])
                dma(GP, d_w[bs], r2bc[:, bs, :],
                    r2scr.rearrange("a b -> (a b)")[Ti * 256:(Ti + 1) * 256]
                    .unsqueeze(0).partition_broadcast(128)[:, 0, :],
                    writes=[buf(f"r2bc{bs}")])

            def p2_norm(Ti):
                bs = Ti % 2
                for cch in range(8):
                    do(GP, "tensor_tensor", xnfm2[:, bs, cch, :], xnfm2[:, bs, cch, :],
                       r2bc[:, bs, :],
                       op=ALU.mult, reads=[buf(f"r2bc{bs}")], writes=[buf(f"xnfm2{bs}")])
                    do(AC, "activation", xnfm2[:, bs, cch, :], xnfm2[:, bs, cch, :], AF.Identity,
                       scale=g_col[:, cch:cch + 1], bias=be_col[:, cch:cch + 1],
                       reads=[buf("g_col"), buf("be_col")], writes=[buf(f"xnfm2{bs}")])

            def p2_fc1(Ti, deq_iter=None):
                bs = Ti % 2
                for mc in range(32):
                    pf = mc % 2
                    mm_group(PE, [("matmul",
                                   (p_f1[:, pf, 0:256], w1b[:, kc, mc * 128:(mc + 1) * 128],
                                    xnfm2[:, bs, kc, :]),
                                   dict(start=(kc == 0), stop=(kc == 7))) for kc in range(8)],
                             reads=[buf(f"w1b{mc // 8}"), buf(f"xnfm2{bs}")], writes=[buf(f"p_f1{pf}")])
                    do(AC, "activation", yb[:, mc, :], p_f1[:, pf, 0:256], AF.Silu,
                       bias=f1b_col[:, mc:mc + 1],
                       reads=[buf(f"p_f1{pf}"), buf("f1b_col")], writes=[buf("yb")])
                    if deq_iter is not None:
                        next(deq_iter, None)

            def p2_fc2(Ti):
                for j in range(2):
                    rows = slice(Ti * 256 + j * 128, Ti * 256 + (j + 1) * 128)
                    dma(SP, d_x[2 + j], s1sub[:, j, :], out[rows, :], writes=[buf(f"s1s{j}")])
                    for gi in range(4):
                        mm_group(PE, [("matmul",
                                       (p_z[:, j, nn * 512:(nn + 1) * 512],
                                        yb[:, kc, j * 128:(j + 1) * 128],
                                        w2b[:, kc, nn * 512:(nn + 1) * 512]),
                                       dict(start=(kc == 0), stop=(kc == 31)))
                                      for nn in range(2) for kc in range(gi * 8, (gi + 1) * 8)],
                                 reads=[buf("yb"), buf(f"w2b{gi}")], writes=[buf(f"p_z{j}")])
                    do(VE, "scalar_tensor_tensor", out=p_z[:, j, :], in0=p_z[:, j, :],
                       scalar=0.5, in1=cb_bc[:, :], op0=ALU.mult, op1=ALU.add,
                       reads=[buf("cb_bc")], writes=[buf(f"p_z{j}")], hard=True)
                    do(VE, "tensor_tensor", s1sub[:, j, :], s1sub[:, j, :], p_z[:, j, :],
                       op=ALU.add, reads=[buf(f"p_z{j}")], writes=[buf(f"s1s{j}")])
                    dmaq(SP, d_st[j], out[rows, :], s1sub[:, j, :], reads=[buf(f"s1s{j}")])

            p2_load(0)
            p2_norm(0)
            dq2 = deq2_gen()
            for Ti in range(16):
                if Ti + 1 < 16:
                    p2_load(Ti + 1)
                p2_fc1(Ti, dq2 if Ti == 0 else None)
                if Ti == 0:
                    for _ in dq2:
                        pass
                if Ti + 1 < 16:
                    p2_norm(Ti + 1)
                p2_fc2(Ti)

        barrier()
        emit()
        es3.close()
    es0.close()
    return nc


_CACHED = {}


def _consts(half):
    a = np.arange(128)[:, None]
    b = np.arange(256)[None, :]
    if half == 0:
        madd_t0 = np.where((b >= 128) & (b - 128 < a), 0.0, NEG).astype(np.float32)
        vrow_t0 = np.ones((1, 128), np.float32)
        vrow_t0[0, 0] = 0.0
        tok0 = np.zeros((128, 1), np.float32)
    else:
        madd_t0 = np.where((b >= a) & (b <= a + 127), 0.0, NEG).astype(np.float32)
        vrow_t0 = np.ones((1, 128), np.float32)
        tok0 = np.ones((128, 1), np.float32)
    c0 = 32 * half
    latm = np.full((32, 64), NEG, np.float32)
    for s in range(32):
        c_abs = c0 + s
        for l_loc in range(64):
            l_glob = (c0 + l_loc) if l_loc < 32 else ((32 - c0) + (l_loc - 32))
            if l_glob < c_abs:
                latm[s, l_loc] = 0.0
    return madd_t0, vrow_t0, tok0, latm


def kernel(L0_attn_x, L0_attn_qkv_w, L0_attn_qkv_b, L0_attn_rq_w, L0_attn_rq_b,
           L0_attn_rkv_w, L0_attn_rkv_b, L0_attn_rout_w, L0_attn_rout_b,
           L0_attn_out_w, L0_attn_out_b, L0_attn_gl_w, L0_attn_gr_w,
           L0_ffn_ln_gamma, L0_ffn_ln_beta, L0_ffn_fc1_q, L0_ffn_fc1_scales,
           L0_ffn_fc1_bias, L0_ffn_fc2_q, L0_ffn_fc2_scales, L0_ffn_fc2_bias):
    x = np.ascontiguousarray(np.asarray(L0_attn_x, dtype=np.float32))
    c32 = lambda v: np.ascontiguousarray(np.asarray(v, dtype=np.float32))
    ci32 = lambda v: np.ascontiguousarray(np.asarray(v, dtype=np.int32))

    glbd = np.zeros((D, H), np.float32)
    grbd = np.zeros((D, H), np.float32)
    gl = np.asarray(L0_attn_gl_w, np.float32)[:, 0]
    gr = np.asarray(L0_attn_gr_w, np.float32)[:, 0]
    for h in range(H):
        glbd[h * HD:(h + 1) * HD, h] = gl
        grbd[h * HD:(h + 1) * HD, h] = gr

    common = dict(
        wqkv=c32(L0_attn_qkv_w), bqkv=c32(L0_attn_qkv_b),
        wrq=c32(L0_attn_rq_w), brq=c32(L0_attn_rq_b),
        wrkv=c32(L0_attn_rkv_w), brkv=c32(L0_attn_rkv_b),
        wrout=c32(L0_attn_rout_w), brout=c32(L0_attn_rout_b),
        wout=c32(L0_attn_out_w), bout=c32(L0_attn_out_b),
        glbd=glbd, grbd=grbd,
        gln=c32(L0_ffn_ln_gamma), bln=c32(L0_ffn_ln_beta),
        q1d=ci32(L0_ffn_fc1_q), s1d=c32(L0_ffn_fc1_scales), b1d=c32(L0_ffn_fc1_bias),
        q2d=ci32(L0_ffn_fc2_q), s2d=c32(L0_ffn_fc2_scales), b2d=c32(L0_ffn_fc2_bias),
    )
    halves = [_consts(0), _consts(1)]

    if "nc" not in _CACHED:
        _CACHED["nc"] = build_nc()
    nc = _CACHED["nc"]

    in_maps = []
    for core in range(8):
        bb = core // 2
        half = core % 2
        madd_t0, vrow_t0, tok0, latm = halves[half]
        m = dict(common)
        m["xo"] = np.ascontiguousarray(x[bb, half * T:(half + 1) * T])
        m["xp"] = np.ascontiguousarray(x[bb, (1 - half) * T:(2 - half) * T])
        m["madd_t0"] = madd_t0
        m["vrow_t0"] = vrow_t0
        m["tok0"] = tok0
        m["latm"] = latm
        in_maps.append(m)

    res = run_bass_kernel_spmd(nc, in_maps, core_ids=list(range(8)))
    outs = [np.asarray(r["out"]) for r in res.results]
    full = np.zeros((4, 2 * T, D), np.float32)
    for core in range(8):
        bb = core // 2
        half = core % 2
        full[bb, half * T:(half + 1) * T] = outs[core]
    return full


# version 44 (final)
# speedup vs baseline: 1.0308x; 1.0214x over previous
import os
import numpy as np
from contextlib import ExitStack
_KDBG = os.environ.get('KDBG', '')
_P1G = int(os.environ.get('P1G', '2'))
_P1B = os.environ.get('P1B', '1') == '1'
_P1T = int(os.environ.get('P1T', '4'))
_P1S = int(os.environ.get('P1S', '4'))
_P1W = int(os.environ.get('P1W', '2'))
_P1Q = int(os.environ.get('P1Q', '2'))

import concourse.bass as bass
import concourse.mybir as mybir
from concourse.bass_utils import run_bass_kernel_spmd

F32 = mybir.dt.float32
BF16 = mybir.dt.bfloat16
I32 = mybir.dt.int32
AF = mybir.ActivationFunctionType
ALU = mybir.AluOpType
AX = mybir.AxisListType

T = 4096
D = 1024
H = 16
HD = 64
LD = 256
LH = 4
FF = 4096
EPS = 1e-5
NEG = -1e9


class Sem:
    __slots__ = ("h", "n", "dma")

    def __init__(self, h, dma=False):
        self.h = h
        self.n = dma
        self.n = 0
        self.dma = dma


HARD_TOKS = set()


class St:
    def __init__(self, name, sem):
        self.name = name
        self.sem = sem
        self.ops = []
        self.wm = {}

    def wait(self, tok, force=False):
        if tok is None:
            return
        sem, val = tok
        if sem.dma:
            val = sem.n
        if sem is self.sem and not force and (id(sem), val) not in HARD_TOKS:
            return
        if self.wm.get(id(sem), 0) >= val:
            return
        self.wm[id(sem)] = val
        self.ops.append(("w", sem, val))

    def raw(self, meth, *a, **kw):
        self.ops.append(["o", meth, a, kw, []])

    def mark(self):
        self.sem.n += 1
        self.ops[-1][4].append((self.sem, 1))
        return (self.sem, self.sem.n)


class Buf:
    __slots__ = ("w", "r")

    def __init__(self):
        self.w = None
        self.r = []


_FENCED = ()


def _pre(st, reads, writes):
    f = st.name in _FENCED
    for b in reads:
        if b.w is not None:
            st.wait(b.w, force=f)
    for b in writes:
        if b.w is not None:
            st.wait(b.w, force=f)
        for t in b.r:
            st.wait(t, force=f)


def _post(tok, reads, writes):
    for b in reads:
        b.r.append(tok)
    for b in writes:
        b.w = tok
        b.r = []


HARD_METHS = ("reduce_sum", "reduce_max", "tensor_reduce", "reciprocal")


def do(st, meth, *a, reads=(), writes=(), hard=False, **kw):
    _pre(st, reads, writes)
    st.raw(meth, *a, **kw)
    tok = st.mark()
    if hard or meth in HARD_METHS or kw.get("accum_out") is not None:
        HARD_TOKS.add((id(tok[0]), tok[1]))
    _post(tok, reads, writes)
    return tok


def dma(st, dsem, out, in_, reads=(), writes=(), **kw):
    _pre(st, reads, writes)
    st.wait((dsem, dsem.n))
    st.raw("dma_start", out=out, in_=in_, **kw)
    dsem.n += 16
    st.ops[-1][4].append((dsem, 16))
    tok = (dsem, dsem.n)
    _post(tok, reads, writes)
    return tok


def dmaq(st, dsem, out, in_, reads=(), writes=(), **kw):
    _pre(st, reads, writes)
    st.raw("dma_start", out=out, in_=in_, **kw)
    dsem.n += 16
    st.ops[-1][4].append((dsem, 16))
    tok = (dsem, dsem.n)
    _post(tok, reads, writes)
    return tok


def mm_group(st, mms, reads=(), writes=()):
    _pre(st, reads, writes)
    for meth, a, kw in mms:
        st.raw(meth, *a, **kw)
    tok = st.mark()
    _post(tok, reads, writes)
    return tok


def build_nc(parts=('p1', 'p2')):
    nc = bass.Bass("TRN2", target_bir_lowering=False)

    xo = nc.dram_tensor("xo", [T, D], F32, kind="ExternalInput").ap()
    xp = nc.dram_tensor("xp", [T, D], F32, kind="ExternalInput").ap()
    wqkv = nc.dram_tensor("wqkv", [D, 3 * D], F32, kind="ExternalInput").ap()
    bqkv = nc.dram_tensor("bqkv", [3 * D], F32, kind="ExternalInput").ap()
    wrq = nc.dram_tensor("wrq", [D, LD], F32, kind="ExternalInput").ap()
    brq = nc.dram_tensor("brq", [LD], F32, kind="ExternalInput").ap()
    wrkv = nc.dram_tensor("wrkv", [D, 2 * LD], F32, kind="ExternalInput").ap()
    brkv = nc.dram_tensor("brkv", [2 * LD], F32, kind="ExternalInput").ap()
    wrout = nc.dram_tensor("wrout", [LD, D], F32, kind="ExternalInput").ap()
    brout = nc.dram_tensor("brout", [D], F32, kind="ExternalInput").ap()
    wout = nc.dram_tensor("wout", [D, D], F32, kind="ExternalInput").ap()
    bout = nc.dram_tensor("bout", [D], F32, kind="ExternalInput").ap()
    glbd = nc.dram_tensor("glbd", [D, H], F32, kind="ExternalInput").ap()
    grbd = nc.dram_tensor("grbd", [D, H], F32, kind="ExternalInput").ap()
    gln = nc.dram_tensor("gln", [D], F32, kind="ExternalInput").ap()
    bln = nc.dram_tensor("bln", [D], F32, kind="ExternalInput").ap()
    q1d = nc.dram_tensor("q1d", [D, FF], I32, kind="ExternalInput").ap()
    s1d = nc.dram_tensor("s1d", [8, FF], F32, kind="ExternalInput").ap()
    b1d = nc.dram_tensor("b1d", [FF], F32, kind="ExternalInput").ap()
    q2d = nc.dram_tensor("q2d", [FF, D], I32, kind="ExternalInput").ap()
    s2d = nc.dram_tensor("s2d", [32, D], F32, kind="ExternalInput").ap()
    b2d = nc.dram_tensor("b2d", [D], F32, kind="ExternalInput").ap()
    madd_t0_in = nc.dram_tensor("madd_t0", [128, 256], F32, kind="ExternalInput").ap()
    vrow_t0_in = nc.dram_tensor("vrow_t0", [1, 128], F32, kind="ExternalInput").ap()
    tok0_in = nc.dram_tensor("tok0", [128, 1], F32, kind="ExternalInput").ap()
    latm_in = nc.dram_tensor("latm", [32, 64], F32, kind="ExternalInput").ap()

    out = nc.dram_tensor("out", [T, D], F32, kind="ExternalOutput").ap()
    xnscr = nc.dram_tensor("xnscr", [8, 128, T], BF16, kind="Internal").ap()
    r2scr = nc.dram_tensor("r2scr", [32, 128], F32, kind="Internal").ap()
    qscr1 = nc.dram_tensor("qscr1", [D, FF], BF16, kind="Internal").ap()
    qscr2 = nc.dram_tensor("qscr2", [FF, D], BF16, kind="Internal").ap()

    es0 = ExitStack()

    def mksem(name):
        return Sem(es0.enter_context(nc.semaphore(name)))

    def mkdma(name):
        return Sem(es0.enter_context(nc.semaphore(name)), dma=True)

    d_x = [mkdma("d_x0"), mkdma("d_x1"), mkdma("d_x2"), mkdma("d_x3")]
    d_st = [mkdma("d_st0"), mkdma("d_st1")]
    d_w = [mkdma("d_w0"), mkdma("d_w1")]
    d_g = [mkdma("d_g0"), mkdma("d_g1")]
    d_s2 = [mkdma("d_s20"), mkdma("d_s21")]
    d_xb = [mkdma("d_xb0"), mkdma("d_xb1")]
    d_ss = mkdma("d_ss")
    DMAS = d_x + d_st + d_w + d_g + d_s2 + d_xb + [d_ss]
    ph = mksem("ph")
    SP = St("sync", mksem("s_sp"))
    GP = St("gpsimd", mksem("s_gp"))
    VE = St("vector", mksem("s_ve"))
    AC = St("scalar", mksem("s_ac"))
    PE = St("tensor", mksem("s_pe"))
    ALL = [SP, GP, VE, AC, PE]

    def barrier():
        for ds in DMAS:
            SP.wait((ds, ds.n))
            GP.wait((ds, ds.n))
        for st in ALL:
            st.raw("nop")
            ph.n += 1
            st.ops[-1][4].append((ph, 1))
        tgt = ph.n
        for st in ALL:
            st.wait((ph, tgt))

    B = {}

    def buf(name):
        if name not in B:
            B[name] = Buf()
        return B[name]

    def sb0(name, shape, dt):
        return es0.enter_context(nc.sbuf_tensor(name, shape, dt))

    ident_f = sb0("ident_f", [128, 128], F32)
    ident_b = sb0("ident_b", [128, 128], BF16)
    zcol = sb0("zcol", [128, 128], BF16)
    ones_row = sb0("ones_row", [1, 128], BF16)
    ones64 = sb0("ones64", [64, 128], BF16)
    madd = sb0("madd", [128, 256], F32)
    madd_t0 = sb0("madd_t0s", [128, 256], F32)
    madd_b = sb0("madd_b", [128, 256], BF16)
    madd_t0_b = sb0("madd_t0b", [128, 256], BF16)
    mask2 = sb0("mask2", [128, 512], BF16)
    mask2_t0 = sb0("mask2_t0", [128, 512], BF16)
    latmadd = sb0("latmadd", [128, 32, 64], BF16)
    tok0col = sb0("tok0col", [128, 1], F32)
    qb_col = sb0("qb_col", [128, 8], F32)
    kb_col = sb0("kb_col", [128, 8], F32)
    vb_colT = sb0("vb_colT", [128, 8], F32)
    rqb_col = sb0("rqb_col", [128, 2], F32)
    rkb_col = sb0("rkb_col", [128, 2], F32)
    rvb_row = sb0("rvb_row", [1, LD], BF16)
    routb_col = sb0("routb_col", [128, 8], F32)
    f1b_col = sb0("f1b_col", [128, 32], F32)
    g_col = sb0("g_col", [128, 8], F32)
    be_col = sb0("be_col", [128, 8], F32)
    glbd_b = sb0("glbd_b", [128, 8, H], BF16)
    grbd_b = sb0("grbd_b", [128, 8, H], BF16)
    sel = sb0("sel", [16, 8, 128], BF16)
    pooled_fm = sb0("pooled_fm", [128, 8, 64], BF16)
    rk_fm = sb0("rk_fm", [128, 2, 64], BF16)
    rv_tm = sb0("rv_tm", [64, 4, 64], BF16)
    xn_prev = sb0("xn_prev", [128, 8, 128], BF16)
    r2sb = sb0("r2sb", [128, 32], F32)
    smalls = sb0("smalls", [128, 32], F32)
    negmax = sb0("negmax", [128, 16], F32)
    sums = sb0("sums", [128, 16], F32)
    recip = sb0("recip", [128, 16], F32)

    NSM = 8

    def sc(i, par):
        j = i + NSM * par
        return smalls[:, j:j + 1]

    S_MU, S_VAR, S_RSTD, S_NB, S_A1, S_T0, S_T1 = range(7)

    with nc.Block() as block:

        def emit():
            for st in ALL:
                ops = st.ops
                st.ops = []
                if not ops:
                    continue

                def run(eng, ops=ops):
                    for o in ops:
                        if o[0] == "w":
                            eng.wait_ge(o[1].h, o[2])
                        else:
                            instr = getattr(eng, o[1])(*o[2], **o[3])
                            for (s, k) in o[4]:
                                instr.then_inc(s.h, k)

                with nc.allow_low_precision(reason="bf16 score psum per stated bf16-matmul bar"):
                    getattr(block, st.name)(run)

        es_w = ExitStack()

        def sbw(name, shape, dt):
            return es_w.enter_context(nc.sbuf_tensor(name, shape, dt))

        qkvw_b = sbw("qkvw_b", [128, 8, 3 * D], BF16)
        outw_b = sbw("outw_b", [128, 8, D], BF16)
        rqw_b = sbw("rqw_b", [128, 8, LD], BF16)
        routw_b = sbw("routw_b", [128, 2, D], BF16)
        rkvw_b = sbw("rkvw_b", [128, 8, 2 * LD], BF16)

        es1 = ExitStack()

        def sb1(name, shape, dt):
            return es1.enter_context(nc.sbuf_tensor(name, shape, dt))

        xt = sb1("xt", [128, 8, D], F32)
        xn_tmb = sb1("xn_tmb", [128, 4, D], BF16)
        s0t = sb1("s0t", [128, 4, D], F32)
        stage = sb1("stage", [128, 2, 8, 512], BF16)
        bnst = sb1("bnst", [128, 8, 12], F32)
        brows = sb1("brows", [32, 4, 128], F32)
        pooled_tm = sb1("pooled_tm", [64, D], BF16)
        p_pool = es1.enter_context(nc.psum_tensor("p_pool", [64, D], F32))
        p_tr = es1.enter_context(nc.psum_tensor("p_tr", [128, 8, 128], BF16))
        p_bias = es1.enter_context(nc.psum_tensor("p_bias", [128, 128], F32))

        dmaq(GP, d_w[0], latmadd[:, :, :], latm_in.partition_broadcast(128),
             writes=[buf("latmadd")])
        dmaq(GP, d_w[0], rvb_row[:, :], brkv[LD:2 * LD].unsqueeze(0),
             writes=[buf("rvb_row")])
        for c in range(8):
            dmaq(GP, d_w[0], glbd_b[:, c, :], glbd[c * 128:(c + 1) * 128, :],
                 writes=[buf("glbd_b")])
            dmaq(GP, d_w[0], grbd_b[:, c, :], grbd[c * 128:(c + 1) * 128, :],
                 writes=[buf("grbd_b")])
        dmaq(GP, d_w[1], madd_t0[:, :], madd_t0_in, writes=[buf("madd_t0s")])
        dmaq(GP, d_w[1], tok0col[:, :], tok0_in, writes=[buf("tok0col")])
        dmaq(GP, d_w[1], brows[0:24, 0, :],
             bqkv.rearrange("(r p) -> r p", p=128)[0:24, :], writes=[buf("brows")])
        dmaq(GP, d_w[1], brows[0:2, 1, :], brq.rearrange("(r p) -> r p", p=128),
             writes=[buf("brows")])
        dmaq(GP, d_w[1], brows[2:4, 1, :],
             brkv.rearrange("(r p) -> r p", p=128)[0:2, :], writes=[buf("brows")])
        dmaq(GP, d_w[1], brows[4:12, 1, :], brout.rearrange("(r p) -> r p", p=128),
             writes=[buf("brows")])
        dmaq(GP, d_w[1], brows[12:20, 1, :], gln.rearrange("(r p) -> r p", p=128),
             writes=[buf("brows")])
        dmaq(GP, d_w[1], brows[20:28, 1, :], bln.rearrange("(r p) -> r p", p=128),
             writes=[buf("brows")])
        dmaq(GP, d_w[1], brows[0:32, 2, :], b1d.rearrange("(r p) -> r p", p=128),
             writes=[buf("brows")])

        def emit_weight_loads():
            GP.wait((d_w[0], d_w[0].n))
            GP.wait((d_w[1], d_w[1].n))
            for c in range(8):
                dmaq(GP, d_w[c % 2], qkvw_b[:, c, :], wqkv[c * 128:(c + 1) * 128, :],
                     writes=[buf("qkvw_b")])
            for c in range(8):
                dmaq(GP, d_w[c % 2], rkvw_b[:, c, :], wrkv[c * 128:(c + 1) * 128, :],
                     writes=[buf("rkvw_b")])
            for c in range(8):
                dmaq(GP, d_w[c % 2], outw_b[:, c, :], wout[c * 128:(c + 1) * 128, :],
                     writes=[buf("outw_b")])
                dmaq(GP, d_w[c % 2], rqw_b[:, c, :], wrq[c * 128:(c + 1) * 128, :],
                     writes=[buf("rqw_b")])
            for c in range(2):
                dmaq(GP, d_w[c % 2], routw_b[:, c, :], wrout[c * 128:(c + 1) * 128, :],
                     writes=[buf("routw_b")])

        for t_ in (ident_f, ident_b):
            do(GP, "memset", t_[:, :], 0.0, hard=True, writes=[buf(t_.name)])
            do(GP, "affine_select", out=t_[:, :], in_=t_[:, :],
               compare_op=ALU.not_equal, fill=1.0, base=0,
               pattern=[[-1, 128]], channel_multiplier=1, hard=True,
               reads=[buf(t_.name)], writes=[buf(t_.name)])
        do(GP, "memset", zcol[:, :], 0.0, hard=True, writes=[buf("zcol")])
        do(GP, "affine_select", out=zcol[:, :], in_=zcol[:, :],
           compare_op=ALU.not_equal, fill=1.0 / 128.0, base=64,
           pattern=[[-1, 128]], channel_multiplier=0, hard=True,
           reads=[buf("zcol")], writes=[buf("zcol")])
        do(GP, "memset", ones_row[:, :], 1.0, writes=[buf("ones_row")])
        do(GP, "memset", ones64[:, :], 1.0, writes=[buf("ones64")])
        do(GP, "memset", madd[:, :], 0.0, hard=True, writes=[buf("madd")])
        do(GP, "affine_select", out=madd[:, :], in_=madd[:, :],
           compare_op=ALU.is_ge, fill=NEG, base=0,
           pattern=[[1, 256]], channel_multiplier=-1, hard=True,
           reads=[buf("madd")], writes=[buf("madd")])
        do(GP, "affine_select", out=madd[:, :], in_=madd[:, :],
           compare_op=ALU.is_ge, fill=NEG, base=127,
           pattern=[[-1, 256]], channel_multiplier=1, hard=True,
           reads=[buf("madd")], writes=[buf("madd")])
        do(GP, "memset", sel[:, :, :], 0.0, hard=True, writes=[buf("sel")])
        for c in range(8):
            do(GP, "affine_select", out=sel[0:16, c, 0:64], in_=sel[0:16, c, 0:64],
               compare_op=ALU.not_equal, fill=1.0, base=-(2 * c),
               pattern=[[0, 64]], channel_multiplier=1, hard=True,
               reads=[buf("sel")], writes=[buf("sel")])
            do(GP, "affine_select", out=sel[0:16, c, 64:128], in_=sel[0:16, c, 64:128],
               compare_op=ALU.not_equal, fill=1.0, base=-(2 * c + 1),
               pattern=[[0, 64]], channel_multiplier=1, hard=True,
               reads=[buf("sel")], writes=[buf("sel")])

        do(GP, "tensor_copy", madd_b[:, :], madd[:, :],
           reads=[buf("madd")], writes=[buf("madd_b")])
        do(GP, "tensor_copy", madd_t0_b[:, :], madd_t0[:, :],
           reads=[buf("madd_t0s")], writes=[buf("madd_t0b")])
        for du in range(2):
            do(GP, "tensor_copy", mask2[:, du * 256:(du + 1) * 256], madd[:, :],
               reads=[buf("madd")], writes=[buf("mask2")])
            do(GP, "tensor_copy", mask2_t0[:, du * 256:(du + 1) * 256], madd_t0[:, :],
               reads=[buf("madd_t0s")], writes=[buf("mask2_t0")])

        mm_group(PE, [("transpose", (p_bias[:, 0:24], brows[0:24, 0, :], ident_f[0:24, 0:24]), {})],
                 reads=[buf("brows"), buf("ident_f")], writes=[buf("p_bias")])
        mm_group(PE, [("transpose", (p_bias[:, 24:52], brows[0:28, 1, :], ident_f[0:28, 0:28]), {})],
                 reads=[buf("brows"), buf("ident_f")], writes=[buf("p_bias")])
        mm_group(PE, [("transpose", (p_bias[:, 52:84], brows[0:32, 2, :], ident_f[0:32, 0:32]), {})],
                 reads=[buf("brows"), buf("ident_f")], writes=[buf("p_bias")])
        for dst, s0_, s1_, scale in ((qb_col, 0, 8, 0.125), (kb_col, 8, 16, 1.0),
                                     (vb_colT, 16, 24, 1.0),
                                     (rqb_col, 24, 26, 0.125), (rkb_col, 26, 28, 1.0),
                                     (routb_col, 28, 36, 1.0), (g_col, 36, 44, 1.0),
                                     (be_col, 44, 52, 1.0), (f1b_col, 52, 84, 1.0)):
            do(AC, "activation", dst[:, :], p_bias[:, s0_:s1_], AF.Copy, scale=scale,
               reads=[buf("p_bias")], writes=[buf(dst.name)])

        XQ = d_x + d_xb

        def st_load(idx):
            if idx % 2 == 1:
                return
            own = idx < 32
            i = idx if own else idx - 32
            src = xo if own else xp
            s = idx % 8
            dmaq(SP, XQ[(idx // 2) % 6], xt[:, s:s + 2, :],
                 src[i * 128:(i + 2) * 128, :].rearrange("(a p) c -> p a c", p=128),
                 writes=[buf(f"xt{s}"), buf(f"xt{s + 1}")])

        def st_stats(idx):
            s = idx % 8
            par = idx % 4
            bx = buf(f"xt{s}")
            do(VE, "bn_stats", bnst[:, s, 0:6], xt[:, s, 0:512], hard=True,
               reads=[bx], writes=[buf(f"bnst{s}")])
            do(VE, "bn_stats", bnst[:, s, 6:12], xt[:, s, 512:1024], hard=True,
               reads=[bx], writes=[buf(f"bnst{s}")])
            do(VE, "bn_aggr", smalls[:, NSM * par + S_MU:NSM * par + S_VAR + 1],
               bnst[:, s, 0:12], hard=True,
               reads=[buf(f"bnst{s}")],
               writes=[buf(f"sm_mu{par}"), buf(f"sm_var{par}")])
            do(VE, "tensor_scalar", hard=True, out=sc(S_T1, par), in0=sc(S_VAR, par),
               scalar1=EPS, scalar2=None, op0=ALU.add,
               reads=[buf(f"sm_var{par}")], writes=[buf(f"sm_t1{par}")])

        def st_rstd(idx):
            par = idx % 4
            do(AC, "activation", sc(S_T0, par), sc(S_T1, par), AF.Ln, hard=True,
               reads=[buf(f"sm_t1{par}")], writes=[buf(f"sm_t0{par}")])
            do(AC, "activation", sc(S_RSTD, par), sc(S_T0, par), AF.Exp, scale=-0.5,
               hard=True, reads=[buf(f"sm_t0{par}")], writes=[buf(f"sm_rstd{par}")])
            do(VE, "scalar_tensor_tensor", hard=True, out=sc(S_NB, par), in0=sc(S_MU, par),
               scalar=-1.0, in1=sc(S_RSTD, par), op0=ALU.mult, op1=ALU.mult,
               reads=[buf(f"sm_mu{par}"), buf(f"sm_rstd{par}")], writes=[buf(f"sm_nb{par}")])
            if idx < 32:
                do(VE, "tensor_scalar", hard=True, out=sc(S_A1, par), in0=sc(S_RSTD, par),
                   scalar1=1.0, scalar2=None, op0=ALU.add,
                   reads=[buf(f"sm_rstd{par}")], writes=[buf(f"sm_a1{par}")])

        def st_main(idx):
            own = idx < 32
            i = idx if own else idx - 32
            s = idx % 4
            sx = idx % 8
            par = idx % 4
            bx = buf(f"xt{sx}")
            do(AC, "activation", xn_tmb[:, s, :], xt[:, sx, :], AF.Identity,
               scale=sc(S_RSTD, par), bias=sc(S_NB, par),
               reads=[bx, buf(f"sm_rstd{par}"), buf(f"sm_nb{par}")],
               writes=[buf(f"xn{s}")])
            c = i if own else 32 + i
            mm_group(PE, [
                ("matmul", (p_pool[:, 0:512], zcol[:, 64 - c:128 - c], xn_tmb[:, s, 0:512]),
                 dict(start=(idx == 0), stop=(idx == 63))),
                ("matmul", (p_pool[:, 512:1024], zcol[:, 64 - c:128 - c], xn_tmb[:, s, 512:1024]),
                 dict(start=(idx == 0), stop=(idx == 63))),
            ], reads=[buf(f"xn{s}"), buf("zcol")], writes=[buf("p_pool")])
            if own:
                do(VE, "tensor_scalar", out=s0t[:, s, :], in0=xt[:, sx, :],
                   scalar1=sc(S_A1, par), scalar2=sc(S_NB, par),
                   op0=ALU.mult, op1=ALU.add,
                   reads=[bx, buf(f"sm_a1{par}"), buf(f"sm_nb{par}")],
                   writes=[buf(f"s0t{s}")])
                if idx % 2 == 1:
                    dmaq(SP, d_st[(idx // 2) % 2],
                         out[(i - 1) * 128:(i + 1) * 128, :]
                         .rearrange("(a p) c -> p a c", p=128),
                         s0t[:, s - 1:s + 1, :],
                         reads=[buf(f"s0t{s - 1}"), buf(f"s0t{s}")])
                do(VE, "reciprocal", sc(S_T1, par), sc(S_T1, par),
                   reads=[buf(f"sm_t1{par}")], writes=[buf(f"sm_t1{par}")])
                do(VE, "tensor_tensor", sc(S_T0, par), sc(S_VAR, par), sc(S_T1, par), hard=True,
                   op=ALU.mult, reads=[buf(f"sm_var{par}"), buf(f"sm_t1{par}")],
                   writes=[buf(f"sm_t0{par}")])
                do(VE, "tensor_scalar", hard=True, out=sc(S_T0, par), in0=sc(S_T0, par),
                   scalar1=EPS, scalar2=None, op0=ALU.add,
                   reads=[buf(f"sm_t0{par}")], writes=[buf(f"sm_t0{par}")])
                mm_group(PE, [("transpose",
                               (p_tr[:, cc, :], xn_tmb[:, s, cc * 128:(cc + 1) * 128],
                                ident_b[:, :]), {}) for cc in range(8)],
                         reads=[buf(f"xn{s}"), buf("ident_b")], writes=[buf("p_tr")])
                ss = (i // 4) % 2
                do(VE, "tensor_copy", stage[:, ss, :, (i % 4) * 128:(i % 4 + 1) * 128],
                   p_tr[:, :, :], reads=[buf("p_tr")], writes=[buf(f"stage{ss}")])
            if (not own) and i == 31:
                mm_group(PE, [("transpose",
                               (p_tr[:, cc, :], xn_tmb[:, s, cc * 128:(cc + 1) * 128],
                                ident_b[:, :]), {}) for cc in range(8)],
                         reads=[buf(f"xn{s}"), buf("ident_b")], writes=[buf("p_tr")])
                do(VE, "tensor_copy", xn_prev[:, :, :], p_tr[:, :, :],
                   reads=[buf("p_tr")], writes=[buf("xn_prev")])

        def st_tail(idx):
            own = idx < 32
            if not own:
                return
            i = idx
            par = idx % 4
            do(AC, "activation", sc(S_T1, par), sc(S_T0, par), AF.Ln, hard=True,
               reads=[buf(f"sm_t0{par}")], writes=[buf(f"sm_t1{par}")])
            do(AC, "activation", r2sb[:, i:i + 1], sc(S_T1, par), AF.Exp, scale=-0.5,
               hard=True, reads=[buf(f"sm_t1{par}")], writes=[buf("r2sb")])
            ss = (i // 4) % 2
            if i % 4 == 3:
                g0 = i // 4
                dma(GP, d_ss,
                    xnscr[:, :, g0 * 512:(g0 + 1) * 512].rearrange("c p t -> p c t"),
                    stage[:, ss, :, :], reads=[buf(f"stage{ss}")])

        STAGES = (st_load, st_stats, st_rstd, st_main, st_tail)
        for w in range(64 + len(STAGES) - 1):
            if w == 12:
                emit_weight_loads()
            for k, fn in enumerate(STAGES):
                t_ = w - k
                if 0 <= t_ < 64:
                    fn(t_)

        mm_group(PE, [("transpose", (p_bias[0:32, 0:128], r2sb[:, :], ident_f[:, :]), {})],
                 reads=[buf("r2sb"), buf("ident_f")], writes=[buf("p_bias")])
        do(VE, "tensor_copy", brows[0:32, 3, :], p_bias[0:32, 0:128],
           reads=[buf("p_bias")], writes=[buf("brows")])
        dma(GP, d_ss, r2scr[:, :], brows[0:32, 3, :], reads=[buf("brows")])

        do(AC, "activation", pooled_tm[:, :], p_pool[:, :], AF.Copy,
           reads=[buf("p_pool")], writes=[buf("pooled_tm")])
        mm_group(PE, [("transpose",
                       (p_tr[:, cc, 0:64], pooled_tm[0:64, cc * 128:(cc + 1) * 128],
                        ident_b[0:64, 0:64]), {}) for cc in range(8)],
                 reads=[buf("pooled_tm"), buf("ident_b")], writes=[buf("p_tr")])
        do(VE, "tensor_copy", pooled_fm[:, :, :], p_tr[:, :, 0:64],
           reads=[buf("p_tr")], writes=[buf("pooled_fm")])

        barrier()
        emit()
        es1.close()

        run_p1 = 'p1' in parts
        es2 = ExitStack()

        def sb2(name, shape, dt):
            return es2.enter_context(nc.sbuf_tensor(name, shape, dt))

        QPQ = [d_g[0], d_g[1], d_s2[0], d_s2[1]]
        for c in range(8):
            dma(GP, QPQ[c % 4], qscr1[c * 128:(c + 1) * 128, :],
                q1d[c * 128:(c + 1) * 128, :])
        for c in range(32):
            dma(GP, QPQ[c % 4], qscr2[c * 128:(c + 1) * 128, :],
                q2d[c * 128:(c + 1) * 128, :])

        big1 = sb2("big1", [128, 8, 2048], BF16)
        xnfm = sb2("xnfm", [128, 2, 8, 512], BF16)
        q_fm = sb2("q_fm", [128, 8, 512], BF16)
        k_fm = sb2("k_fm", [128, 8, 512], BF16)
        k_roll = sb2("k_roll", [128, 8, 128], BF16)
        vring = sb2("vring", [128, 4, D], BF16)
        probs = sb2("probs", [128, 8, 256], BF16)
        probsT = sb2("probsT", [128, 8, 256], BF16)
        s0sub = sb2("s0sub", [128, 2, D], F32)
        remote_fm = sb2("remote_fm", [128, 8, 512], BF16)
        rq_fm = sb2("rq_fm", [128, 2, 512], BF16)
        latprobs = sb2("latprobs", [128, 4, 64], BF16)
        latTs = sb2("latTs", [64, 4, 128], BF16)
        lat_fm = sb2("lat_fm", [128, 2, 512], BF16)
        gate_sbT = sb2("gate_sbT", [16, 512], BF16)
        gexp = sb2("gexp", [16, 512], F32)

        p_qk = es2.enter_context(nc.psum_tensor("p_qk", [128, 512], F32))
        p_v = es2.enter_context(nc.psum_tensor("p_v", [128, D], F32))
        p_scA = es2.enter_context(nc.psum_tensor("p_scA", [128, 4, 256], F32))
        p_scB = es2.enter_context(nc.psum_tensor("p_scB", [128, 4, 256], F32))
        p_av = es2.enter_context(nc.psum_tensor("p_av", [128, 512], F32))

        if run_p1:
            for mc in range(2 if run_p1 else 0):
                mm_group(PE, [("matmul",
                               (p_qk[:, mc * 64:(mc + 1) * 64], rkvw_b[:, kc, mc * 128:(mc + 1) * 128],
                                pooled_fm[:, kc, :]),
                               dict(start=(kc == 0), stop=(kc == 7))) for kc in range(8)],
                         reads=[buf("rkvw_b"), buf("pooled_fm")], writes=[buf("p_qk")])
            if _P1W >= 1:
                do(AC, "activation", rk_fm[:, 0, :], p_qk[:, 0:64], AF.Identity, bias=rkb_col[:, 0:1],
                   reads=[buf("p_qk"), buf("rkb_col")], writes=[buf("rk_fm")])
                do(AC, "activation", rk_fm[:, 1, :], p_qk[:, 64:128], AF.Identity, bias=rkb_col[:, 1:2],
                   reads=[buf("p_qk"), buf("rkb_col")], writes=[buf("rk_fm")])
                mm_group(PE, [("matmul",
                               (p_v[0:64, 0:256], pooled_fm[:, kc, :], rkvw_b[:, kc, 256:512]),
                               dict(start=(kc == 0), stop=False)) for kc in range(8)] +
                         [("matmul", (p_v[0:64, 0:256], ones_row[0:1, 0:64], rvb_row[0:1, :]),
                           dict(start=False, stop=True))],
                         reads=[buf("rkvw_b"), buf("pooled_fm"), buf("ones_row"), buf("rvb_row")],
                         writes=[buf("p_vh0")])
                do(VE, "tensor_copy", rv_tm[0:64, :, :], p_v[0:64, 0:256],
                   reads=[buf("p_vh0")], writes=[buf("rv_tm")])

            if True:
                def atile(Ti):
                    tl = Ti % 4
                    bs = Ti % 2
                    dma(SP, d_x[bs], xnfm[:, bs, :, :],
                        xnscr[:, :, Ti * 512:(Ti + 1) * 512].rearrange("c p t -> p c t"),
                        writes=[buf(f"xnfm{bs}")])
                    if Ti == 0:
                        for mc in range(8):
                            ph_ = mc % 2
                            pv_h = p_v[:, ph_ * 512:ph_ * 512 + 128]
                            mm_group(PE, [("matmul",
                                           (pv_h, qkvw_b[:, kc, D + mc * 128:D + (mc + 1) * 128],
                                            xn_prev[:, kc, :]),
                                           dict(start=(kc == 0), stop=(kc == 7))) for kc in range(8)],
                                     reads=[buf("qkvw_b"), buf("xn_prev")], writes=[buf(f"p_vh{ph_}")])
                            do(AC, "activation", k_roll[:, mc, :], pv_h, AF.Identity,
                               bias=kb_col[:, mc:mc + 1],
                               reads=[buf(f"p_vh{ph_}"), buf("kb_col")], writes=[buf("k_roll")])
                        mm_group(PE, [("matmul",
                                       (p_v[:, nn * 512:(nn + 1) * 512], xn_prev[:, kc, :],
                                        qkvw_b[:, kc, 2 * D + nn * 512:2 * D + (nn + 1) * 512]),
                                       dict(start=(kc == 0), stop=(kc == 7)))
                                      for nn in range(2) for kc in range(8)],
                                 reads=[buf("qkvw_b"), buf("xn_prev")],
                                 writes=[buf("p_vh0"), buf("p_vh1")])
                        do(AC, "activation", vring[:, 3, :], p_v[:, :], AF.Copy,
                           reads=[buf("p_vh0"), buf("p_vh1")], writes=[buf("vring3")])
                    yield
                    for w0, dst, bcol, scl, nbuf in ((0, q_fm, qb_col, 0.125, "q_fm"),
                                                     (D, k_fm, kb_col, 1.0, "k_fm")):
                        for mc in range(8):
                            ph_ = mc % 2
                            pv_h = p_v[:, ph_ * 512:(ph_ + 1) * 512]
                            mm_group(PE, [("matmul",
                                           (pv_h, qkvw_b[:, kc, w0 + mc * 128:w0 + (mc + 1) * 128],
                                            xnfm[:, bs, kc, :]),
                                           dict(start=(kc == 0), stop=(kc == 7))) for kc in range(8)],
                                     reads=[buf("qkvw_b"), buf(f"xnfm{bs}")], writes=[buf(f"p_vh{ph_}")])
                            do(AC, "activation", dst[:, mc, :], pv_h, AF.Identity,
                               scale=scl, bias=bcol[:, mc:mc + 1],
                               reads=[buf(f"p_vh{ph_}"), buf(bcol.name)], writes=[buf(nbuf)])
                            if mc % 2 == 1:
                                yield
                    def vproj(j, bs=bs, Ti=Ti):
                        slot = (Ti * 4 + j) % 4
                        mm_group(PE, [("matmul",
                                       (p_v[:, nn * 512:(nn + 1) * 512],
                                        xnfm[:, bs, kc, j * 128:(j + 1) * 128],
                                        qkvw_b[:, kc, 2 * D + nn * 512:2 * D + (nn + 1) * 512]),
                                       dict(start=(kc == 0), stop=(kc == 7)))
                                      for nn in range(2) for kc in range(8)],
                                 reads=[buf("qkvw_b"), buf(f"xnfm{bs}")],
                                 writes=[buf("p_vh0"), buf("p_vh1")])
                        do(AC, "activation", vring[:, slot, :], p_v[:, :], AF.Copy,
                           reads=[buf("p_vh0"), buf("p_vh1")], writes=[buf(f"vring{slot}")])
                    for j in range(3):
                        vproj(j)
                        yield
                    for j in range(4):
                        if j == 1:
                            vproj(3)
                            yield
                        Gs = Ti * 4 + j
                        slot = Gs % 4
                        pslot = (Gs - 1) % 4
                        mmask = mask2_t0 if Gs == 0 else mask2
                        mmaskb = buf("mask2_t0") if Gs == 0 else buf("mask2")
                        RMAP = (0, 2, 1, 3)
                        RINV = (0, 2, 1, 3)

                        def sc_stage(g4, j=j, Gs=Gs, mmask=mmask, mmaskb=mmaskb):
                            par = g4 % 2
                            psc = p_scA if par == 0 else p_scB
                            pbuf = f"psc{par}"
                            h0 = g4 * 4
                            mms = []
                            for r, hh in enumerate(RMAP):
                                h = h0 + hh
                                po = (h % 2) * 64
                                hc = h // 2
                                qap = q_fm[po:po + 64, hc, j * 128:(j + 1) * 128]
                                first = (r % 2) == 0
                                if j == 0:
                                    mms.append(("matmul",
                                                (psc[:, r, 0:128], qap, k_roll[po:po + 64, hc, :]),
                                                dict(start=first, stop=False,
                                                     skip_group_check=not first)))
                                    mms.append(("matmul",
                                                (psc[:, r, 128:256], qap, k_fm[po:po + 64, hc, 0:128]),
                                                dict(start=False, stop=False, skip_group_check=True)))
                                else:
                                    mms.append(("matmul",
                                                (psc[:, r, :], qap,
                                                 k_fm[po:po + 64, hc, (j - 1) * 128:(j + 1) * 128]),
                                                dict(start=first, stop=False,
                                                     skip_group_check=not first)))
                                if r % 2 == 1:
                                    mms.append(("matmul",
                                                (psc[:, r - 1:r + 1, :]
                                                 .rearrange("p a b -> p (a b)"),
                                                 ident_b[:, :], mmask[:, :]),
                                                dict(start=False, stop=True)))
                            mm_group(PE, mms,
                                     reads=[buf("q_fm"), buf("k_fm"), buf("k_roll"),
                                            mmaskb, buf("ident_b")],
                                     writes=[buf(pbuf)])

                        def th_stage(g4, j=j):
                            par = g4 % 2
                            psc = p_scA if par == 0 else p_scB
                            pbuf = f"psc{par}"
                            h0 = g4 * 4
                            gsl = slice(h0, h0 + 4)
                            pb = par * 4
                            do(AC, "activation", probs[:, pb:pb + 2, :], psc[:, 0:2, :],
                               AF.Exp, reads=[buf(pbuf)],
                               writes=[buf(f"pr{par}{r}") for r in range(2)])
                            do(VE, "reduce_sum", sums[:, h0:h0 + 2],
                               probs[:, pb:pb + 2, :], axis=AX.X,
                               reads=[buf(f"pr{par}{r}") for r in range(2)],
                               writes=[buf(f"sums{par}a")])
                            do(AC, "activation", probs[:, pb + 2:pb + 4, :], psc[:, 2:4, :],
                               AF.Exp, reads=[buf(pbuf)],
                               writes=[buf(f"pr{par}{r}") for r in range(2, 4)])
                            do(VE, "reduce_sum", sums[:, h0 + 2:h0 + 4],
                               probs[:, pb + 2:pb + 4, :], axis=AX.X,
                               reads=[buf(f"pr{par}{r}") for r in range(2, 4)],
                               writes=[buf(f"sums{par}b")])
                            do(VE, "tensor_scalar", hard=True, out=sums[:, gsl],
                               in0=sums[:, gsl], scalar1=1e-30, scalar2=None, op0=ALU.add,
                               reads=[buf(f"sums{par}a"), buf(f"sums{par}b")],
                               writes=[buf(f"sums{par}")])
                            do(VE, "reciprocal", recip[:, gsl], sums[:, gsl],
                               reads=[buf(f"sums{par}")], writes=[buf(f"recip{par}")])
                            for r in range(4):
                                do(VE, "tensor_scalar",
                                   out=probs[:, pb + r, :],
                                   in0=probs[:, pb + r, :],
                                   scalar1=recip[:, h0 + r:h0 + r + 1], scalar2=None,
                                   op0=ALU.mult,
                                   reads=[buf(f"recip{par}")], writes=[buf(f"pr{par}{r}")])

                        def tf_stage(g4, j=j, tl=tl, slot=slot, pslot=pslot):
                            par = g4 % 2
                            psc_o = p_scB if par == 0 else p_scA
                            obuf = f"psc{1 - par}"
                            psc_bf = psc_o.ap().bitcast(BF16)
                            h0 = g4 * 4
                            pb = par * 4
                            mm_group(PE, [("transpose",
                                           (psc_bf[:, r, kk * 128:(kk + 1) * 128],
                                            probs[:, pb + r, kk * 128:(kk + 1) * 128],
                                            ident_b[:, :]), {})
                                          for r in range(4) for kk in range(2)],
                                     reads=[buf(f"pr{par}{r}") for r in range(4)] +
                                           [buf("ident_b")],
                                     writes=[buf(obuf)])
                            if par == 0:
                                do(VE, "tensor_copy", probsT[:, 0:4, :],
                                   psc_bf[:, :, 0:256],
                                   reads=[buf(obuf)], writes=[buf(f"probsT{par}")])
                            else:
                                do(AC, "activation", probsT[:, 4:8, :],
                                   psc_bf[:, :, 0:256], AF.Copy,
                                   reads=[buf(obuf)], writes=[buf(f"probsT{par}")])
                            mms = []
                            for hh in range(4):
                                h = h0 + hh
                                po = (h % 2) * 64
                                oap = psc_o[po:po + 64, hh // 2, 128:256]
                                pt = probsT[:, par * 4 + RINV[hh], :]
                                mms.append(("matmul",
                                            (oap, vring[:, pslot, h * 64:(h + 1) * 64],
                                             pt[:, 0:128]), dict(start=True, stop=False)))
                                mms.append(("matmul",
                                            (oap, vring[:, slot, h * 64:(h + 1) * 64],
                                             pt[:, 128:256]), dict(start=False, stop=True)))
                            mm_group(PE, mms,
                                     reads=[buf(f"probsT{par}"), buf(f"vring{slot}"),
                                            buf(f"vring{pslot}")],
                                     writes=[buf(obuf)])
                            for gg in range(2):
                                do(AC, "activation",
                                   big1[:, 2 * g4 + gg,
                                        tl * 512 + j * 128:tl * 512 + (j + 1) * 128],
                                   psc_o[:, gg, 128:256], AF.Identity,
                                   bias=vb_colT[:, 2 * g4 + gg:2 * g4 + gg + 1],
                                   reads=[buf(obuf), buf("vb_colT")],
                                   writes=[buf(f"big1c{tl}")])

                        sc_stage(0)
                        yield
                        sc_stage(1)
                        yield
                        th_stage(0)
                        yield
                        th_stage(1)
                        yield
                        tf_stage(0)
                        yield
                        sc_stage(2)
                        yield
                        th_stage(2)
                        yield
                        tf_stage(1)
                        yield
                        sc_stage(3)
                        yield
                        th_stage(3)
                        yield
                        tf_stage(2)
                        yield
                        tf_stage(3)
                        yield
                    do(VE, "tensor_copy", k_roll[:, :, :], k_fm[:, :, 384:512],
                       reads=[buf("k_fm")], writes=[buf("k_roll")])
                    if Ti == 0:
                        do(VE, "tensor_scalar", out=big1[:, :, 0:1], in0=big1[:, :, 0:1],
                           scalar1=tok0col[:, 0:1], scalar2=None, op0=ALU.mult,
                           reads=[buf("tok0col")], writes=[buf("big1c0")])

                def btile(Ti):
                    tl = Ti % 4
                    bs = Ti % 2
                    dma(SP, d_x[bs], xnfm[:, bs, :, :],
                        xnscr[:, :, Ti * 512:(Ti + 1) * 512].rearrange("c p t -> p c t"),
                        writes=[buf(f"xnfm{bs}")])
                    for mc in range(2):
                        pp, ppb = (p_qk, "p_qk") if mc % 2 == 0 else (p_av, "p_av")
                        mm_group(PE, [("matmul",
                                       (pp[:, :], rqw_b[:, kc, mc * 128:(mc + 1) * 128],
                                        xnfm[:, bs, kc, :]),
                                       dict(start=(kc == 0), stop=(kc == 7))) for kc in range(8)],
                                 reads=[buf("rqw_b"), buf(f"xnfm{bs}")], writes=[buf(ppb)])
                        do(AC, "activation", rq_fm[:, mc, :], pp[:, :], AF.Identity,
                           scale=0.125, bias=rqb_col[:, mc:mc + 1],
                           reads=[buf(ppb), buf("rqb_col")], writes=[buf("rq_fm")])
                        yield
                    pav_bf = p_av.ap().bitcast(BF16)
                    for j in range(4):
                        Gs = Ti * 4 + j
                        for half in range(2):
                            tk = mm_group(PE, [("matmul",
                                           (p_qk[:, lh * 64:(lh + 1) * 64],
                                            rq_fm[(lh % 2) * 64:(lh % 2) * 64 + 64, lh // 2,
                                                  j * 128:(j + 1) * 128],
                                            rk_fm[(lh % 2) * 64:(lh % 2) * 64 + 64, lh // 2, :]),
                                           dict(start=True, stop=True)) for lh in (half, half + 2)],
                                     reads=[buf("rq_fm"), buf("rk_fm")], writes=[buf("p_qk")])
                            if half == 0:
                                PE.wait(tk, force=True)
                        latview = p_qk[:, 0:256].rearrange("p (a b) -> p a b", b=64)
                        do(VE, "tensor_tensor", latview, latview,
                           latmadd[:, Gs, :].unsqueeze(1).broadcast_to([128, 4, 64]),
                           op=ALU.add, hard=True, reads=[buf("latmadd")], writes=[buf("p_qk")])
                        yield
                        do(AC, "activation", latprobs[:, :, :],
                           p_qk[:, 0:256].rearrange("p (a b) -> p a b", b=64),
                           AF.Exp, reads=[buf("p_qk")], writes=[buf("latprobs")])
                        do(VE, "reduce_sum", sums[:, 8:12], latprobs[:, :, :],
                           axis=AX.X, reads=[buf("latprobs")], writes=[buf("sumsL")])
                        do(VE, "tensor_scalar", hard=True, out=sums[:, 8:12],
                           in0=sums[:, 8:12], scalar1=1e-30, scalar2=None, op0=ALU.add,
                           reads=[buf("sumsL")], writes=[buf("sumsL")])
                        do(VE, "reciprocal", recip[:, 8:12], sums[:, 8:12],
                           reads=[buf("sumsL")], writes=[buf("recipL")])
                        yield
                        for lh in range(LH):
                            do(VE, "tensor_scalar", out=latprobs[:, lh, :], in0=latprobs[:, lh, :],
                               scalar1=recip[:, 8 + lh:9 + lh], scalar2=None, op0=ALU.mult,
                               reads=[buf("recipL")], writes=[buf("latprobs")])
                        mm_group(PE, [("transpose",
                                       (pav_bf[0:64, lh * 128:(lh + 1) * 128], latprobs[:, lh, :],
                                        ident_b[:, :]), {}) for lh in range(LH)],
                                 reads=[buf("latprobs"), buf("ident_b")], writes=[buf("p_av")])
                        do(VE, "tensor_copy", latTs[0:64, :, :],
                           pav_bf[0:64, 0:512].rearrange("p (a b) -> p a b", b=128),
                           reads=[buf("p_av")], writes=[buf("latTs")])
                        mms = []
                        for lh in range(LH):
                            po = (lh % 2) * 64
                            hp = lh // 2
                            oap = p_qk[po:po + 64, 256 + hp * 128:256 + (hp + 1) * 128]
                            mms.append(("matmul", (oap, rv_tm[0:64, lh, :], latTs[0:64, lh, :]),
                                        dict(start=True, stop=True)))
                        mm_group(PE, mms, reads=[buf("rv_tm"), buf("latTs")], writes=[buf("p_qk")])
                        do(AC, "activation", lat_fm[:, :, j * 128:(j + 1) * 128],
                           p_qk[:, 256:512].rearrange("p (a b) -> p a b", a=2),
                           AF.Copy, reads=[buf("p_qk")], writes=[buf("lat_fm")])
                        yield
                    yield
                    if Ti == 0:
                        do(VE, "tensor_scalar", out=lat_fm[:, :, 0:128], in0=lat_fm[:, :, 0:128],
                           scalar1=tok0col[:, 0:1], scalar2=None, op0=ALU.mult,
                           reads=[buf("tok0col")], writes=[buf("lat_fm")])
                    for mc in range(8):
                        pp, ppb = (p_qk, "p_qk") if mc % 2 == 0 else (p_av, "p_av")
                        mm_group(PE, [("matmul",
                                       (pp[:, :], routw_b[:, kc, mc * 128:(mc + 1) * 128],
                                        lat_fm[:, kc, :]),
                                       dict(start=(kc == 0), stop=(kc == 1))) for kc in range(2)],
                                 reads=[buf("routw_b"), buf("lat_fm")], writes=[buf(ppb)])
                        do(AC, "activation", remote_fm[:, mc, :], pp[:, :], AF.Identity,
                           bias=routb_col[:, mc:mc + 1],
                           reads=[buf(ppb), buf("routb_col")], writes=[buf("remote_fm")])
                        if mc % 2 == 1:
                            yield
                    mm_group(PE,
                             [("matmul", (p_av[0:16, :], glbd_b[:, kc, :],
                                          big1[:, kc, tl * 512:(tl + 1) * 512]),
                               dict(start=(kc == 0), stop=False)) for kc in range(8)] +
                             [("matmul", (p_av[0:16, :], grbd_b[:, kc, :], remote_fm[:, kc, :]),
                               dict(start=False, stop=(kc == 7))) for kc in range(8)],
                             reads=[buf("glbd_b"), buf("grbd_b"), buf(f"big1c{tl}"), buf("remote_fm")],
                             writes=[buf("p_av")])
                    do(AC, "activation", gexp[:, :], p_av[0:16, :], AF.Exp, scale=-1.0,
                       reads=[buf("p_av")], writes=[buf("gexp")])
                    do(VE, "tensor_scalar", out=gexp[:, :], in0=gexp[:, :], scalar1=1.0,
                       scalar2=None, op0=ALU.add, reads=[], writes=[buf("gexp")])
                    do(VE, "reciprocal", gate_sbT[:, :], gexp[:, :],
                       reads=[buf("gexp")], writes=[buf("gate_sbT")])
                    yield
                    for cch in range(8):
                        pp, ppb = (p_qk, "p_qk") if cch % 2 == 0 else (p_av, "p_av")
                        mm_group(PE, [("matmul", (pp[:, :], sel[0:16, cch, :], gate_sbT[:, :]),
                                       dict(start=True, stop=True))],
                                 reads=[buf("sel"), buf("gate_sbT")], writes=[buf(ppb)])
                        bcols = slice(tl * 512, (tl + 1) * 512)
                        do(VE, "tensor_tensor", big1[:, cch, bcols], big1[:, cch, bcols],
                           remote_fm[:, cch, :], op=ALU.subtract, writes=[buf(f"big1c{tl}")],
                           reads=[buf("remote_fm")])
                        do(VE, "tensor_tensor", big1[:, cch, bcols], big1[:, cch, bcols],
                           pp[:, :], op=ALU.mult, reads=[buf(ppb)], writes=[buf(f"big1c{tl}")])
                        do(VE, "tensor_tensor", big1[:, cch, bcols], big1[:, cch, bcols],
                           remote_fm[:, cch, :], op=ALU.add, reads=[buf("remote_fm")],
                           writes=[buf(f"big1c{tl}")])
                        if cch % 2 == 1:
                            yield
                    for j in range(4):
                        bs2 = j % 2
                        rows = slice(Ti * 512 + j * 128, Ti * 512 + (j + 1) * 128)
                        dma(SP, d_x[2 + bs2], s0sub[:, bs2, :], out[rows, :],
                            writes=[buf(f"s0sub{bs2}")])
                        mm_group(PE, [("matmul",
                                       (p_qk[:, :],
                                        big1[:, kc, tl * 512 + j * 128:tl * 512 + (j + 1) * 128],
                                        outw_b[:, kc, 0:512]),
                                       dict(start=(kc == 0), stop=(kc == 7)))
                                      for kc in range(8)],
                                 reads=[buf(f"big1c{tl}"), buf("outw_b")], writes=[buf("p_qk")])
                        mm_group(PE, [("matmul",
                                       (p_av[:, :],
                                        big1[:, kc, tl * 512 + j * 128:tl * 512 + (j + 1) * 128],
                                        outw_b[:, kc, 512:1024]),
                                       dict(start=(kc == 0), stop=(kc == 7)))
                                      for kc in range(8)],
                                 reads=[buf(f"big1c{tl}"), buf("outw_b")], writes=[buf("p_av")])
                        do(VE, "tensor_tensor", s0sub[:, bs2, 0:512], s0sub[:, bs2, 0:512],
                           p_qk[:, :], op=ALU.add, reads=[buf("p_qk")], writes=[buf(f"s0sub{bs2}")])
                        do(VE, "tensor_tensor", s0sub[:, bs2, 512:1024], s0sub[:, bs2, 512:1024],
                           p_av[:, :], op=ALU.add, reads=[buf("p_av")], writes=[buf(f"s0sub{bs2}")])
                        dmaq(SP, d_st[bs2], out[rows, :], s0sub[:, bs2, :],
                             reads=[buf(f"s0sub{bs2}")])
                        yield
                ntiles = 2 * _P1G if _P1T >= 4 else _P1T
                for Tix in range(ntiles):
                    ga = atile(Tix)
                    gb = btile(Tix - 1) if (Tix >= 1 and _P1B) else None
                    alive = True
                    while alive:
                        alive = False
                        try:
                            next(ga)
                            alive = True
                        except StopIteration:
                            pass
                        if gb is not None:
                            try:
                                next(gb)
                                alive = True
                            except StopIteration:
                                gb = None
                    while gb is not None:
                        try:
                            next(gb)
                        except StopIteration:
                            gb = None
                if _P1B and ntiles >= 1:
                    for _ in btile(ntiles - 1):
                        pass

        barrier()
        emit()
        es2.close()
        es_w.close()

        run_p2 = 'p2' in parts
        es3 = ExitStack()

        def sb3(name, shape, dt):
            return es3.enter_context(nc.sbuf_tensor(name, shape, dt))

        w1b = sb3("w1b", [128, 8, FF], BF16)
        w2b = sb3("w2b", [128, 32, D], BF16)
        yb = sb3("yb", [128, 32, 256], BF16)
        xnfm2 = sb3("xnfm2", [128, 2, 8, 256], BF16)
        s1sub = sb3("s1sub", [128, 2, D], F32)
        qstage = sb3("qstage", [128, 4, 1024], BF16)
        scrows_f = sb3("scrows_f", [64, 1024], F32)
        scb = sb3("scb", [64, 2, 1024], BF16)
        onehotb = sb3("onehotb", [64, 2, 128], BF16)
        cbtmp = sb3("cbtmp", [128, D], F32)
        cb_bc = sb3("cb_bc", [128, D], F32)
        r2bc = sb3("r2bc", [128, 2, 256], F32)
        p_f1 = es3.enter_context(nc.psum_tensor("p_f1", [128, 2, 512], F32))
        p_z = es3.enter_context(nc.psum_tensor("p_z", [128, 2, D], F32))
        p_sc = es3.enter_context(nc.psum_tensor("p_sc", [128, 1024], F32))

        if run_p2:
            QQ = [d_g[0], d_g[1], d_s2[0], d_s2[1]]
            dma(GP, d_ss, scrows_f[0:32, :],
                s1d.rearrange("r (w c) -> (r w) c", c=1024), writes=[buf("scrows")])
            dma(GP, d_ss, scrows_f[32:64, :], s2d, writes=[buf("scrows")])
            do(VE, "tensor_copy", scb[:, 0, :], scrows_f[:, :],
               reads=[buf("scrows")], writes=[buf("scb")])
            do(VE, "tensor_tensor", scb[:, 1, :], scrows_f[:, :], scb[:, 0, :],
               op=ALU.subtract, reads=[buf("scrows")], writes=[buf("scb")])

            def deq_piece(i2, kc, wv, qsrc, srow, dstw, bufname):
                sq = i2 % 4
                oh = i2 % 2
                dma(GP, QQ[sq], qstage[:, sq, :],
                    qsrc[kc * 128:(kc + 1) * 128, wv * 1024:(wv + 1) * 1024],
                    writes=[buf(f"qst{sq}")])
                do(GP, "tensor_copy", onehotb[:, oh, :],
                   ident_b[0:64, srow:srow + 1].broadcast_to([64, 128]),
                   reads=[buf("ident_b")], writes=[buf(f"oh{oh}")])
                mm_group(PE, [
                    ("matmul", (p_sc[:, hf * 512:(hf + 1) * 512], onehotb[0:64, oh, :],
                                scb[0:64, hl, hf * 512:(hf + 1) * 512]),
                     dict(start=(hl == 0), stop=(hl == 1)))
                    for hf in range(2) for hl in range(2)
                ], reads=[buf(f"oh{oh}"), buf("scb")], writes=[buf("p_sc")])
                do(VE, "scalar_tensor_tensor",
                   out=dstw[:, kc, wv * 1024:(wv + 1) * 1024],
                   in0=qstage[:, sq, :], scalar=-128.0, in1=p_sc[:, :],
                   op0=ALU.add, op1=ALU.mult,
                   reads=[buf(f"qst{sq}"), buf("p_sc")],
                   writes=[buf(bufname)])

            for i2, (kc, wv) in enumerate((kc, wv) for wv in range(4) for kc in range(8)):
                deq_piece(i2, kc, wv, qscr1, kc * 4 + wv, w1b, f"w1b{wv}")

            dma(GP, d_w[0], cbtmp[:, :],
                b2d.unsqueeze(0).partition_broadcast(128), writes=[buf("cbt")])
            dma(GP, d_w[1], cb_bc[:, :],
                bout.unsqueeze(0).partition_broadcast(128), writes=[buf("cb_bc")])
            do(VE, "scalar_tensor_tensor", out=cb_bc[:, :],
               in0=cbtmp[:, :], scalar=0.5,
               in1=cb_bc[:, :], op0=ALU.mult, op1=ALU.add,
               reads=[buf("cbt"), buf("cb_bc")],
               writes=[buf("cb_bc")])

            def deq2_gen():
                for i2, kc in enumerate(range(32)):
                    deq_piece(32 + i2, kc, 0, qscr2, 32 + kc, w2b, f"w2b{kc // 8}")
                    yield

            def p2_load(Ti):
                bs = Ti % 2
                cols = slice(Ti * 256, (Ti + 1) * 256)
                dma(SP, d_x[bs], xnfm2[:, bs, :, :],
                    xnscr[:, :, cols].rearrange("c p t -> p c t"), writes=[buf(f"xnfm2{bs}")])
                dma(GP, d_w[bs], r2bc[:, bs, :],
                    r2scr.rearrange("a b -> (a b)")[Ti * 256:(Ti + 1) * 256]
                    .unsqueeze(0).partition_broadcast(128)[:, 0, :],
                    writes=[buf(f"r2bc{bs}")])

            def p2_norm(Ti):
                bs = Ti % 2
                for cch in range(8):
                    do(GP, "tensor_tensor", xnfm2[:, bs, cch, :], xnfm2[:, bs, cch, :],
                       r2bc[:, bs, :],
                       op=ALU.mult, reads=[buf(f"r2bc{bs}")], writes=[buf(f"xnfm2{bs}")])
                    do(AC, "activation", xnfm2[:, bs, cch, :], xnfm2[:, bs, cch, :], AF.Identity,
                       scale=g_col[:, cch:cch + 1], bias=be_col[:, cch:cch + 1],
                       reads=[buf("g_col"), buf("be_col")], writes=[buf(f"xnfm2{bs}")])

            def p2_fc1(Ti, deq_iter=None):
                bs = Ti % 2
                for mc in range(32):
                    pf = mc % 2
                    mm_group(PE, [("matmul",
                                   (p_f1[:, pf, 0:256], w1b[:, kc, mc * 128:(mc + 1) * 128],
                                    xnfm2[:, bs, kc, :]),
                                   dict(start=(kc == 0), stop=(kc == 7))) for kc in range(8)],
                             reads=[buf(f"w1b{mc // 8}"), buf(f"xnfm2{bs}")], writes=[buf(f"p_f1{pf}")])
                    do(AC, "activation", yb[:, mc, :], p_f1[:, pf, 0:256], AF.Silu,
                       bias=f1b_col[:, mc:mc + 1],
                       reads=[buf(f"p_f1{pf}"), buf("f1b_col")], writes=[buf("yb")])
                    if deq_iter is not None:
                        next(deq_iter, None)

            def p2_fc2(Ti):
                for j in range(2):
                    rows = slice(Ti * 256 + j * 128, Ti * 256 + (j + 1) * 128)
                    dma(SP, d_x[2 + j], s1sub[:, j, :], out[rows, :], writes=[buf(f"s1s{j}")])
                    for gi in range(4):
                        mm_group(PE, [("matmul",
                                       (p_z[:, j, nn * 512:(nn + 1) * 512],
                                        yb[:, kc, j * 128:(j + 1) * 128],
                                        w2b[:, kc, nn * 512:(nn + 1) * 512]),
                                       dict(start=(kc == 0), stop=(kc == 31)))
                                      for nn in range(2) for kc in range(gi * 8, (gi + 1) * 8)],
                                 reads=[buf("yb"), buf(f"w2b{gi}")], writes=[buf(f"p_z{j}")])
                    do(VE, "scalar_tensor_tensor", out=p_z[:, j, :], in0=p_z[:, j, :],
                       scalar=0.5, in1=cb_bc[:, :], op0=ALU.mult, op1=ALU.add,
                       reads=[buf("cb_bc")], writes=[buf(f"p_z{j}")], hard=True)
                    do(VE, "tensor_tensor", s1sub[:, j, :], s1sub[:, j, :], p_z[:, j, :],
                       op=ALU.add, reads=[buf(f"p_z{j}")], writes=[buf(f"s1s{j}")])
                    dmaq(SP, d_st[j], out[rows, :], s1sub[:, j, :], reads=[buf(f"s1s{j}")])

            p2_load(0)
            p2_norm(0)
            dq2 = deq2_gen()
            for Ti in range(16):
                if Ti + 1 < 16:
                    p2_load(Ti + 1)
                p2_fc1(Ti, dq2 if Ti == 0 else None)
                if Ti == 0:
                    for _ in dq2:
                        pass
                if Ti + 1 < 16:
                    p2_norm(Ti + 1)
                p2_fc2(Ti)

        barrier()
        emit()
        es3.close()
    es0.close()
    return nc


_CACHED = {}


def _consts(half):
    a = np.arange(128)[:, None]
    b = np.arange(256)[None, :]
    if half == 0:
        madd_t0 = np.where((b >= 128) & (b - 128 < a), 0.0, NEG).astype(np.float32)
        vrow_t0 = np.ones((1, 128), np.float32)
        vrow_t0[0, 0] = 0.0
        tok0 = np.zeros((128, 1), np.float32)
    else:
        madd_t0 = np.where((b >= a) & (b <= a + 127), 0.0, NEG).astype(np.float32)
        vrow_t0 = np.ones((1, 128), np.float32)
        tok0 = np.ones((128, 1), np.float32)
    c0 = 32 * half
    latm = np.full((32, 64), NEG, np.float32)
    for s in range(32):
        c_abs = c0 + s
        for l_loc in range(64):
            l_glob = (c0 + l_loc) if l_loc < 32 else ((32 - c0) + (l_loc - 32))
            if l_glob < c_abs:
                latm[s, l_loc] = 0.0
    return madd_t0, vrow_t0, tok0, latm


def kernel(L0_attn_x, L0_attn_qkv_w, L0_attn_qkv_b, L0_attn_rq_w, L0_attn_rq_b,
           L0_attn_rkv_w, L0_attn_rkv_b, L0_attn_rout_w, L0_attn_rout_b,
           L0_attn_out_w, L0_attn_out_b, L0_attn_gl_w, L0_attn_gr_w,
           L0_ffn_ln_gamma, L0_ffn_ln_beta, L0_ffn_fc1_q, L0_ffn_fc1_scales,
           L0_ffn_fc1_bias, L0_ffn_fc2_q, L0_ffn_fc2_scales, L0_ffn_fc2_bias):
    x = np.ascontiguousarray(np.asarray(L0_attn_x, dtype=np.float32))
    c32 = lambda v: np.ascontiguousarray(np.asarray(v, dtype=np.float32))
    ci32 = lambda v: np.ascontiguousarray(np.asarray(v, dtype=np.int32))

    glbd = np.zeros((D, H), np.float32)
    grbd = np.zeros((D, H), np.float32)
    gl = np.asarray(L0_attn_gl_w, np.float32)[:, 0]
    gr = np.asarray(L0_attn_gr_w, np.float32)[:, 0]
    for h in range(H):
        glbd[h * HD:(h + 1) * HD, h] = gl
        grbd[h * HD:(h + 1) * HD, h] = gr

    common = dict(
        wqkv=c32(L0_attn_qkv_w), bqkv=c32(L0_attn_qkv_b),
        wrq=c32(L0_attn_rq_w), brq=c32(L0_attn_rq_b),
        wrkv=c32(L0_attn_rkv_w), brkv=c32(L0_attn_rkv_b),
        wrout=c32(L0_attn_rout_w), brout=c32(L0_attn_rout_b),
        wout=c32(L0_attn_out_w), bout=c32(L0_attn_out_b),
        glbd=glbd, grbd=grbd,
        gln=c32(L0_ffn_ln_gamma), bln=c32(L0_ffn_ln_beta),
        q1d=ci32(L0_ffn_fc1_q), s1d=c32(L0_ffn_fc1_scales), b1d=c32(L0_ffn_fc1_bias),
        q2d=ci32(L0_ffn_fc2_q), s2d=c32(L0_ffn_fc2_scales), b2d=c32(L0_ffn_fc2_bias),
    )
    halves = [_consts(0), _consts(1)]

    if "nc" not in _CACHED:
        _CACHED["nc"] = build_nc()
    nc = _CACHED["nc"]

    in_maps = []
    for core in range(8):
        bb = core // 2
        half = core % 2
        madd_t0, vrow_t0, tok0, latm = halves[half]
        m = dict(common)
        m["xo"] = np.ascontiguousarray(x[bb, half * T:(half + 1) * T])
        m["xp"] = np.ascontiguousarray(x[bb, (1 - half) * T:(2 - half) * T])
        m["madd_t0"] = madd_t0
        m["vrow_t0"] = vrow_t0
        m["tok0"] = tok0
        m["latm"] = latm
        in_maps.append(m)

    res = run_bass_kernel_spmd(nc, in_maps, core_ids=list(range(8)))
    outs = [np.asarray(r["out"]) for r in res.results]
    full = np.zeros((4, 2 * T, D), np.float32)
    for core in range(8):
        bb = core // 2
        half = core % 2
        full[bb, half * T:(half + 1) * T] = outs[core]
    return full
